# version 02
# baseline (speedup 1.0000x reference)
import math
import contextlib
import numpy as np
import concourse.bass as bass
import concourse.mybir as mybir
from concourse.bass_utils import run_bass_kernel_spmd

F32 = mybir.dt.float32
BF16 = mybir.dt.bfloat16
AF = mybir.ActivationFunctionType
ALU = mybir.AluOpType
AX = mybir.AxisListType

NCORES = 8
B = 2
S = 8192
D = 1024
TSH = S // 4
NCH = D // 128

_PROG_CACHE = {}


def cached(key, fn):
    if key not in _PROG_CACHE:
        _PROG_CACHE[key] = fn()
    return _PROG_CACHE[key]


class Seq:

    def __init__(self):
        self.phases = []

    def ph(self, engine):
        def deco(fn):
            self.phases.append((engine, "op", fn))
        return deco

    def dma(self, engine="sync"):
        def deco(fn):
            self.phases.append((engine, "dma", fn))
        return deco

    _NOFENCE = {"wait_ge", "nop", "dma_start", "then_inc", "sem_inc", "memzero"}

    class _Fenced:
        def __init__(self, eng, sem):
            self._eng = eng
            self._sem = sem
            self._n = 0

        def __getattr__(self, name):
            attr = getattr(self._eng, name)
            if not callable(attr) or name.startswith("_") or name in Seq._NOFENCE:
                return attr

            def wrapper(*a, **k):
                inst = attr(*a, **k)
                try:
                    self._n += 1
                    inst.then_inc(self._sem, 1)
                    self._eng.wait_ge(self._sem, self._n)
                except Exception:
                    raise
                return inst
            return wrapper

    def build(self, nc):
        with contextlib.ExitStack() as st:
            g = st.enter_context(nc.semaphore("g_seq"))
            dsems = {e: st.enter_context(nc.semaphore(f"d_{e}"))
                     for e in ("sync", "vector", "scalar", "tensor", "gpsimd")}
            csems = {e: st.enter_context(nc.semaphore(f"c_{e}"))
                     for e in ("vector", "scalar", "gpsimd")}
            block = st.enter_context(nc.Block())

            def run_engine(eng, name):
                dma_cum = 0
                fenced = Seq._Fenced(eng, csems[name]) if name in csems else None
                for i, (ename, kind, fn) in enumerate(self.phases):
                    if ename != name:
                        continue
                    if i > 0:
                        eng.wait_ge(g, i)
                    if kind == "op":
                        last = fn(fenced if fenced is not None else eng)
                        assert last is not None, f"phase {i} ({name}) returned None"
                        if fenced is not None:
                            eng.nop().then_inc(g, 1)
                        else:
                            last.then_inc(g, 1)
                    else:
                        insts = fn(eng)
                        assert insts, f"dma phase {i} empty"
                        for inst in insts:
                            inst.then_inc(dsems[name], 16)
                        dma_cum += 16 * len(insts)
                        eng.wait_ge(dsems[name], dma_cum)
                        eng.nop().then_inc(g, 1)

            @block.sync
            def _(e):
                run_engine(e, "sync")

            @block.vector
            def _(e):
                run_engine(e, "vector")

            @block.scalar
            def _(e):
                run_engine(e, "scalar")

            @block.tensor
            def _(e):
                run_engine(e, "tensor")

            @block.gpsimd
            def _(e):
                run_engine(e, "gpsimd")
        return nc


def new_nc():
    return bass.Bass("TRN2", target_bir_lowering=False)



def add_ln_phases(sq, h, T=TSH, eps=1e-5):
    ntc = T // 512
    for tc in range(ntc):
        @sq.ph("tensor")
        def _(t, tc=tc):
            for k in range(NCH):
                last = t.matmul(h["psA"][0:1, 0:512], h["ones_col"][:, :],
                                h["xs"][:, k * T + tc * 512:k * T + tc * 512 + 512],
                                start=(k == 0), stop=(k == NCH - 1))
            return last

        @sq.ph("vector")
        def _(v, tc=tc):
            return v.tensor_copy(h["row_su"][0:1, tc * 512:tc * 512 + 512], h["psA"][0:1, 0:512])

        @sq.ph("scalar")
        def _(s, tc=tc):
            for k in range(NCH):
                last = s.activation(out=h["scr"][:, k * 256:k * 256 + 256],
                                    in_=h["xs"][:, k * T + tc * 512:k * T + tc * 512 + 256],
                                    func=AF.Square)
            return last

        @sq.ph("tensor")
        def _(t, tc=tc):
            for k in range(NCH):
                last = t.matmul(h["psA"][0:1, 0:256], h["ones_col"][:, :],
                                h["scr"][:, k * 256:k * 256 + 256],
                                start=(k == 0), stop=(k == NCH - 1))
            return last

        @sq.ph("scalar")
        def _(s, tc=tc):
            for k in range(NCH):
                last = s.activation(out=h["scr"][:, k * 256:k * 256 + 256],
                                    in_=h["xs"][:, k * T + tc * 512 + 256:k * T + tc * 512 + 512],
                                    func=AF.Square)
            return last

        @sq.ph("tensor")
        def _(t, tc=tc):
            for k in range(NCH):
                last = t.matmul(h["psA"][0:1, 256:512], h["ones_col"][:, :],
                                h["scr"][:, k * 256:k * 256 + 256],
                                start=(k == 0), stop=(k == NCH - 1))
            return last

        @sq.ph("vector")
        def _(v, tc=tc):
            return v.tensor_copy(h["row_va"][0:1, tc * 512:tc * 512 + 512], h["psA"][0:1, 0:512])

    @sq.ph("vector")
    def _(v):
        su = h["row_su"][0:1, 0:T]
        va = h["row_va"][0:1, 0:T]
        v.tensor_scalar_mul(su, su, 1.0 / D)
        v.tensor_mul(h["scr"][0:1, 0:T], su, su)
        v.scalar_tensor_tensor(out=va, in0=va, scalar=1.0 / D,
                               in1=h["scr"][0:1, 0:T], op0=ALU.mult, op1=ALU.subtract)
        return v.tensor_scalar_add(va, va, eps)

    @sq.ph("scalar")
    def _(s):
        va = h["row_va"][0:1, 0:T]
        s.activation(out=h["scr"][0:1, 0:T], in_=va, func=AF.Ln)
        return s.activation(out=va, in_=h["scr"][0:1, 0:T], func=AF.Exp, scale=-0.5)

    for tc in range(ntc):
        @sq.ph("tensor")
        def _(t, tc=tc):
            t.matmul(h["psR"][:, 0:512], h["ones_row"][:, :],
                     h["row_va"][0:1, tc * 512:tc * 512 + 512], start=True, stop=True)
            return t.matmul(h["psR"][:, 512:1024], h["ones_row"][:, :],
                            h["row_su"][0:1, tc * 512:tc * 512 + 512], start=True, stop=True)

        @sq.ph("vector")
        def _(v, tc=tc):
            for k in range(NCH):
                sl = slice(k * T + tc * 512, k * T + tc * 512 + 512)
                v.tensor_sub(h["scr"][:, 0:512], h["xs"][:, sl], h["psR"][:, 512:1024])
                last = v.tensor_mul(h["xln"][:, sl], h["scr"][:, 0:512], h["psR"][:, 0:512])
                if "xln2" in h:
                    last = v.tensor_mul(h["xln2"][:, sl], h["scr"][:, 0:512], h["psR"][:, 0:512])
            return last



def build_a1():
    T = TSH
    nc = new_nc()
    x_fm = nc.dram_tensor("x_fm", [D, T], F32, kind="ExternalInput").ap()
    w_in = nc.dram_tensor("w_in", [D, 2 * D], F32, kind="ExternalInput").ap()
    w_xp = nc.dram_tensor("w_xp", [D, 128], F32, kind="ExternalInput").ap()
    w_dt = nc.dram_tensor("w_dt", [65, D], F32, kind="ExternalInput").ap()
    w_res = nc.dram_tensor("w_res", [D, 32], F32, kind="ExternalInput").ap()
    ident = nc.dram_tensor("ident", [128, 128], F32, kind="ExternalInput").ap()
    o_siluz = nc.dram_tensor("o_siluz", [D, T], F32, kind="ExternalOutput").ap()
    o_dt = nc.dram_tensor("o_dt", [D, T], F32, kind="ExternalOutput").ap()
    o_dtx = nc.dram_tensor("o_dtx", [D, T], F32, kind="ExternalOutput").ap()
    o_bc = nc.dram_tensor("o_bc", [64, T], F32, kind="ExternalOutput").ap()
    o_xhat = nc.dram_tensor("o_xhat", [T, 32], F32, kind="ExternalOutput").ap()

    sq = Seq()
    with contextlib.ExitStack() as cx:
        E = cx.enter_context
        A = E(nc.sbuf_tensor("A", [128, NCH * T], F32))
        Bt = E(nc.sbuf_tensor("Bt", [128, NCH * T], BF16))
        Ct = E(nc.sbuf_tensor("Ct", [128, NCH * T], BF16))
        wb = E(nc.sbuf_tensor("wb", [128, NCH * 2048], BF16))
        wxb = E(nc.sbuf_tensor("wxb", [128, NCH * 128], BF16))
        wdtb = E(nc.sbuf_tensor("wdtb", [65, D], BF16))
        wresb = E(nc.sbuf_tensor("wresb", [128, NCH * 32], F32))
        idb = E(nc.sbuf_tensor("idb", [128, 128], F32))
        scr = E(nc.sbuf_tensor("scr", [128, 4096], F32))
        scr2 = E(nc.sbuf_tensor("scr2", [128, 2048], F32))
        xpb = E(nc.sbuf_tensor("xpb", [65, T], BF16))
        dtt = E(nc.sbuf_tensor("dtt", [128, T], F32))
        xhat = E(nc.sbuf_tensor("xhat", [128, 16 * 32], F32))
        ones_col = E(nc.sbuf_tensor("ones_col", [128, 1], F32))
        ones_row = E(nc.sbuf_tensor("ones_row", [1, 128], F32))
        cst = E(nc.sbuf_tensor("cst", [128, 4], F32))
        psA = E(nc.psum_tensor([1, 512], F32))
        psR = E(nc.psum_tensor([128, 1024], F32))
        psM = E(nc.psum_tensor([128, 2048], F32))
        psT = E(nc.psum_tensor([128, 128], F32))

        @sq.dma("sync")
        def _(q):
            ins = [q.dma_start(out=A[:, k * T:(k + 1) * T], in_=x_fm[k * 128:(k + 1) * 128, :])
                   for k in range(NCH)]
            ins.append(q.dma_start(out=idb[:, :], in_=ident[:, :]))
            return ins

        @sq.ph("vector")
        def _(v):
            v.memset(ones_col[:, :], 1.0)
            v.memset(ones_row[:, :], 1.0)
            v.memset(cst[:, 0:1], 1.0)
            v.memset(cst[:, 1:2], 1e-8)
            return v.memset(cst[:, 2:3], 1e-10)

        h = dict(xs=A, xln=Bt, scr=scr,
                 row_su=dtt[0:1, :], row_va=scr2[0:1, 0:T],
                 psA=psA, psR=psR, ones_col=ones_col, ones_row=ones_row)
        add_ln_phases(sq, h, T)

        for k in range(NCH):
            @sq.dma("sync")
            def _(q, k=k):
                return [q.dma_start(out=scr[:, 0:2048], in_=w_in[k * 128:(k + 1) * 128, :])]

            @sq.ph("vector")
            def _(v, k=k):
                return v.tensor_copy(wb[:, k * 2048:(k + 1) * 2048], scr[:, 0:2048])

        @sq.dma("sync")
        def _(q):
            return [q.dma_start(out=scr[:, 0:1024].rearrange("p (c n) -> p c n", c=8),
                                in_=w_xp.rearrange("(c p) n -> p c n", p=128)),
                    q.dma_start(out=scr[0:65, 1024:2048], in_=w_dt[:, :]),
                    q.dma_start(out=scr[0:128, 2048:2048 + 256].rearrange("p (c n) -> p c n", c=8),
                                in_=w_res.rearrange("(c p) n -> p c n", p=128))]

        @sq.ph("vector")
        def _(v):
            v.tensor_copy(wxb[:, :], scr[:, 0:1024])
            v.tensor_copy(wdtb[:, :], scr[0:65, 1024:2048])
            return v.tensor_copy(wresb[:, :], scr[:, 2048:2048 + 256])

        for o in range(16):
            @sq.ph("tensor")
            def _(t, o=o):
                for tc in range(4):
                    for k in range(NCH):
                        last = t.matmul(psM[:, tc * 512:tc * 512 + 512],
                                        wb[:, k * 2048 + o * 128:k * 2048 + o * 128 + 128],
                                        Bt[:, k * T + tc * 512:k * T + tc * 512 + 512],
                                        start=(k == 0), stop=(k == NCH - 1))
                return last

            if o < 8:
                @sq.ph("scalar")
                def _(s, o=o):
                    s.activation(out=A[:, o * T:(o + 1) * T], in_=psM[:, 0:T], func=AF.Silu)
                    return s.activation(out=Ct[:, o * T:(o + 1) * T], in_=psM[:, 0:T], func=AF.Silu)
            else:
                @sq.ph("scalar")
                def _(s, o=o):
                    return s.activation(out=scr2[:, 0:T], in_=psM[:, 0:T], func=AF.Silu)

                @sq.dma("sync")
                def _(q, o=o):
                    return [q.dma_start(out=o_siluz[(o - 8) * 128:(o - 7) * 128, :], in_=scr2[:, 0:T])]

        @sq.ph("tensor")
        def _(t):
            for tc in range(4):
                for k in range(NCH):
                    last = t.matmul(psM[:, tc * 512:tc * 512 + 512],
                                    wxb[:, k * 128:(k + 1) * 128],
                                    Ct[:, k * T + tc * 512:k * T + tc * 512 + 512],
                                    start=(k == 0), stop=(k == NCH - 1))
            return last

        @sq.ph("vector")
        def _(v):
            return v.tensor_copy(scr2[:, 0:T], psM[:, 0:T])

        @sq.dma("sync")
        def _(q):
            return [q.dma_start(out=o_bc[:, :], in_=scr2[64:128, 0:T])]

        @sq.ph("vector")
        def _(v):
            v.tensor_copy(xpb[0:64, :], scr2[0:64, 0:T])
            return v.memset(xpb[64:65, :], 1.0)

        for o in range(NCH):
            @sq.ph("tensor")
            def _(t, o=o):
                for tc in range(4):
                    last = t.matmul(psM[:, tc * 512:tc * 512 + 512],
                                    wdtb[:, o * 128:(o + 1) * 128],
                                    xpb[:, tc * 512:tc * 512 + 512],
                                    start=True, stop=True)
                return last

            @sq.ph("scalar")
            def _(s, o=o):
                s.activation(out=scr[:, 0:T], in_=psM[:, 0:T], func=AF.Abs)
                s.activation(out=scr[:, 0:T], in_=scr[:, 0:T], func=AF.Exp, scale=-1.0)
                return s.activation(out=scr[:, 0:T], in_=scr[:, 0:T], func=AF.Ln, bias=cst[:, 0:1])

            @sq.ph("vector")
            def _(v, o=o):
                v.tensor_scalar_max(scr2[:, 0:T], psM[:, 0:T], 0.0)
                v.tensor_add(dtt[:, :], scr2[:, 0:T], scr[:, 0:T])
                return v.tensor_mul(scr2[:, 0:T], dtt[:, :], A[:, o * T:(o + 1) * T])

            @sq.dma("sync")
            def _(q, o=o):
                return [q.dma_start(out=o_dt[o * 128:(o + 1) * 128, :], in_=dtt[:, :]),
                        q.dma_start(out=o_dtx[o * 128:(o + 1) * 128, :], in_=scr2[:, 0:T])]

        @sq.ph("tensor")
        def _(t):
            for tc in range(4):
                for k in range(NCH):
                    last = t.matmul(psM[0:32, tc * 512:tc * 512 + 512],
                                    wresb[:, k * 32:(k + 1) * 32],
                                    A[:, k * T + tc * 512:k * T + tc * 512 + 512],
                                    start=(k == 0), stop=(k == NCH - 1))
            return last

        xres = scr[0:32, 2048:4096]

        @sq.ph("vector")
        def _(v):
            return v.tensor_copy(xres, psM[0:32, 0:T])

        for tl in range(16):
            @sq.ph("tensor")
            def _(t, tl=tl):
                t.transpose(psT[:, 0:32], xres[:, tl * 128:(tl + 1) * 128], idb[0:32, 0:32])
                return t.matmul(psT[0:1, 32:33], ones_col[0:1, 0:1], ones_col[0:1, 0:1],
                                start=True, stop=True)

            @sq.ph("vector")
            def _(v, tl=tl):
                v.tensor_copy(scr[:, 256:288], psT[:, 0:32])
                re = scr[:, 256:272]
                return v.tensor_add(scr[:, 0:16], re, cst[:, 2:3].broadcast_to([128, 16]))

            @sq.ph("scalar")
            def _(sc, tl=tl):
                sc.activation(out=scr[:, 64:80], in_=scr[:, 0:16], func=AF.Square)
                return sc.activation(out=scr[:, 128:144], in_=scr[:, 272:288], func=AF.Square)

            @sq.ph("vector")
            def _(v, tl=tl):
                return v.tensor_add(scr[:, 64:80], scr[:, 64:80], scr[:, 128:144])

            @sq.ph("scalar")
            def _(sc, tl=tl):
                sc.activation(out=scr[:, 80:96], in_=scr[:, 64:80], func=AF.Ln)
                return sc.activation(out=scr[:, 80:96], in_=scr[:, 80:96], func=AF.Exp, scale=-0.5)

            @sq.ph("vector")
            def _(v, tl=tl):
                dst = xhat[:, tl * 32:(tl + 1) * 32]
                v.tensor_mul(dst[:, 0:16], scr[:, 0:16], scr[:, 80:96])
                return v.tensor_mul(dst[:, 16:32], scr[:, 272:288], scr[:, 80:96])

        @sq.dma("sync")
        def _(q):
            return [q.dma_start(out=o_xhat[tl * 128:(tl + 1) * 128, :],
                                in_=xhat[:, tl * 32:(tl + 1) * 32]) for tl in range(16)]

        sq.build(nc)
    return nc



def build_a2(nstep, chunk=128):
    nc = new_nc()
    nchunk = nstep // chunk
    CH = chunk * 32
    h_re0 = nc.dram_tensor("h_re0", [128, 32], F32, kind="ExternalInput").ap()
    h_im0 = nc.dram_tensor("h_im0", [128, 32], F32, kind="ExternalInput").ap()
    th0 = nc.dram_tensor("th0", [128, 32], F32, kind="ExternalInput").ap()
    dt_s = nc.dram_tensor("dt_s", [32, nstep * 32], F32, kind="ExternalInput").ap()
    dtx_s = nc.dram_tensor("dtx_s", [32, nstep * 32], F32, kind="ExternalInput").ap()
    bre_s = nc.dram_tensor("bre_s", [128, nstep], F32, kind="ExternalInput").ap()
    bim_s = nc.dram_tensor("bim_s", [128, nstep], F32, kind="ExternalInput").ap()
    cre_s = nc.dram_tensor("cre_s", [128, nstep], F32, kind="ExternalInput").ap()
    cim_s = nc.dram_tensor("cim_s", [128, nstep], F32, kind="ExternalInput").ap()
    xr_s = nc.dram_tensor("xr_s", [128, nstep], F32, kind="ExternalInput").ap()
    xi_s = nc.dram_tensor("xi_s", [128, nstep], F32, kind="ExternalInput").ap()
    en4 = nc.dram_tensor("en4", [128, 4], F32, kind="ExternalInput").ap()
    en32 = nc.dram_tensor("en32", [128, 32], F32, kind="ExternalInput").ap()
    en4t = nc.dram_tensor("en4t", [4, 128], F32, kind="ExternalInput").ap()
    emask = nc.dram_tensor("emask", [32, 128], F32, kind="ExternalInput").ap()
    alogp = nc.dram_tensor("alogp", [1, 128], F32, kind="ExternalInput").ap()
    o_y = nc.dram_tensor("o_y", [32, nstep * 32], F32, kind="ExternalOutput").ap()
    o_hre = nc.dram_tensor("o_hre", [128, 32], F32, kind="ExternalOutput").ap()
    o_him = nc.dram_tensor("o_him", [128, 32], F32, kind="ExternalOutput").ap()
    o_th = nc.dram_tensor("o_th", [128, 32], F32, kind="ExternalOutput").ap()

    sq = Seq()
    with contextlib.ExitStack() as cx:
        E = cx.enter_context
        hre = E(nc.sbuf_tensor("hre", [128, 32], F32))
        him = E(nc.sbuf_tensor("him", [128, 32], F32))
        th = E(nc.sbuf_tensor("th", [128, 32], F32))
        dtc = E(nc.sbuf_tensor("dtc", [32, CH], F32))
        dtxc = E(nc.sbuf_tensor("dtxc", [32, CH], F32))
        dtcb = E(nc.sbuf_tensor("dtcb", [32, CH], BF16))
        dtxcb = E(nc.sbuf_tensor("dtxcb", [32, CH], BF16))
        gt = E(nc.sbuf_tensor("gt", [128, CH], F32))
        ga = E(nc.sbuf_tensor("ga", [128, CH], F32))
        dtxr = E(nc.sbuf_tensor("dtxr", [128, CH], F32))
        bxre = E(nc.sbuf_tensor("bxre", [128, CH], F32))
        bxim = E(nc.sbuf_tensor("bxim", [128, CH], F32))
        brc = E(nc.sbuf_tensor("brc", [128, chunk], F32))
        bic = E(nc.sbuf_tensor("bic", [128, chunk], F32))
        crc = E(nc.sbuf_tensor("crc", [128, chunk], F32))
        cic = E(nc.sbuf_tensor("cic", [128, chunk], F32))
        xrc = E(nc.sbuf_tensor("xrc", [128, chunk], F32))
        xic = E(nc.sbuf_tensor("xic", [128, chunk], F32))
        w2 = E(nc.sbuf_tensor("w2", [128, 12], F32))
        yb = E(nc.sbuf_tensor("yb", [128, CH], F32))
        en4b = E(nc.sbuf_tensor("en4b", [128, 4], F32))
        en32b = E(nc.sbuf_tensor("en32b", [128, 32], F32))
        yred = E(nc.sbuf_tensor("yred", [32, 4096], F32))
        en4tb = E(nc.sbuf_tensor("en4tb", [4, 128], F32))
        eab = E(nc.sbuf_tensor("eab", [32, 128], BF16))
        erb = E(nc.sbuf_tensor("erb", [32, 128], BF16))
        emk = E(nc.sbuf_tensor("emk", [32, 128], F32))
        alg = E(nc.sbuf_tensor("alg", [1, 256], F32))
        cst = E(nc.sbuf_tensor("cst", [128, 8], F32))
        sc = E(nc.sbuf_tensor("sc", [128, 64], F32))
        th2 = E(nc.sbuf_tensor("th2", [128, 32], F32))
        Lt = E(nc.sbuf_tensor("Lt", [128, 32], F32))
        glt = E(nc.sbuf_tensor("glt", [128, 32], F32))
        ut = E(nc.sbuf_tensor("ut", [128, 32], F32))
        Ft = E(nc.sbuf_tensor("Ft", [128, 32], F32))
        sqre = E(nc.sbuf_tensor("sqre", [128, 32], F32))
        sqim = E(nc.sbuf_tensor("sqim", [128, 32], F32))
        m2t = E(nc.sbuf_tensor("m2t", [128, 32], F32))
        L2t = E(nc.sbuf_tensor("L2t", [128, 32], F32))
        mg2 = E(nc.sbuf_tensor("mg2", [128, 32], F32))
        rm2 = E(nc.sbuf_tensor("rm2", [128, 32], F32))
        F2t = E(nc.sbuf_tensor("F2t", [128, 32], F32))
        yt1 = E(nc.sbuf_tensor("yt1", [128, 32], F32))
        tiny = E(nc.sbuf_tensor("tiny", [4, 16], F32))
        wq128 = E(nc.sbuf_tensor("wq128", [128, 1], F32))
        scr32 = E(nc.sbuf_tensor("scr32", [32, 128], F32))
        psC = E(nc.psum_tensor([4, 64], F32))
        psW = E(nc.psum_tensor([128, 2], F32))
        psB = E(nc.psum_tensor([128, 2048], F32))
        psE = E(nc.psum_tensor([32, 128], F32))

        @sq.dma("sync")
        def _(q):
            return [q.dma_start(out=hre[:, :], in_=h_re0[:, :]),
                    q.dma_start(out=him[:, :], in_=h_im0[:, :]),
                    q.dma_start(out=th[:, :], in_=th0[:, :]),
                    q.dma_start(out=en4b[:, :], in_=en4[:, :]),
                    q.dma_start(out=en32b[:, :], in_=en32[:, :]),
                    q.dma_start(out=en4tb[:, :], in_=en4t[:, :]),
                    q.dma_start(out=emk[:, :], in_=emask[:, :]),
                    q.dma_start(out=alg[0:1, 0:128], in_=alogp[:, :])]

        @sq.ph("vector")
        def _(v):
            v.memset(cst[:, 0:1], 1.0)
            v.memset(cst[:, 1:2], 1e-8)
            v.memset(cst[:, 2:3], 1.024e-7)
            v.memset(tiny[:, :], 0.0)
            v.memset(w2[:, :], 0.0)
            v.memset(sc[:, :], 0.0)
            return v.memset(cst[:, 3:4], 0.0)

        @sq.ph("scalar")
        def _(s):
            return s.activation(out=alg[0:1, 128:256], in_=alg[0:1, 0:128], func=AF.Exp)

        @sq.ph("tensor")
        def _(t):
            return t.matmul(psE[:, :], cst[0:1, 0:1].broadcast_to([1, 32]),
                            alg[0:1, 128:256], start=True, stop=True)

        @sq.ph("vector")
        def _(v):
            v.tensor_mul(scr32[:, :], psE[:, :], emk[:, :])
            v.tensor_scalar_mul(scr32[:, :], scr32[:, :], -1.0)
            v.tensor_copy(eab[:, :], scr32[:, :])
            return v.tensor_copy(erb[:, :], emk[:, :])

        for ck in range(nchunk):
            t0 = ck * chunk

            @sq.dma("sync")
            def _(q, t0=t0):
                return [q.dma_start(out=dtc[:, :], in_=dt_s[:, t0 * 32:(t0 + chunk) * 32]),
                        q.dma_start(out=dtxc[:, :], in_=dtx_s[:, t0 * 32:(t0 + chunk) * 32]),
                        q.dma_start(out=brc[:, :], in_=bre_s[:, t0:t0 + chunk]),
                        q.dma_start(out=bic[:, :], in_=bim_s[:, t0:t0 + chunk]),
                        q.dma_start(out=crc[:, :], in_=cre_s[:, t0:t0 + chunk]),
                        q.dma_start(out=cic[:, :], in_=cim_s[:, t0:t0 + chunk]),
                        q.dma_start(out=xrc[:, :], in_=xr_s[:, t0:t0 + chunk]),
                        q.dma_start(out=xic[:, :], in_=xi_s[:, t0:t0 + chunk])]

            @sq.ph("vector")
            def _(v):
                v.tensor_copy(dtcb[:, :], dtc[:, :])
                return v.tensor_copy(dtxcb[:, :], dtxc[:, :])

            for half in range(CH // 2048):
                hsl = slice(half * 2048, (half + 1) * 2048)

                @sq.ph("tensor")
                def _(t, half=half):
                    for j in range(4):
                        sl = slice(half * 2048 + j * 512, half * 2048 + j * 512 + 512)
                        last = t.matmul(psB[:, j * 512:(j + 1) * 512], eab[:, :],
                                        dtcb[:, sl], start=True, stop=True)
                    return last

                @sq.ph("scalar")
                def _(s, half=half, hsl=hsl):
                    return s.activation(out=gt[:, hsl], in_=psB[:, 0:2048], func=AF.Exp)

                @sq.ph("tensor")
                def _(t, half=half):
                    for j in range(4):
                        sl = slice(half * 2048 + j * 512, half * 2048 + j * 512 + 512)
                        last = t.matmul(psB[:, j * 512:(j + 1) * 512], erb[:, :],
                                        dtxcb[:, sl], start=True, stop=True)
                    return last

                @sq.ph("scalar")
                def _(s, half=half, hsl=hsl):
                    return s.copy(out=dtxr[:, hsl], in_=psB[:, 0:2048])

            @sq.ph("vector")
            def _(v):
                v.tensor_scalar(ga[:, :], gt[:, :], -0.25, 0.25, ALU.mult, ALU.add)
                dtv = dtxr[:, :].rearrange("p (t d) -> p t d", d=32)
                brv = brc[:, :].rearrange("p (t u) -> p t u", u=1).broadcast_to([128, chunk, 32])
                biv = bic[:, :].rearrange("p (t u) -> p t u", u=1).broadcast_to([128, chunk, 32])
                v.tensor_tensor(out=bxre[:, :].rearrange("p (t d) -> p t d", d=32),
                                in0=dtv, in1=brv, op=ALU.mult)
                return v.tensor_tensor(out=bxim[:, :].rearrange("p (t d) -> p t d", d=32),
                                       in0=dtv, in1=biv, op=ALU.mult)

            for ct in range(chunk):
                ssl = slice(ct * 32, ct * 32 + 32)

                @sq.ph("tensor")
                def _(t, ct=ct):
                    t.matmul(psC[0:4, 0:32], en4b[:, :], hre[:, :], start=True, stop=True)
                    return t.matmul(psC[0:4, 32:64], en4b[:, :], him[:, :], start=True, stop=True)

                @sq.ph("vector")
                def _(v, ct=ct):
                    v.tensor_reduce(tiny[0:4, 0:1], psC[0:4, 0:32], AX.X, ALU.add)
                    v.tensor_reduce(tiny[0:4, 1:2], psC[0:4, 32:64], AX.X, ALU.add)
                    return v.tensor_add(tiny[0:4, 2:3], tiny[0:4, 0:1], cst[0:4, 2:3])

                @sq.ph("tensor")
                def _(t, ct=ct):
                    t.matmul(psW[:, 0:1], en4tb[:, :], tiny[0:4, 2:3], start=True, stop=True)
                    return t.matmul(psW[:, 1:2], en4tb[:, :], tiny[0:4, 1:2], start=True, stop=True)

                @sq.ph("vector")
                def _(v, ct=ct):
                    v.tensor_copy(w2[:, 0:2], psW[:, 0:2])
                    v.tensor_mul(w2[:, 2:3], xrc[:, ct:ct + 1], w2[:, 0:1])
                    return v.scalar_tensor_tensor(out=w2[:, 3:4], in0=xic[:, ct:ct + 1],
                                                  scalar=w2[:, 1:2], in1=w2[:, 2:3],
                                                  op0=ALU.mult, op1=ALU.add)

                @sq.ph("scalar")
                def _(s, ct=ct):
                    s.activation(out=w2[:, 4:5], in_=w2[:, 0:1], func=AF.Square)
                    s.activation(out=w2[:, 5:6], in_=w2[:, 1:2], func=AF.Square)
                    s.activation(out=th2[:, :], in_=th[:, :], func=AF.Square)
                    return s.activation(out=Lt[:, :], in_=th2[:, :], func=AF.Ln, bias=cst[:, 1:2])

                @sq.ph("vector")
                def _(v, ct=ct, ssl=ssl):
                    v.tensor_add(w2[:, 4:5], w2[:, 4:5], w2[:, 5:6])
                    return v.tensor_mul(glt[:, :], ga[:, ssl], Lt[:, :])

                @sq.ph("scalar")
                def _(s, ct=ct):
                    s.activation(out=w2[:, 5:6], in_=w2[:, 4:5], func=AF.Ln)
                    return s.activation(out=w2[:, 6:7], in_=w2[:, 5:6], func=AF.Exp, scale=-0.5)

                @sq.ph("vector")
                def _(v, ct=ct):
                    v.tensor_mul(w2[:, 7:8], w2[:, 3:4], w2[:, 6:7])
                    v.tensor_scalar(w2[:, 8:9], w2[:, 7:8], -0.5, 0.5, ALU.mult, ALU.add)
                    return v.tensor_scalar(ut[:, :], glt[:, :], w2[:, 8:9], None, ALU.mult)

                @sq.ph("scalar")
                def _(s, ct=ct):
                    return s.activation(out=Ft[:, :], in_=ut[:, :], func=AF.Exp)

                @sq.ph("vector")
                def _(v, ct=ct, ssl=ssl):
                    v.tensor_mul(hre[:, :], Ft[:, :], hre[:, :])
                    v.tensor_mul(him[:, :], Ft[:, :], him[:, :])
                    v.tensor_add(hre[:, :], hre[:, :], bxre[:, ssl])
                    return v.tensor_add(him[:, :], him[:, :], bxim[:, ssl])

                @sq.ph("scalar")
                def _(s, ct=ct):
                    s.activation(out=sqre[:, :], in_=hre[:, :], func=AF.Square)
                    return s.activation(out=sqim[:, :], in_=him[:, :], func=AF.Square)

                @sq.ph("vector")
                def _(v, ct=ct):
                    return v.tensor_add(m2t[:, :], sqre[:, :], sqim[:, :])

                @sq.ph("scalar")
                def _(s, ct=ct):
                    s.activation(out=L2t[:, :], in_=m2t[:, :], func=AF.Ln, bias=cst[:, 1:2])
                    s.activation(out=mg2[:, :], in_=L2t[:, :], func=AF.Exp, scale=0.5)
                    s.activation(out=th[:, :], in_=mg2[:, :], func=AF.Tanh)
                    return s.activation(out=rm2[:, :], in_=L2t[:, :], func=AF.Exp, scale=-0.5)

                @sq.ph("vector")
                def _(v, ct=ct, ssl=ssl):
                    v.tensor_mul(F2t[:, :], th[:, :], rm2[:, :])
                    v.tensor_mul(hre[:, :], F2t[:, :], hre[:, :])
                    v.tensor_mul(him[:, :], F2t[:, :], him[:, :])
                    v.tensor_scalar(yt1[:, :], him[:, :], cic[:, ct:ct + 1], None, ALU.mult)
                    return v.scalar_tensor_tensor(out=yb[:, ssl], in0=hre[:, :],
                                                  scalar=crc[:, ct:ct + 1], in1=yt1[:, :],
                                                  op0=ALU.mult, op1=ALU.add)

            for half in range(CH // 2048):
                @sq.ph("tensor")
                def _(t, half=half):
                    for j in range(4):
                        sl = slice(half * 2048 + j * 512, half * 2048 + j * 512 + 512)
                        last = t.matmul(psB[0:32, j * 512:(j + 1) * 512], en32b[:, :],
                                        yb[:, sl], start=True, stop=True)
                    return last

                @sq.ph("scalar")
                def _(s, half=half):
                    return s.copy(out=yred[0:32, half * 2048:(half + 1) * 2048], in_=psB[0:32, 0:2048])

            @sq.dma("sync")
            def _(q, t0=t0):
                return [q.dma_start(out=o_y[:, t0 * 32:(t0 + chunk) * 32], in_=yred[0:32, 0:CH])]

        @sq.dma("sync")
        def _(q):
            return [q.dma_start(out=o_hre[:, :], in_=hre[:, :]),
                    q.dma_start(out=o_him[:, :], in_=him[:, :]),
                    q.dma_start(out=o_th[:, :], in_=th[:, :])]

        sq.build(nc)
    return nc



def build_a3():
    T = TSH
    nc = new_nc()
    x_fm = nc.dram_tensor("x_fm", [D, T], F32, kind="ExternalInput").ap()
    siluz = nc.dram_tensor("siluz", [D, T], F32, kind="ExternalInput").ap()
    yps = [nc.dram_tensor(f"yp{i}", [D, T], F32, kind="ExternalInput").ap() for i in range(4)]
    w_out = nc.dram_tensor("w_out", [1025, D], F32, kind="ExternalInput").ap()
    o_x = nc.dram_tensor("o_x", [D, T], F32, kind="ExternalOutput").ap()

    sq = Seq()
    with contextlib.ExitStack() as cx:
        E = cx.enter_context
        A = E(nc.sbuf_tensor("A", [128, NCH * T], F32))
        Yg = E(nc.sbuf_tensor("Yg", [128, NCH * T], BF16))
        wob = E(nc.sbuf_tensor("wob", [128, NCH * D], BF16))
        wbias = E(nc.sbuf_tensor("wbias", [1, D], BF16))
        onesr = E(nc.sbuf_tensor("onesr", [1, 512], BF16))
        scr = E(nc.sbuf_tensor("scr", [128, 2048], F32))
        sc2 = E(nc.sbuf_tensor("sc2", [128, 2048], F32))
        sc3 = E(nc.sbuf_tensor("sc3", [128, 2048], F32))
        psM = E(nc.psum_tensor([128, 2048], F32))

        @sq.dma("sync")
        def _(q):
            return [q.dma_start(out=A[:, k * T:(k + 1) * T], in_=x_fm[k * 128:(k + 1) * 128, :])
                    for k in range(NCH)]

        @sq.ph("vector")
        def _(v):
            return v.memset(onesr[:, :], 1.0)

        for k in range(NCH):
            @sq.dma("sync")
            def _(q, k=k):
                return [q.dma_start(out=scr[:, 0:1024], in_=w_out[k * 128:(k + 1) * 128, :])]

            @sq.ph("vector")
            def _(v, k=k):
                return v.tensor_copy(wob[:, k * D:(k + 1) * D], scr[:, 0:1024])

        @sq.dma("sync")
        def _(q):
            return [q.dma_start(out=scr[0:1, 0:1024], in_=w_out[1024:1025, :])]

        @sq.ph("vector")
        def _(v):
            return v.tensor_copy(wbias[:, :], scr[0:1, 0:1024])

        for k in range(NCH):
            @sq.dma("sync")
            def _(q, k=k):
                return [q.dma_start(out=scr[:, :], in_=yps[0][k * 128:(k + 1) * 128, :]),
                        q.dma_start(out=sc2[:, :], in_=yps[1][k * 128:(k + 1) * 128, :])]

            @sq.ph("vector")
            def _(v, k=k):
                return v.tensor_add(sc3[:, :], scr[:, :], sc2[:, :])

            @sq.dma("sync")
            def _(q, k=k):
                return [q.dma_start(out=scr[:, :], in_=yps[2][k * 128:(k + 1) * 128, :]),
                        q.dma_start(out=sc2[:, :], in_=yps[3][k * 128:(k + 1) * 128, :])]

            @sq.ph("vector")
            def _(v, k=k):
                v.tensor_add(sc3[:, :], sc3[:, :], scr[:, :])
                return v.tensor_add(sc3[:, :], sc3[:, :], sc2[:, :])

            @sq.dma("sync")
            def _(q, k=k):
                return [q.dma_start(out=scr[:, :], in_=siluz[k * 128:(k + 1) * 128, :])]

            @sq.ph("vector")
            def _(v, k=k):
                return v.tensor_mul(Yg[:, k * T:(k + 1) * T], sc3[:, :], scr[:, :])

        for o in range(NCH):
            @sq.ph("tensor")
            def _(t, o=o):
                for tc in range(4):
                    for k in range(NCH):
                        t.matmul(psM[:, tc * 512:tc * 512 + 512],
                                 wob[:, k * D + o * 128:k * D + o * 128 + 128],
                                 Yg[:, k * T + tc * 512:k * T + tc * 512 + 512],
                                 start=(k == 0), stop=False)
                    last = t.matmul(psM[:, tc * 512:tc * 512 + 512],
                                    wbias[0:1, o * 128:(o + 1) * 128],
                                    onesr[0:1, :], start=False, stop=True)
                return last

            @sq.ph("vector")
            def _(v, o=o):
                return v.tensor_add(A[:, o * T:(o + 1) * T], A[:, o * T:(o + 1) * T], psM[:, 0:T])

        @sq.dma("sync")
        def _(q):
            return [q.dma_start(out=o_x[k * 128:(k + 1) * 128, :], in_=A[:, k * T:(k + 1) * T])
                    for k in range(NCH)]

        sq.build(nc)
    return nc



def build_f1():
    T = TSH
    nc = new_nc()
    x_fm = nc.dram_tensor("x_fm", [D, T], F32, kind="ExternalInput").ap()
    w1t = nc.dram_tensor("w1t", [D, 512], F32, kind="ExternalInput").ap()
    w1bot = nc.dram_tensor("w1bot", [D, 512], F32, kind="ExternalInput").ap()
    b1r = nc.dram_tensor("b1r", [1, 512], F32, kind="ExternalInput").ap()
    w2 = nc.dram_tensor("w2", [512, 64], F32, kind="ExternalInput").ap()
    slot_init = nc.dram_tensor("slot_init", [64, D], F32, kind="ExternalInput").ap()
    taub = nc.dram_tensor("taub", [128, 1], F32, kind="ExternalInput").ap()
    b2r = nc.dram_tensor("b2r", [1, 64], F32, kind="ExternalInput").ap()
    ident = nc.dram_tensor("ident", [128, 128], F32, kind="ExternalInput").ap()
    o_alpha = nc.dram_tensor("o_alpha", [T, 64], F32, kind="ExternalOutput").ap()
    o_spart = nc.dram_tensor("o_spart", [64, 1025], F32, kind="ExternalOutput").ap()

    sq = Seq()
    with contextlib.ExitStack() as cx:
        E = cx.enter_context
        A = E(nc.sbuf_tensor("A", [128, NCH * T], F32))
        Bt = E(nc.sbuf_tensor("Bt", [128, NCH * T], BF16))
        hb = E(nc.sbuf_tensor("hb", [128, 4 * T], BF16))
        w1b = E(nc.sbuf_tensor("w1b", [128, NCH * 512], BF16))
        w2b = E(nc.sbuf_tensor("w2b", [128, 4 * 64], BF16))
        sib = E(nc.sbuf_tensor("sib", [64, D], F32))
        idb = E(nc.sbuf_tensor("idb", [128, 128], F32))
        hcol = E(nc.sbuf_tensor("hcol", [128, 8], F32))
        xtm = E(nc.sbuf_tensor("xtm", [128, 1025], BF16))
        alb = E(nc.sbuf_tensor("alb", [128, 64], BF16))
        scr = E(nc.sbuf_tensor("scr", [128, 2048], F32))
        lg0 = E(nc.sbuf_tensor("lg0", [128, 64], F32))
        lgw = E(nc.sbuf_tensor("lgw", [128, 64], F32))
        ecol = E(nc.sbuf_tensor("ecol", [128, 16], F32))
        ee = E(nc.sbuf_tensor("ee", [128, 64], F32))
        msk = E(nc.sbuf_tensor("msk", [128, 64], F32))
        alphaf = E(nc.sbuf_tensor("alphaf", [128, 64], F32))
        smrow = E(nc.sbuf_tensor("smrow", [1, D], F32))
        smcol = E(nc.sbuf_tensor("smcol", [128, NCH], F32))
        ones_col = E(nc.sbuf_tensor("ones_col", [128, 1], F32))
        ones_row = E(nc.sbuf_tensor("ones_row", [1, 128], F32))
        b2s = E(nc.sbuf_tensor("b2s", [1, 64], F32))
        psA = E(nc.psum_tensor([1, 512], F32))
        psR = E(nc.psum_tensor([128, 1024], F32))
        psM = E(nc.psum_tensor([128, 2048], F32))
        psT = E(nc.psum_tensor([128, 128], F32))

        @sq.dma("sync")
        def _(q):
            ins = [q.dma_start(out=A[:, k * T:(k + 1) * T], in_=x_fm[k * 128:(k + 1) * 128, :])
                   for k in range(NCH)]
            ins += [q.dma_start(out=sib[:, :], in_=slot_init[:, :]),
                    q.dma_start(out=idb[:, :], in_=ident[:, :]),
                    q.dma_start(out=ecol[:, 1:2], in_=taub[:, :]),
                    q.dma_start(out=b2s[:, :], in_=b2r[:, :])]
            return ins

        @sq.ph("vector")
        def _(v):
            v.memset(ones_col[:, :], 1.0)
            v.memset(ones_row[:, :], 1.0)
            v.memset(xtm[:, 1024:1025], 1.0)
            return v.memset(scr[:, :], 0.0)

        @sq.ph("scalar")
        def _(s):
            return s.activation(out=ecol[:, 2:3], in_=ecol[:, 1:2], func=AF.Abs)

        @sq.ph("vector")
        def _(v):
            v.tensor_scalar_add(ecol[:, 3:4], ecol[:, 2:3], 0.1)
            return v.reciprocal(ecol[:, 1:2], ecol[:, 3:4])

        h = dict(xs=A, xln=Bt, scr=scr,
                 row_su=lg0[0:1, 0:64].tensor[0:1, 0:64] if False else None,
                 psA=psA, psR=psR, ones_col=ones_col, ones_row=ones_row)
        rowsu = E(nc.sbuf_tensor("rowsu", [1, T], F32))
        rowva = E(nc.sbuf_tensor("rowva", [1, T], F32))
        h["row_su"] = rowsu
        h["row_va"] = rowva
        h["xln2"] = A
        add_ln_phases(sq, h, T)

        @sq.dma("sync")
        def _(q):
            return [q.dma_start(out=scr[:, 0:512].rearrange("p (c n) -> p c n", c=1),
                                in_=w2.rearrange("(c p) n -> p c n", p=128)[:, 0:1, :]) if False else
                    q.dma_start(out=scr[:, 0:256].rearrange("p (c n) -> p c n", c=4),
                                in_=w2.rearrange("(c p) n -> p c n", p=128))]

        @sq.ph("vector")
        def _(v):
            return v.tensor_copy(w2b[:, :], scr[:, 0:256])

        for k in range(NCH):
            @sq.dma("sync")
            def _(q, k=k):
                return [q.dma_start(out=scr[:, 0:512], in_=w1t[k * 128:(k + 1) * 128, :])]

            @sq.ph("vector")
            def _(v, k=k):
                return v.tensor_copy(w1b[:, k * 512:(k + 1) * 512], scr[:, 0:512])

        @sq.ph("tensor")
        def _(t):
            for j in range(2):
                last = t.matmul(psA[0:1, j * 512:j * 512 + 512] if False else psA[0:1, 0:512],
                                ones_col[0:64, 0:1], sib[:, j * 512:(j + 1) * 512],
                                start=True, stop=True)
            return last

        sq.phases.pop()
        for j in range(2):
            @sq.ph("tensor")
            def _(t, j=j):
                return t.matmul(psA[0:1, 0:512], ones_col[0:64, 0:1],
                                sib[:, j * 512:(j + 1) * 512], start=True, stop=True)

            @sq.ph("vector")
            def _(v, j=j):
                return v.tensor_scalar(smrow[0:1, j * 512:(j + 1) * 512], psA[0:1, 0:512],
                                       1.0 / 64.0, None, ALU.mult)

        for k in range(NCH):
            @sq.ph("tensor")
            def _(t, k=k):
                t.transpose(psT[:, 0:1], smrow[0:1, k * 128:(k + 1) * 128], idb[0:1, 0:1])
                return t.matmul(psT[0:1, 2:3], ones_col[0:1, 0:1], ones_col[0:1, 0:1],
                                start=True, stop=True)

            @sq.ph("vector")
            def _(v, k=k):
                return v.tensor_copy(smcol[:, k:k + 1], psT[:, 0:1])

        for o in range(4):
            @sq.dma("sync")
            def _(q, o=o):
                return [q.dma_start(out=scr[:, 0:1024].rearrange("p (c n) -> p c n", c=8),
                                    in_=w1bot.rearrange("(c p) n -> p c n", p=128)[:, :, o * 128:(o + 1) * 128])]

            @sq.ph("vector")
            def _(v, o=o):
                return v.tensor_copy(scr[:, 1024:2048], scr[:, 0:1024])

            @sq.ph("tensor")
            def _(t, o=o):
                for k in range(NCH):
                    last = t.matmul(psT[:, 4:5], scr[:, 1024 + k * 128:1024 + (k + 1) * 128],
                                    smcol[:, k:k + 1], start=(k == 0), stop=(k == NCH - 1))
                return last

            @sq.ph("vector")
            def _(v, o=o):
                return v.tensor_copy(hcol[:, o:o + 1], psT[:, 4:5])

        @sq.dma("sync")
        def _(q):
            return [q.dma_start(out=scr[0:1, 0:512], in_=b1r[:, :])]

        for o in range(4):
            @sq.ph("tensor")
            def _(t, o=o):
                t.transpose(psT[:, 0:1], scr[0:1, o * 128:(o + 1) * 128], idb[0:1, 0:1])
                return t.matmul(psT[0:1, 2:3], ones_col[0:1, 0:1], ones_col[0:1, 0:1],
                                start=True, stop=True)

            @sq.ph("vector")
            def _(v, o=o):
                return v.tensor_add(hcol[:, o:o + 1], hcol[:, o:o + 1], psT[:, 0:1])

        for o in range(4):
            @sq.ph("tensor")
            def _(t, o=o):
                for tc in range(4):
                    for k in range(NCH):
                        last = t.matmul(psM[:, tc * 512:tc * 512 + 512],
                                        w1b[:, k * 512 + o * 128:k * 512 + o * 128 + 128],
                                        Bt[:, k * T + tc * 512:k * T + tc * 512 + 512],
                                        start=(k == 0), stop=(k == NCH - 1))
                return last

            @sq.ph("scalar")
            def _(s, o=o):
                return s.activation(out=hb[:, o * T:(o + 1) * T], in_=psM[:, 0:T],
                                    func=AF.Gelu, bias=hcol[:, o:o + 1])

        for tl in range(16):
            tsl = slice(tl * 128, (tl + 1) * 128)

            @sq.ph("tensor")
            def _(t, tl=tl, tsl=tsl):
                for k in range(4):
                    t.matmul(psT[:, 0:64], hb[:, k * T + tl * 128:k * T + tl * 128 + 128],
                             w2b[:, k * 64:(k + 1) * 64], start=(k == 0), stop=False)
                return t.matmul(psT[:, 0:64], ones_row[:, :], b2s[:, :],
                                start=False, stop=True)

            @sq.ph("vector")
            def _(v, tl=tl):
                v.tensor_copy(lg0[:, :], psT[:, 0:64])
                return v.tensor_copy(lgw[:, :], psT[:, 0:64])

            for it in range(8):
                @sq.ph("vector")
                def _(v, tl=tl, it=it):
                    src = lgw if it % 2 == 0 else ee
                    dst = ee if it % 2 == 0 else lgw
                    mx = ecol[:, 4:5] if it == 0 else ecol[:, 5:6]
                    v.reduce_max(mx, src[:, :], AX.X)
                    v.tensor_scalar(msk[:, :], src[:, :], mx, None, ALU.is_ge)
                    return v.scalar_tensor_tensor(out=dst[:, :], in0=msk[:, :], scalar=-1e30,
                                                  in1=src[:, :], op0=ALU.mult, op1=ALU.add)

            @sq.ph("vector")
            def _(v, tl=tl):
                v.tensor_mul(ecol[:, 6:7], ecol[:, 4:5], ecol[:, 1:2])
                return v.tensor_scalar(ecol[:, 7:8], ecol[:, 6:7], -1.0, None, ALU.mult)

            @sq.ph("scalar")
            def _(s, tl=tl):
                return s.activation(out=ee[:, :], in_=lg0[:, :], func=AF.Exp,
                                    scale=ecol[:, 1:2], bias=ecol[:, 7:8])

            @sq.ph("vector")
            def _(v, tl=tl, tsl=tsl):
                v.tensor_scalar(msk[:, :], lgw[:, :], -1e29, None, ALU.is_lt)
                v.tensor_mul(alphaf[:, :], ee[:, :], msk[:, :])
                v.reduce_sum(ecol[:, 8:9], alphaf[:, :], AX.X)
                v.reciprocal(ecol[:, 9:10], ecol[:, 8:9])
                v.tensor_scalar(ee[:, :], alphaf[:, :], ecol[:, 9:10], None, ALU.mult)
                v.tensor_copy(alphaf[:, :], ee[:, :])
                return v.tensor_copy(alb[:, :], ee[:, :])

            @sq.dma("sync")
            def _(q, tl=tl, tsl=tsl):
                return [q.dma_start(out=o_alpha[tsl, :], in_=alphaf[:, :])]

            for k in range(NCH):
                @sq.ph("tensor")
                def _(t, tl=tl, k=k):
                    t.transpose(psT[:, 0:128], A[:, k * T + tl * 128:k * T + tl * 128 + 128], idb[:, :])
                    return t.matmul(psT[0:1, 2:3], ones_col[0:1, 0:1], ones_col[0:1, 0:1],
                                    start=True, stop=True)

                @sq.ph("vector")
                def _(v, tl=tl, k=k):
                    return v.tensor_copy(xtm[:, k * 128:(k + 1) * 128], psT[:, 0:128])

            @sq.ph("tensor")
            def _(t, tl=tl):
                t.matmul(psM[0:64, 0:512], alb[:, :], xtm[:, 0:512], start=(tl == 0), stop=(tl == 15))
                t.matmul(psM[0:64, 512:1024], alb[:, :], xtm[:, 512:1024], start=(tl == 0), stop=(tl == 15))
                return t.matmul(psM[0:64, 1024:1025], alb[:, :], xtm[:, 1024:1025],
                                start=(tl == 0), stop=(tl == 15))

        @sq.ph("vector")
        def _(v):
            v.tensor_copy(scr[0:64, 0:1024], psM[0:64, 0:1024])
            return v.tensor_copy(scr[0:64, 1024:1025], psM[0:64, 1024:1025])

        @sq.dma("sync")
        def _(q):
            return [q.dma_start(out=o_spart[:, :], in_=scr[0:64, 0:1025])]

        sq.build(nc)
    return nc



def build_f2():
    nc = new_nc()
    sps = [nc.dram_tensor(f"sp{i}", [64, 1025], F32, kind="ExternalInput").ap() for i in range(4)]
    slot_init = nc.dram_tensor("slot_init", [64, D], F32, kind="ExternalInput").ap()
    wihT = nc.dram_tensor("wihT", [1025, 3072], F32, kind="ExternalInput").ap()
    whhT = nc.dram_tensor("whhT", [1025, 3072], F32, kind="ExternalInput").ap()
    wq = nc.dram_tensor("wq", [D, 256], F32, kind="ExternalInput").ap()
    wk = nc.dram_tensor("wk", [D, 256], F32, kind="ExternalInput").ap()
    wv = nc.dram_tensor("wv", [D, D], F32, kind="ExternalInput").ap()
    wvp = nc.dram_tensor("wvp", [D, D], F32, kind="ExternalInput").ap()
    bq = nc.dram_tensor("bq", [256, 1], F32, kind="ExternalInput").ap()
    bk = nc.dram_tensor("bk", [256, 1], F32, kind="ExternalInput").ap()
    bv = nc.dram_tensor("bv", [D, 1], F32, kind="ExternalInput").ap()
    bvp = nc.dram_tensor("bvp", [1, D], F32, kind="ExternalInput").ap()
    lng = nc.dram_tensor("lng", [64, D], F32, kind="ExternalInput").ap()
    lnb = nc.dram_tensor("lnb", [64, D], F32, kind="ExternalInput").ap()
    ident = nc.dram_tensor("ident", [128, 128], F32, kind="ExternalInput").ap()
    o_sv = nc.dram_tensor("o_sv", [64, D], F32, kind="ExternalOutput").ap()

    sq = Seq()
    with contextlib.ExitStack() as cx:
        E = cx.enter_context
        spt = [E(nc.sbuf_tensor(f"spt{i}", [64, 1025], F32)) for i in range(4)]
        S0 = E(nc.sbuf_tensor("S0", [64, D], F32))
        si = E(nc.sbuf_tensor("si", [64, D], F32))
        gi = E(nc.sbuf_tensor("gi", [64, 3072], F32))
        gh = E(nc.sbuf_tensor("gh", [64, 3072], F32))
        wbig = E(nc.sbuf_tensor("wbig", [128, NCH * 3072], BF16))
        brow = E(nc.sbuf_tensor("brow", [1, 3072], BF16))
        wqb = E(nc.sbuf_tensor("wqb", [128, NCH * 256], BF16))
        wkb = E(nc.sbuf_tensor("wkb", [128, NCH * 256], BF16))
        wvb = E(nc.sbuf_tensor("wvb", [128, NCH * D], BF16))
        sT = E(nc.sbuf_tensor("sT", [128, NCH * 64], BF16))
        qf = E(nc.sbuf_tensor("qf", [128, 2 * 64], BF16))
        kf = E(nc.sbuf_tensor("kf", [128, 2 * 64], BF16))
        vf = E(nc.sbuf_tensor("vf", [128, NCH * 64], F32))
        vtm = E(nc.sbuf_tensor("vtm", [64, NCH * 128], BF16))
        Pb = E(nc.sbuf_tensor("Pb", [64, 64], F32))
        PTb = E(nc.sbuf_tensor("PTb", [64, 64], BF16))
        yy = E(nc.sbuf_tensor("yy", [64, D], F32))
        lngt = E(nc.sbuf_tensor("lngt", [64, D], F32))
        lnbt = E(nc.sbuf_tensor("lnbt", [64, D], F32))
        colz = E(nc.sbuf_tensor("colz", [64, 16], F32))
        bcol = E(nc.sbuf_tensor("bcol", [128, 12], F32))
        scr = E(nc.sbuf_tensor("scr", [128, 3072], F32))
        ones_col = E(nc.sbuf_tensor("ones_col", [128, 1], F32))
        ones_rowb = E(nc.sbuf_tensor("ones_rowb", [1, 64], BF16))
        idb = E(nc.sbuf_tensor("idb", [128, 128], F32))
        psM = E(nc.psum_tensor([128, 1024], F32))
        psT = E(nc.psum_tensor([128, 144], F32))

        @sq.dma("sync")
        def _(q):
            ins = [q.dma_start(out=spt[i][:, :], in_=sps[i][:, :]) for i in range(4)]
            ins += [q.dma_start(out=S0[:, :], in_=slot_init[:, :]),
                    q.dma_start(out=lngt[:, :], in_=lng[:, :]),
                    q.dma_start(out=lnbt[:, :], in_=lnb[:, :]),
                    q.dma_start(out=idb[:, :], in_=ident[:, :])]
            return ins

        @sq.ph("vector")
        def _(v):
            v.memset(ones_col[:, :], 1.0)
            return v.memset(ones_rowb[:, :], 1.0)

        @sq.ph("vector")
        def _(v):
            v.tensor_add(spt[0][:, :], spt[0][:, :], spt[1][:, :])
            v.tensor_add(spt[2][:, :], spt[2][:, :], spt[3][:, :])
            v.tensor_add(spt[0][:, :], spt[0][:, :], spt[2][:, :])
            v.tensor_scalar_add(colz[:, 0:1], spt[0][:, 1024:1025], 1e-8)
            v.reciprocal(colz[:, 1:2], colz[:, 0:1])
            return v.tensor_scalar(si[:, :], spt[0][:, 0:1024], colz[:, 1:2], None, ALU.mult)

        def add_transpose_to(sq, src, dst, nchunks, width=64, cast=True):
            for k in range(nchunks):
                @sq.ph("tensor")
                def _(t, k=k):
                    t.transpose(psT[:, 0:width], src[0:width, k * 128:(k + 1) * 128], idb[0:width, 0:width])
                    return t.matmul(psT[0:1, 128:129], ones_col[0:1, 0:1], ones_col[0:1, 0:1],
                                    start=True, stop=True)

                @sq.ph("vector")
                def _(v, k=k):
                    return v.tensor_copy(dst[:, k * width:(k + 1) * width], psT[:, 0:width])

        add_transpose_to(sq, si, sT, NCH)

        for k in range(NCH):
            @sq.dma("sync")
            def _(q, k=k):
                return [q.dma_start(out=scr[:, :], in_=wihT[k * 128:(k + 1) * 128, :])]

            @sq.ph("vector")
            def _(v, k=k):
                return v.tensor_copy(wbig[:, k * 3072:(k + 1) * 3072], scr[:, :])

        @sq.dma("sync")
        def _(q):
            return [q.dma_start(out=scr[0:1, 0:3072], in_=wihT[1024:1025, :])]

        @sq.ph("vector")
        def _(v):
            return v.tensor_copy(brow[:, :], scr[0:1, 0:3072])

        def add_gru_matmul(sq, out_t):
            for sl in range(6):
                @sq.ph("tensor")
                def _(t, sl=sl):
                    for k in range(NCH):
                        t.matmul(psM[0:64, 0:512], sT[:, k * 64:(k + 1) * 64],
                                 wbig[:, k * 3072 + sl * 512:k * 3072 + sl * 512 + 512],
                                 start=(k == 0), stop=False)
                    return t.matmul(psM[0:64, 0:512], ones_rowb[0:1, :],
                                    brow[0:1, sl * 512:(sl + 1) * 512], start=False, stop=True)

                @sq.ph("vector")
                def _(v, sl=sl):
                    return v.tensor_copy(out_t[:, sl * 512:(sl + 1) * 512], psM[0:64, 0:512])

        add_gru_matmul(sq, gi)

        add_transpose_to(sq, S0, sT, NCH)

        for k in range(NCH):
            @sq.dma("sync")
            def _(q, k=k):
                return [q.dma_start(out=scr[:, :], in_=whhT[k * 128:(k + 1) * 128, :])]

            @sq.ph("vector")
            def _(v, k=k):
                return v.tensor_copy(wbig[:, k * 3072:(k + 1) * 3072], scr[:, :])

        @sq.dma("sync")
        def _(q):
            return [q.dma_start(out=scr[0:1, 0:3072], in_=whhT[1024:1025, :])]

        @sq.ph("vector")
        def _(v):
            return v.tensor_copy(brow[:, :], scr[0:1, 0:3072])

        add_gru_matmul(sq, gh)

        @sq.ph("vector")
        def _(v):
            v.tensor_add(scr[0:64, 0:1024], gi[:, 0:1024], gh[:, 0:1024])
            return v.tensor_add(scr[0:64, 1024:2048], gi[:, 1024:2048], gh[:, 1024:2048])

        @sq.ph("scalar")
        def _(s):
            s.activation(out=scr[0:64, 0:1024], in_=scr[0:64, 0:1024], func=AF.Sigmoid)
            return s.activation(out=scr[0:64, 1024:2048], in_=scr[0:64, 1024:2048], func=AF.Sigmoid)

        @sq.ph("vector")
        def _(v):
            v.tensor_mul(scr[0:64, 2048:3072], scr[0:64, 0:1024], gh[:, 2048:3072])
            return v.tensor_add(scr[0:64, 2048:3072], gi[:, 2048:3072], scr[0:64, 2048:3072])

        @sq.ph("scalar")
        def _(s):
            return s.activation(out=scr[0:64, 2048:3072], in_=scr[0:64, 2048:3072], func=AF.Tanh)

        @sq.ph("vector")
        def _(v):
            z = scr[0:64, 1024:2048]
            n = scr[0:64, 2048:3072]
            v.tensor_mul(gi[:, 0:1024], z, n)
            v.tensor_sub(gi[:, 1024:2048], n, gi[:, 0:1024])
            v.tensor_mul(gi[:, 2048:3072], z, S0[:, :])
            return v.tensor_add(si[:, :], gi[:, 1024:2048], gi[:, 2048:3072])

        add_transpose_to(sq, si, sT, NCH)

        @sq.dma("sync")
        def _(q):
            return [q.dma_start(out=scr[:, 0:2048].rearrange("p (c n) -> p c n", c=8),
                                in_=wq.rearrange("(c p) n -> p c n", p=128))]

        @sq.ph("vector")
        def _(v):
            return v.tensor_copy(wqb[:, :], scr[:, 0:2048])

        @sq.dma("sync")
        def _(q):
            return [q.dma_start(out=scr[:, 0:2048].rearrange("p (c n) -> p c n", c=8),
                                in_=wk.rearrange("(c p) n -> p c n", p=128))]

        @sq.ph("vector")
        def _(v):
            return v.tensor_copy(wkb[:, :], scr[:, 0:2048])

        for k in range(NCH):
            @sq.dma("sync")
            def _(q, k=k):
                return [q.dma_start(out=scr[:, 0:1024], in_=wv[k * 128:(k + 1) * 128, :])]

            @sq.ph("vector")
            def _(v, k=k):
                return v.tensor_copy(wvb[:, k * D:(k + 1) * D], scr[:, 0:1024])

        @sq.dma("sync")
        def _(q):
            ds = []
            for c in range(2):
                ds.append(q.dma_start(out=bcol[:, c:c + 1], in_=bq[c * 128:(c + 1) * 128, :]))
                ds.append(q.dma_start(out=bcol[:, 2 + c:3 + c], in_=bk[c * 128:(c + 1) * 128, :]))
            for c in range(NCH):
                ds.append(q.dma_start(out=bcol[:, 4 + c:5 + c], in_=bv[c * 128:(c + 1) * 128, :]))
            return ds

        for o in range(2):
            @sq.ph("tensor")
            def _(t, o=o):
                for k in range(NCH):
                    last = t.matmul(psT[:, 0:64], wqb[:, k * 256 + o * 128:k * 256 + o * 128 + 128],
                                    sT[:, k * 64:(k + 1) * 64], start=(k == 0), stop=(k == NCH - 1))
                return last

            @sq.ph("scalar")
            def _(s, o=o):
                return s.activation(out=qf[:, o * 64:(o + 1) * 64], in_=psT[:, 0:64],
                                    func=AF.Identity, bias=bcol[:, o:o + 1])

            @sq.ph("tensor")
            def _(t, o=o):
                for k in range(NCH):
                    last = t.matmul(psT[:, 0:64], wkb[:, k * 256 + o * 128:k * 256 + o * 128 + 128],
                                    sT[:, k * 64:(k + 1) * 64], start=(k == 0), stop=(k == NCH - 1))
                return last

            @sq.ph("scalar")
            def _(s, o=o):
                return s.activation(out=kf[:, o * 64:(o + 1) * 64], in_=psT[:, 0:64],
                                    func=AF.Identity, bias=bcol[:, 2 + o:3 + o])

        for o in range(NCH):
            @sq.ph("tensor")
            def _(t, o=o):
                for k in range(NCH):
                    last = t.matmul(psT[:, 0:64], wvb[:, k * D + o * 128:k * D + o * 128 + 128],
                                    sT[:, k * 64:(k + 1) * 64], start=(k == 0), stop=(k == NCH - 1))
                return last

            @sq.ph("scalar")
            def _(s, o=o):
                return s.activation(out=vf[:, o * 64:(o + 1) * 64], in_=psT[:, 0:64],
                                    func=AF.Identity, bias=bcol[:, 4 + o:5 + o])

        @sq.ph("vector")
        def _(v):
            v.tensor_copy(scr[:, 0:64], qf[:, 0:64])
            return v.tensor_copy(scr[:, 64:128], qf[:, 64:128])

        @sq.ph("tensor")
        def _(t):
            t.matmul(psT[0:64, 0:64], qf[:, 0:64], kf[:, 0:64], start=True, stop=False)
            return t.matmul(psT[0:64, 0:64], qf[:, 64:128], kf[:, 64:128], start=False, stop=True)

        @sq.ph("vector")
        def _(v):
            v.tensor_copy(Pb[:, :], psT[0:64, 0:64])
            v.reduce_max(colz[:, 2:3], Pb[:, :], AX.X)
            return v.tensor_scalar(colz[:, 3:4], colz[:, 2:3], -0.0625, None, ALU.mult)

        @sq.ph("scalar")
        def _(s):
            return s.activation(out=Pb[:, :], in_=Pb[:, :], func=AF.Exp, scale=0.0625,
                                bias=colz[:, 3:4])

        @sq.ph("vector")
        def _(v):
            v.reduce_sum(colz[:, 4:5], Pb[:, :], AX.X)
            v.reciprocal(colz[:, 5:6], colz[:, 4:5])
            v.tensor_scalar(scr[0:64, 0:64], Pb[:, :], colz[:, 5:6], None, ALU.mult)
            return v.tensor_copy(Pb[:, :], scr[0:64, 0:64])

        @sq.ph("tensor")
        def _(t):
            t.transpose(psT[0:64, 64:128], Pb[:, :], idb[0:64, 0:64])
            return t.matmul(psT[0:1, 128:129], ones_col[0:1, 0:1], ones_col[0:1, 0:1],
                            start=True, stop=True)

        @sq.ph("vector")
        def _(v):
            return v.tensor_copy(PTb[:, :], psT[0:64, 64:128])

        for k in range(NCH):
            @sq.ph("vector")
            def _(v, k=k):
                return v.tensor_copy(scr[:, 0:64], vf[:, k * 64:(k + 1) * 64])

            @sq.ph("tensor")
            def _(t, k=k):
                t.transpose(psT[0:64, 0:128], scr[:, 0:64], idb[:, :])
                return t.matmul(psT[0:1, 128:129], ones_col[0:1, 0:1], ones_col[0:1, 0:1],
                                start=True, stop=True)

            @sq.ph("vector")
            def _(v, k=k):
                return v.tensor_copy(vtm[:, k * 128:(k + 1) * 128], psT[0:64, 0:128])

        for sl in range(2):
            @sq.ph("tensor")
            def _(t, sl=sl):
                return t.matmul(psM[0:64, 0:512], PTb[:, :], vtm[:, sl * 512:(sl + 1) * 512],
                                start=True, stop=True)

            @sq.ph("vector")
            def _(v, sl=sl):
                return v.tensor_add(yy[:, sl * 512:(sl + 1) * 512], si[:, sl * 512:(sl + 1) * 512],
                                    psM[0:64, 0:512])

        @sq.ph("vector")
        def _(v):
            return v.reduce_sum(colz[:, 6:7], yy[:, :], AX.X)

        @sq.ph("scalar")
        def _(s):
            return s.activation(out=scr[0:64, 0:1024], in_=yy[:, :], func=AF.Square)

        @sq.ph("vector")
        def _(v):
            v.reduce_sum(colz[:, 7:8], scr[0:64, 0:1024], AX.X)
            v.tensor_scalar(colz[:, 6:7], colz[:, 6:7], 1.0 / D, None, ALU.mult)
            return v.tensor_scalar(colz[:, 7:8], colz[:, 7:8], 1.0 / D, None, ALU.mult)

        @sq.ph("scalar")
        def _(s):
            return s.activation(out=colz[:, 8:9], in_=colz[:, 6:7], func=AF.Square)

        @sq.ph("vector")
        def _(v):
            v.tensor_sub(colz[:, 9:10], colz[:, 7:8], colz[:, 8:9])
            return v.tensor_scalar_add(colz[:, 9:10], colz[:, 9:10], 1e-5)

        @sq.ph("scalar")
        def _(s):
            s.activation(out=colz[:, 10:11], in_=colz[:, 9:10], func=AF.Ln)
            return s.activation(out=colz[:, 11:12], in_=colz[:, 10:11], func=AF.Exp, scale=-0.5)

        @sq.ph("vector")
        def _(v):
            v.tensor_scalar(scr[0:64, 0:1024], yy[:, :], colz[:, 6:7], None, ALU.subtract)
            v.tensor_scalar(scr[0:64, 1024:2048], scr[0:64, 0:1024], colz[:, 11:12], None, ALU.mult)
            v.tensor_mul(scr[0:64, 0:1024], scr[0:64, 1024:2048], lngt[:, :])
            return v.tensor_add(yy[:, :], scr[0:64, 0:1024], lnbt[:, :])

        add_transpose_to(sq, yy, sT, NCH)

        for k in range(NCH):
            @sq.dma("sync")
            def _(q, k=k):
                return [q.dma_start(out=scr[:, 0:1024], in_=wvp[k * 128:(k + 1) * 128, :])]

            @sq.ph("vector")
            def _(v, k=k):
                return v.tensor_copy(wvb[:, k * D:(k + 1) * D], scr[:, 0:1024])

        @sq.dma("sync")
        def _(q):
            return [q.dma_start(out=scr[0:1, 0:1024], in_=bvp[:, :])]

        @sq.ph("vector")
        def _(v):
            return v.tensor_copy(brow[:, 0:1024], scr[0:1, 0:1024])

        for sl in range(2):
            @sq.ph("tensor")
            def _(t, sl=sl):
                for k in range(NCH):
                    t.matmul(psM[0:64, 0:512], sT[:, k * 64:(k + 1) * 64],
                             wvb[:, k * D + sl * 512:k * D + sl * 512 + 512],
                             start=(k == 0), stop=False)
                return t.matmul(psM[0:64, 0:512], ones_rowb[0:1, :],
                                brow[0:1, sl * 512:(sl + 1) * 512], start=False, stop=True)

            @sq.ph("vector")
            def _(v, sl=sl):
                return v.tensor_copy(yy[:, sl * 512:(sl + 1) * 512], psM[0:64, 0:512])

        @sq.dma("sync")
        def _(q):
            return [q.dma_start(out=o_sv[:, :], in_=yy[:, :])]

        sq.build(nc)
    return nc



def build_f3():
    T = TSH
    nc = new_nc()
    x_fm = nc.dram_tensor("x_fm", [D, T], F32, kind="ExternalInput").ap()
    alphaT = nc.dram_tensor("alphaT", [64, T], F32, kind="ExternalInput").ap()
    sv = nc.dram_tensor("sv", [64, D], F32, kind="ExternalInput").ap()
    wo = nc.dram_tensor("wo", [1025, D], F32, kind="ExternalInput").ap()
    o_x = nc.dram_tensor("o_x", [D, T], F32, kind="ExternalOutput").ap()

    sq = Seq()
    with contextlib.ExitStack() as cx:
        E = cx.enter_context
        A = E(nc.sbuf_tensor("A", [128, NCH * T], F32))
        U = E(nc.sbuf_tensor("U", [128, NCH * T], BF16))
        alb = E(nc.sbuf_tensor("alb", [64, T], BF16))
        svb = E(nc.sbuf_tensor("svb", [64, D], BF16))
        wob = E(nc.sbuf_tensor("wob", [128, NCH * D], BF16))
        wbias = E(nc.sbuf_tensor("wbias", [1, D], BF16))
        onesr = E(nc.sbuf_tensor("onesr", [1, 512], BF16))
        scr = E(nc.sbuf_tensor("scr", [128, 2048], F32))
        psM = E(nc.psum_tensor([128, 2048], F32))

        @sq.dma("sync")
        def _(q):
            ins = [q.dma_start(out=A[:, k * T:(k + 1) * T], in_=x_fm[k * 128:(k + 1) * 128, :])
                   for k in range(NCH)]
            ins.append(q.dma_start(out=scr[0:64, 0:T], in_=alphaT[:, :]))
            return ins

        @sq.ph("vector")
        def _(v):
            v.memset(onesr[:, :], 1.0)
            return v.tensor_copy(alb[:, :], scr[0:64, 0:T])

        @sq.dma("sync")
        def _(q):
            return [q.dma_start(out=scr[0:64, 0:1024], in_=sv[:, :])]

        @sq.ph("vector")
        def _(v):
            return v.tensor_copy(svb[:, :], scr[0:64, 0:1024])

        for o in range(NCH):
            @sq.ph("tensor")
            def _(t, o=o):
                for tc in range(4):
                    last = t.matmul(psM[:, tc * 512:tc * 512 + 512],
                                    svb[:, o * 128:(o + 1) * 128],
                                    alb[:, tc * 512:tc * 512 + 512], start=True, stop=True)
                return last

            @sq.ph("vector")
            def _(v, o=o):
                return v.tensor_copy(U[:, o * T:(o + 1) * T], psM[:, 0:T])

        for k in range(NCH):
            @sq.dma("sync")
            def _(q, k=k):
                return [q.dma_start(out=scr[:, 0:1024], in_=wo[k * 128:(k + 1) * 128, :])]

            @sq.ph("vector")
            def _(v, k=k):
                return v.tensor_copy(wob[:, k * D:(k + 1) * D], scr[:, 0:1024])

        @sq.dma("sync")
        def _(q):
            return [q.dma_start(out=scr[0:1, 0:1024], in_=wo[1024:1025, :])]

        @sq.ph("vector")
        def _(v):
            return v.tensor_copy(wbias[:, :], scr[0:1, 0:1024])

        for o in range(NCH):
            @sq.ph("tensor")
            def _(t, o=o):
                for tc in range(4):
                    for k in range(NCH):
                        t.matmul(psM[:, tc * 512:tc * 512 + 512],
                                 wob[:, k * D + o * 128:k * D + o * 128 + 128],
                                 U[:, k * T + tc * 512:k * T + tc * 512 + 512],
                                 start=(k == 0), stop=False)
                    last = t.matmul(psM[:, tc * 512:tc * 512 + 512],
                                    wbias[0:1, o * 128:(o + 1) * 128],
                                    onesr[0:1, :], start=False, stop=True)
                return last

            @sq.ph("vector")
            def _(v, o=o):
                return v.tensor_add(A[:, o * T:(o + 1) * T], A[:, o * T:(o + 1) * T], psM[:, 0:T])

        @sq.dma("sync")
        def _(q):
            return [q.dma_start(out=o_x[k * 128:(k + 1) * 128, :], in_=A[:, k * T:(k + 1) * T])
                    for k in range(NCH)]

        sq.build(nc)
    return nc



def build_lnonly():
    T = TSH
    nc = new_nc()
    x_fm = nc.dram_tensor("x_fm", [D, T], F32, kind="ExternalInput").ap()
    o_xln = nc.dram_tensor("o_xln", [D, T], BF16, kind="ExternalOutput").ap()
    sq = Seq()
    with contextlib.ExitStack() as cx:
        E = cx.enter_context
        A = E(nc.sbuf_tensor("A", [128, NCH * T], F32))
        Bt = E(nc.sbuf_tensor("Bt", [128, NCH * T], BF16))
        scr = E(nc.sbuf_tensor("scr", [128, 2048], F32))
        rowsu = E(nc.sbuf_tensor("rowsu", [1, T], F32))
        rowva = E(nc.sbuf_tensor("rowva", [1, T], F32))
        ones_col = E(nc.sbuf_tensor("ones_col", [128, 1], F32))
        ones_row = E(nc.sbuf_tensor("ones_row", [1, 128], F32))
        psA = E(nc.psum_tensor([1, 512], F32))
        psR = E(nc.psum_tensor([128, 1024], F32))

        @sq.dma("sync")
        def _(q):
            return [q.dma_start(out=A[:, k * T:(k + 1) * T], in_=x_fm[k * 128:(k + 1) * 128, :])
                    for k in range(NCH)]

        @sq.ph("vector")
        def _(v):
            v.memset(ones_col[:, :], 1.0)
            return v.memset(ones_row[:, :], 1.0)

        h = dict(xs=A, xln=Bt, scr=scr, row_su=rowsu, row_va=rowva,
                 psA=psA, psR=psR, ones_col=ones_col, ones_row=ones_row)
        add_ln_phases(sq, h, T)

        @sq.dma("sync")
        def _(q):
            return [q.dma_start(out=o_xln[k * 128:(k + 1) * 128, :], in_=Bt[:, k * T:(k + 1) * T])
                    for k in range(NCH)]

        sq.build(nc)
    return nc



def build_c1():
    T = TSH
    nc = new_nc()
    x_fm = nc.dram_tensor("x_fm", [D, T], F32, kind="ExternalInput").ap()
    w_in = nc.dram_tensor("w_in", [1025, 2 * D], F32, kind="ExternalInput").ap()
    o_q = nc.dram_tensor("o_q", [D, T], BF16, kind="ExternalOutput").ap()
    o_v = nc.dram_tensor("o_v", [D, T], F32, kind="ExternalOutput").ap()

    sq = Seq()
    with contextlib.ExitStack() as cx:
        E = cx.enter_context
        A = E(nc.sbuf_tensor("A", [128, NCH * T], F32))
        Bt = E(nc.sbuf_tensor("Bt", [128, NCH * T], BF16))
        wb = E(nc.sbuf_tensor("wb", [128, NCH * 2048], BF16))
        wbias = E(nc.sbuf_tensor("wbias", [1, 2048], BF16))
        onesr = E(nc.sbuf_tensor("onesr", [1, 512], BF16))
        scr = E(nc.sbuf_tensor("scr", [128, 2048], F32))
        rowsu = E(nc.sbuf_tensor("rowsu", [1, T], F32))
        rowva = E(nc.sbuf_tensor("rowva", [1, T], F32))
        qtmp = E(nc.sbuf_tensor("qtmp", [128, T], BF16))
        ones_col = E(nc.sbuf_tensor("ones_col", [128, 1], F32))
        ones_row = E(nc.sbuf_tensor("ones_row", [1, 128], F32))
        psA = E(nc.psum_tensor([1, 512], F32))
        psR = E(nc.psum_tensor([128, 1024], F32))
        psM = E(nc.psum_tensor([128, 2048], F32))

        @sq.dma("sync")
        def _(q):
            return [q.dma_start(out=A[:, k * T:(k + 1) * T], in_=x_fm[k * 128:(k + 1) * 128, :])
                    for k in range(NCH)]

        @sq.ph("vector")
        def _(v):
            v.memset(ones_col[:, :], 1.0)
            v.memset(ones_row[:, :], 1.0)
            return v.memset(onesr[:, :], 1.0)

        h = dict(xs=A, xln=Bt, scr=scr, row_su=rowsu, row_va=rowva,
                 psA=psA, psR=psR, ones_col=ones_col, ones_row=ones_row)
        add_ln_phases(sq, h, T)

        for k in range(NCH):
            @sq.dma("sync")
            def _(q, k=k):
                return [q.dma_start(out=scr[:, :], in_=w_in[k * 128:(k + 1) * 128, :])]

            @sq.ph("vector")
            def _(v, k=k):
                return v.tensor_copy(wb[:, k * 2048:(k + 1) * 2048], scr[:, :])

        @sq.dma("sync")
        def _(q):
            return [q.dma_start(out=scr[0:1, 0:2048], in_=w_in[1024:1025, :])]

        @sq.ph("vector")
        def _(v):
            return v.tensor_copy(wbias[:, :], scr[0:1, 0:2048])

        for o in range(16):
            @sq.ph("tensor")
            def _(t, o=o):
                for tc in range(4):
                    for k in range(NCH):
                        t.matmul(psM[:, tc * 512:tc * 512 + 512],
                                 wb[:, k * 2048 + o * 128:k * 2048 + o * 128 + 128],
                                 Bt[:, k * T + tc * 512:k * T + tc * 512 + 512],
                                 start=(k == 0), stop=False)
                    last = t.matmul(psM[:, tc * 512:tc * 512 + 512],
                                    wbias[0:1, o * 128:(o + 1) * 128],
                                    onesr[0:1, :], start=False, stop=True)
                return last

            if o < 8:
                @sq.ph("vector")
                def _(v, o=o):
                    return v.tensor_copy(qtmp[:, :], psM[:, 0:T])

                @sq.dma("sync")
                def _(q, o=o):
                    return [q.dma_start(out=o_q[o * 128:(o + 1) * 128, :], in_=qtmp[:, :])]
            else:
                @sq.ph("vector")
                def _(v, o=o):
                    return v.tensor_copy(scr[:, 0:T], psM[:, 0:T])

                @sq.dma("sync")
                def _(q, o=o):
                    return [q.dma_start(out=o_v[(o - 8) * 128:(o - 7) * 128, :], in_=scr[:, 0:T])]

        sq.build(nc)
    return nc



def build_c2(variant):
    T = TSH
    HB = 2047
    W = HB + T
    nc = new_nc()
    qext = nc.dram_tensor("qext", [D, W], BF16, kind="ExternalInput").ap()
    wf = nc.dram_tensor("wf", [2049, 64], F32, kind="ExternalInput").ap()
    wout = nc.dram_tensor("wout", [129, D], F32, kind="ExternalInput").ap()
    o_q = nc.dram_tensor("o_q", [D, T], BF16, kind="ExternalOutput").ap()

    sq = Seq()
    with contextlib.ExitStack() as cx:
        E = cx.enter_context
        QA = E(nc.sbuf_tensor("QA", [128, NCH * W], BF16))
        QB = E(nc.sbuf_tensor("QB", [128, NCH * W], BF16))
        wfb = E(nc.sbuf_tensor("wfb", [128, 16 * 64], BF16))
        wfbias = E(nc.sbuf_tensor("wfbias", [1, 64], BF16))
        wob = E(nc.sbuf_tensor("wob", [128, NCH * 128], BF16))
        wobias = E(nc.sbuf_tensor("wobias", [1, D], BF16))
        onesr = E(nc.sbuf_tensor("onesr", [1, 512], BF16))
        sct = E(nc.sbuf_tensor("sct", [128, 512], BF16))
        fr = E(nc.sbuf_tensor("fr", [64, 512], F32))
        f2 = E(nc.sbuf_tensor("f2", [64, 512], F32))
        f3 = E(nc.sbuf_tensor("f3", [64, 512], F32))
        qraw = E(nc.sbuf_tensor("qraw", [128, NCH * 512], F32))
        qabs = E(nc.sbuf_tensor("qabs", [128, NCH * 512], F32))
        qm = E(nc.sbuf_tensor("qm", [128, 512], F32))
        rcol = E(nc.sbuf_tensor("rcol", [128, 8], F32))
        rrow = E(nc.sbuf_tensor("rrow", [1, 512], F32))
        idb = E(nc.sbuf_tensor("idb", [128, 128], F32))
        ident_in = nc.dram_tensor("ident", [128, 128], F32, kind="ExternalInput").ap()
        ones_col = E(nc.sbuf_tensor("ones_col", [128, 1], F32))
        ones_row = E(nc.sbuf_tensor("ones_row", [1, 128], F32))
        scr = E(nc.sbuf_tensor("scr", [128, 2048], F32))
        psF = E(nc.psum_tensor([64, 512], F32))
        psM = E(nc.psum_tensor([128, 512], F32))
        psT = E(nc.psum_tensor([128, 384], F32))
        psR = E(nc.psum_tensor([128, 512], F32))

        @sq.dma("sync")
        def _(q):
            ins = [q.dma_start(out=QA[:, k * W:(k + 1) * W], in_=qext[k * 128:(k + 1) * 128, :])
                   for k in range(NCH)]
            ins.append(q.dma_start(out=idb[:, :], in_=ident_in[:, :]))
            ins.append(q.dma_start(out=scr[:, 0:1024].rearrange("p (c n) -> p c n", c=16),
                                   in_=wf[0:2048, :].rearrange("(c p) n -> p c n", p=128)))
            ins.append(q.dma_start(out=scr[0:1, 1024:1088], in_=wf[2048:2049, :]))
            return ins

        @sq.ph("vector")
        def _(v):
            v.memset(ones_col[:, :], 1.0)
            v.memset(ones_row[:, :], 1.0)
            v.memset(onesr[:, :], 1.0)
            v.tensor_copy(wfb[:, :], scr[:, 0:1024])
            return v.tensor_copy(wfbias[:, :], scr[0:1, 1024:1088])

        @sq.dma("sync")
        def _(q):
            return [q.dma_start(out=scr[:, 0:1024], in_=wout[0:128, :]),
                    q.dma_start(out=scr[0:1, 1024:2048], in_=wout[128:129, :])]

        @sq.ph("vector")
        def _(v):
            v.tensor_copy(wob[:, :], scr[:, 0:1024])
            return v.tensor_copy(wobias[:, :], scr[0:1, 1024:2048])

        if variant == "A":
            @sq.ph("vector")
            def _(v):
                for k in range(NCH):
                    last = v.tensor_copy(QB[:, k * W:k * W + HB], QA[:, k * W:k * W + HB])
                return last

        TWO_PI = 2.0 * math.pi

        src, dst = QA, QB
        for it in range(1, 12):
            n = 1 << (it - 1)
            s_k = HB if variant == "A" else ((1 << it) - 1)
            width_total = W - s_k
            nchk = (width_total + 511) // 512
            for ci in range(nchk):
                cs = s_k + ci * 512
                wd = min(512, W - cs)

                @sq.ph("tensor")
                def _(t, src=src, cs=cs, wd=wd, n=n):
                    for k in range(NCH):
                        t.matmul(psF[:, 0:wd], wfb[:, k * 64:(k + 1) * 64],
                                 src[:, k * W + cs - n:k * W + cs - n + wd],
                                 start=(k == 0), stop=False)
                    for k in range(NCH):
                        t.matmul(psF[:, 0:wd], wfb[:, (8 + k) * 64:(9 + k) * 64],
                                 src[:, k * W + cs:k * W + cs + wd],
                                 start=False, stop=False)
                    return t.matmul(psF[:, 0:wd], wfbias[0:1, :], onesr[0:1, 0:wd],
                                    start=False, stop=True)

                @sq.ph("vector")
                def _(v, wd=wd):
                    v.tensor_scalar(fr[:, 0:wd], psF[0:64, 0:wd], 1.0 / TWO_PI, 12582912.0,
                                    ALU.mult, ALU.add)
                    v.tensor_scalar(f2[:, 0:wd], fr[:, 0:wd], 1.0, -12582912.0, ALU.mult, ALU.add)
                    v.scalar_tensor_tensor(out=f3[:, 0:wd], in0=f2[:, 0:wd], scalar=-TWO_PI,
                                           in1=psF[0:64, 0:wd], op0=ALU.mult, op1=ALU.add)
                    v.tensor_scalar(fr[:, 0:wd], psF[0:64, 0:wd], 1.0 / TWO_PI, 12582912.25,
                                    ALU.mult, ALU.add)
                    v.tensor_scalar(f2[:, 0:wd], fr[:, 0:wd], 1.0, -12582912.0, ALU.mult, ALU.add)
                    v.scalar_tensor_tensor(out=fr[:, 0:wd], in0=f2[:, 0:wd], scalar=-TWO_PI,
                                           in1=psF[0:64, 0:wd], op0=ALU.mult, op1=ALU.add)
                    return v.tensor_scalar(f2[:, 0:wd], fr[:, 0:wd], 1.0, 0.5 * math.pi,
                                           ALU.mult, ALU.add)

                @sq.ph("scalar")
                def _(s, wd=wd):
                    s.activation(out=sct[0:64, 0:wd], in_=f3[:, 0:wd], func=AF.Sin)
                    return s.activation(out=sct[64:128, 0:wd], in_=f2[:, 0:wd], func=AF.Sin)

                for o in range(NCH):
                    @sq.ph("tensor")
                    def _(t, o=o, wd=wd):
                        t.matmul(psM[:, 0:wd], wob[:, o * 128:(o + 1) * 128], sct[:, 0:wd],
                                 start=True, stop=False)
                        return t.matmul(psM[:, 0:wd], wobias[0:1, o * 128:(o + 1) * 128],
                                        onesr[0:1, 0:wd], start=False, stop=True)

                    @sq.ph("vector")
                    def _(v, o=o, wd=wd):
                        return v.tensor_copy(qraw[:, o * 512:o * 512 + wd], psM[:, 0:wd])

                @sq.ph("scalar")
                def _(s, wd=wd):
                    for o in range(NCH):
                        last = s.activation(out=qabs[:, o * 512:o * 512 + wd],
                                            in_=qraw[:, o * 512:o * 512 + wd], func=AF.Abs)
                    return last

                @sq.ph("vector")
                def _(v, wd=wd):
                    v.tensor_max(qm[:, 0:wd], qabs[:, 0:wd], qabs[:, 512:512 + wd])
                    for o in range(2, NCH):
                        last = v.tensor_max(qm[:, 0:wd], qm[:, 0:wd], qabs[:, o * 512:o * 512 + wd])
                    return last

                nsub = (wd + 127) // 128
                for j in range(nsub):
                    swd = min(128, wd - j * 128)

                    @sq.ph("tensor")
                    def _(t, j=j, swd=swd):
                        t.transpose(psT[:, 0:128][0:swd, 0:128] if False else psT[0:swd, 0:128],
                                    qm[:, j * 128:j * 128 + swd], idb[:, :])
                        return t.matmul(psT[0:1, 128:129], ones_col[0:1, 0:1], ones_col[0:1, 0:1],
                                        start=True, stop=True)

                    @sq.ph("vector")
                    def _(v, j=j, swd=swd):
                        v.tensor_copy(scr[0:swd, 0:128], psT[0:swd, 0:128])
                        v.reduce_max(rcol[0:swd, 0:1], scr[0:swd, 0:128], AX.X)
                        v.tensor_scalar_add(rcol[0:swd, 1:2], rcol[0:swd, 0:1], 1e-6)
                        return v.reciprocal(rcol[0:swd, 2:3], rcol[0:swd, 1:2])

                    @sq.ph("tensor")
                    def _(t, j=j, swd=swd):
                        t.transpose(psT[0:1, 129:129 + swd], rcol[0:swd, 2:3], idb[0:swd, 0:swd])
                        return t.matmul(psT[0:1, 128:129], ones_col[0:1, 0:1], ones_col[0:1, 0:1],
                                        start=True, stop=True)

                    @sq.ph("vector")
                    def _(v, j=j, swd=swd):
                        return v.tensor_copy(rrow[0:1, j * 128:j * 128 + swd], psT[0:1, 129:129 + swd])

                @sq.ph("tensor")
                def _(t, wd=wd):
                    return t.matmul(psR[:, 0:wd], ones_row[:, :], rrow[0:1, 0:wd],
                                    start=True, stop=True)

                @sq.ph("vector")
                def _(v, dst=dst, cs=cs, wd=wd):
                    for o in range(NCH):
                        last = v.tensor_mul(dst[:, o * W + cs:o * W + cs + wd],
                                            qraw[:, o * 512:o * 512 + wd], psR[:, 0:wd])
                    return last

            src, dst = dst, src

        @sq.dma("sync")
        def _(q, src=src):
            return [q.dma_start(out=o_q[k * 128:(k + 1) * 128, :], in_=src[:, k * W + HB:k * W + W])
                    for k in range(NCH)]

        sq.build(nc)
    return nc



def build_c34():
    T = TSH
    nc = new_nc()
    qcur = nc.dram_tensor("qcur", [D, T], BF16, kind="ExternalInput").ap()
    qprev = nc.dram_tensor("qprev", [D, T], BF16, kind="ExternalInput").ap()
    wf = nc.dram_tensor("wf", [2049, 64], F32, kind="ExternalInput").ap()
    wout = nc.dram_tensor("wout", [129, D], F32, kind="ExternalInput").ap()
    ident_in = nc.dram_tensor("ident", [128, 128], F32, kind="ExternalInput").ap()
    o_q = nc.dram_tensor("o_q", [D, T], BF16, kind="ExternalOutput").ap()

    sq = Seq()
    with contextlib.ExitStack() as cx:
        E = cx.enter_context
        QA = E(nc.sbuf_tensor("QA", [128, NCH * T], BF16))
        QP = E(nc.sbuf_tensor("QP", [128, NCH * T], BF16))
        QO = E(nc.sbuf_tensor("QO", [128, NCH * T], BF16))
        wfb = E(nc.sbuf_tensor("wfb", [128, 16 * 64], BF16))
        wfbias = E(nc.sbuf_tensor("wfbias", [1, 64], BF16))
        wob = E(nc.sbuf_tensor("wob", [128, NCH * 128], BF16))
        wobias = E(nc.sbuf_tensor("wobias", [1, D], BF16))
        onesr = E(nc.sbuf_tensor("onesr", [1, 512], BF16))
        sct = E(nc.sbuf_tensor("sct", [128, 512], BF16))
        fr = E(nc.sbuf_tensor("fr", [64, 512], F32))
        f2 = E(nc.sbuf_tensor("f2", [64, 512], F32))
        f3 = E(nc.sbuf_tensor("f3", [64, 512], F32))
        qraw = E(nc.sbuf_tensor("qraw", [128, NCH * 512], F32))
        qabs = E(nc.sbuf_tensor("qabs", [128, NCH * 512], F32))
        qm = E(nc.sbuf_tensor("qm", [128, 512], F32))
        rcol = E(nc.sbuf_tensor("rcol", [128, 8], F32))
        rrow = E(nc.sbuf_tensor("rrow", [1, 512], F32))
        idb = E(nc.sbuf_tensor("idb", [128, 128], F32))
        ones_col = E(nc.sbuf_tensor("ones_col", [128, 1], F32))
        ones_row = E(nc.sbuf_tensor("ones_row", [1, 128], F32))
        scr = E(nc.sbuf_tensor("scr", [128, 2048], F32))
        psF = E(nc.psum_tensor([64, 512], F32))
        psM = E(nc.psum_tensor([128, 512], F32))
        psT = E(nc.psum_tensor([128, 384], F32))
        psR = E(nc.psum_tensor([128, 512], F32))

        @sq.dma("sync")
        def _(q):
            ins = [q.dma_start(out=QA[:, k * T:(k + 1) * T], in_=qcur[k * 128:(k + 1) * 128, :])
                   for k in range(NCH)]
            ins += [q.dma_start(out=QP[:, k * T:(k + 1) * T], in_=qprev[k * 128:(k + 1) * 128, :])
                    for k in range(NCH)]
            ins.append(q.dma_start(out=idb[:, :], in_=ident_in[:, :]))
            ins.append(q.dma_start(out=scr[:, 0:1024].rearrange("p (c n) -> p c n", c=16),
                                   in_=wf[0:2048, :].rearrange("(c p) n -> p c n", p=128)))
            ins.append(q.dma_start(out=scr[0:1, 1024:1088], in_=wf[2048:2049, :]))
            return ins

        @sq.ph("vector")
        def _(v):
            v.memset(ones_col[:, :], 1.0)
            v.memset(ones_row[:, :], 1.0)
            v.memset(onesr[:, :], 1.0)
            v.tensor_copy(wfb[:, :], scr[:, 0:1024])
            return v.tensor_copy(wfbias[:, :], scr[0:1, 1024:1088])

        @sq.dma("sync")
        def _(q):
            return [q.dma_start(out=scr[:, 0:1024], in_=wout[0:128, :]),
                    q.dma_start(out=scr[0:1, 1024:2048], in_=wout[128:129, :])]

        @sq.ph("vector")
        def _(v):
            v.tensor_copy(wob[:, :], scr[:, 0:1024])
            return v.tensor_copy(wobias[:, :], scr[0:1, 1024:2048])

        TWO_PI = 2.0 * math.pi
        for ci in range(T // 512):
            cs = ci * 512
            wd = 512

            @sq.ph("tensor")
            def _(t, cs=cs, wd=wd):
                for k in range(NCH):
                    t.matmul(psF[:, 0:wd], wfb[:, k * 64:(k + 1) * 64],
                             QP[:, k * T + cs:k * T + cs + wd], start=(k == 0), stop=False)
                for k in range(NCH):
                    t.matmul(psF[:, 0:wd], wfb[:, (8 + k) * 64:(9 + k) * 64],
                             QA[:, k * T + cs:k * T + cs + wd], start=False, stop=False)
                return t.matmul(psF[:, 0:wd], wfbias[0:1, :], onesr[0:1, 0:wd],
                                start=False, stop=True)

            @sq.ph("vector")
            def _(v, wd=wd):
                v.tensor_scalar(fr[:, 0:wd], psF[0:64, 0:wd], 1.0 / TWO_PI, 12582912.0,
                                ALU.mult, ALU.add)
                v.tensor_scalar(f2[:, 0:wd], fr[:, 0:wd], 1.0, -12582912.0, ALU.mult, ALU.add)
                v.scalar_tensor_tensor(out=f3[:, 0:wd], in0=f2[:, 0:wd], scalar=-TWO_PI,
                                       in1=psF[0:64, 0:wd], op0=ALU.mult, op1=ALU.add)
                v.tensor_scalar(fr[:, 0:wd], psF[0:64, 0:wd], 1.0 / TWO_PI, 12582912.25,
                                ALU.mult, ALU.add)
                v.tensor_scalar(f2[:, 0:wd], fr[:, 0:wd], 1.0, -12582912.0, ALU.mult, ALU.add)
                v.scalar_tensor_tensor(out=fr[:, 0:wd], in0=f2[:, 0:wd], scalar=-TWO_PI,
                                       in1=psF[0:64, 0:wd], op0=ALU.mult, op1=ALU.add)
                return v.tensor_scalar(f2[:, 0:wd], fr[:, 0:wd], 1.0, 0.5 * math.pi,
                                       ALU.mult, ALU.add)

            @sq.ph("scalar")
            def _(s, wd=wd):
                s.activation(out=sct[0:64, 0:wd], in_=f3[:, 0:wd], func=AF.Sin)
                return s.activation(out=sct[64:128, 0:wd], in_=f2[:, 0:wd], func=AF.Sin)

            for o in range(NCH):
                @sq.ph("tensor")
                def _(t, o=o, wd=wd):
                    t.matmul(psM[:, 0:wd], wob[:, o * 128:(o + 1) * 128], sct[:, 0:wd],
                             start=True, stop=False)
                    return t.matmul(psM[:, 0:wd], wobias[0:1, o * 128:(o + 1) * 128],
                                    onesr[0:1, 0:wd], start=False, stop=True)

                @sq.ph("vector")
                def _(v, o=o, wd=wd):
                    return v.tensor_copy(qraw[:, o * 512:o * 512 + wd], psM[:, 0:wd])

            @sq.ph("scalar")
            def _(s, wd=wd):
                for o in range(NCH):
                    last = s.activation(out=qabs[:, o * 512:o * 512 + wd],
                                        in_=qraw[:, o * 512:o * 512 + wd], func=AF.Abs)
                return last

            @sq.ph("vector")
            def _(v, wd=wd):
                v.tensor_max(qm[:, 0:wd], qabs[:, 0:wd], qabs[:, 512:512 + wd])
                for o in range(2, NCH):
                    last = v.tensor_max(qm[:, 0:wd], qm[:, 0:wd], qabs[:, o * 512:o * 512 + wd])
                return last

            for j in range(4):
                @sq.ph("tensor")
                def _(t, j=j):
                    t.transpose(psT[0:128, 0:128], qm[:, j * 128:(j + 1) * 128], idb[:, :])
                    return t.matmul(psT[0:1, 128:129], ones_col[0:1, 0:1], ones_col[0:1, 0:1],
                                    start=True, stop=True)

                @sq.ph("vector")
                def _(v, j=j):
                    v.tensor_copy(scr[:, 0:128], psT[:, 0:128])
                    v.reduce_max(rcol[:, 0:1], scr[:, 0:128], AX.X)
                    v.tensor_scalar_add(rcol[:, 1:2], rcol[:, 0:1], 1e-6)
                    return v.reciprocal(rcol[:, 2:3], rcol[:, 1:2])

                @sq.ph("tensor")
                def _(t, j=j):
                    t.transpose(psT[0:1, 129:257], rcol[:, 2:3], idb[:, :])
                    return t.matmul(psT[0:1, 128:129], ones_col[0:1, 0:1], ones_col[0:1, 0:1],
                                    start=True, stop=True)

                @sq.ph("vector")
                def _(v, j=j):
                    return v.tensor_copy(rrow[0:1, j * 128:(j + 1) * 128], psT[0:1, 129:257])

            @sq.ph("tensor")
            def _(t, wd=wd):
                return t.matmul(psR[:, 0:wd], ones_row[:, :], rrow[0:1, 0:wd],
                                start=True, stop=True)

            @sq.ph("vector")
            def _(v, cs=cs, wd=wd):
                for o in range(NCH):
                    last = v.tensor_mul(QO[:, o * T + cs:o * T + cs + wd],
                                        qraw[:, o * 512:o * 512 + wd], psR[:, 0:wd])
                return last

        @sq.dma("sync")
        def _(q):
            return [q.dma_start(out=o_q[k * 128:(k + 1) * 128, :], in_=QO[:, k * T:(k + 1) * T])
                    for k in range(NCH)]

        sq.build(nc)
    return nc



def build_c5():
    T = TSH
    nc = new_nc()
    x_fm = nc.dram_tensor("x_fm", [D, T], F32, kind="ExternalInput").ap()
    q13 = nc.dram_tensor("q13", [D, T], BF16, kind="ExternalInput").ap()
    vv = nc.dram_tensor("vv", [D, T], F32, kind="ExternalInput").ap()
    proj = nc.dram_tensor("proj", [1025, D], F32, kind="ExternalInput").ap()
    o_x = nc.dram_tensor("o_x", [D, T], F32, kind="ExternalOutput").ap()

    sq = Seq()
    with contextlib.ExitStack() as cx:
        E = cx.enter_context
        A = E(nc.sbuf_tensor("A", [128, NCH * T], F32))
        Yb = E(nc.sbuf_tensor("Yb", [128, NCH * T], BF16))
        wob = E(nc.sbuf_tensor("wob", [128, NCH * D], BF16))
        wbias = E(nc.sbuf_tensor("wbias", [1, D], BF16))
        onesr = E(nc.sbuf_tensor("onesr", [1, 512], BF16))
        scr = E(nc.sbuf_tensor("scr", [128, 2048], F32))
        sc2 = E(nc.sbuf_tensor("sc2", [128, 2048], F32))
        psM = E(nc.psum_tensor([128, 2048], F32))

        @sq.dma("sync")
        def _(q):
            return [q.dma_start(out=A[:, k * T:(k + 1) * T], in_=x_fm[k * 128:(k + 1) * 128, :])
                    for k in range(NCH)]

        @sq.ph("vector")
        def _(v):
            return v.memset(onesr[:, :], 1.0)

        for k in range(NCH):
            @sq.dma("sync")
            def _(q, k=k):
                return [q.dma_start(out=scr[:, 0:T].rearrange("p t -> p t") if False else scr[:, 0:T], in_=vv[k * 128:(k + 1) * 128, :]),
                        q.dma_start(out=Yb[:, k * T:(k + 1) * T], in_=q13[k * 128:(k + 1) * 128, :])]

            @sq.ph("vector")
            def _(v, k=k):
                v.tensor_copy(sc2[:, 0:T], Yb[:, k * T:(k + 1) * T])
                v.tensor_mul(sc2[:, 0:T], sc2[:, 0:T], scr[:, 0:T])
                return v.tensor_copy(Yb[:, k * T:(k + 1) * T], sc2[:, 0:T])

        for k in range(NCH):
            @sq.dma("sync")
            def _(q, k=k):
                return [q.dma_start(out=scr[:, 0:1024], in_=proj[k * 128:(k + 1) * 128, :])]

            @sq.ph("vector")
            def _(v, k=k):
                return v.tensor_copy(wob[:, k * D:(k + 1) * D], scr[:, 0:1024])

        @sq.dma("sync")
        def _(q):
            return [q.dma_start(out=scr[0:1, 0:1024], in_=proj[1024:1025, :])]

        @sq.ph("vector")
        def _(v):
            return v.tensor_copy(wbias[:, :], scr[0:1, 0:1024])

        for o in range(NCH):
            @sq.ph("tensor")
            def _(t, o=o):
                for tc in range(4):
                    for k in range(NCH):
                        t.matmul(psM[:, tc * 512:tc * 512 + 512],
                                 wob[:, k * D + o * 128:k * D + o * 128 + 128],
                                 Yb[:, k * T + tc * 512:k * T + tc * 512 + 512],
                                 start=(k == 0), stop=False)
                    last = t.matmul(psM[:, tc * 512:tc * 512 + 512],
                                    wbias[0:1, o * 128:(o + 1) * 128],
                                    onesr[0:1, :], start=False, stop=True)
                return last

            @sq.ph("vector")
            def _(v, o=o):
                return v.tensor_add(A[:, o * T:(o + 1) * T], A[:, o * T:(o + 1) * T], psM[:, 0:T])

        @sq.dma("sync")
        def _(q):
            return [q.dma_start(out=o_x[k * 128:(k + 1) * 128, :], in_=A[:, k * T:(k + 1) * T])
                    for k in range(NCH)]

        sq.build(nc)
    return nc



def build_bmm_sigstat(ns):
    nc = new_nc()
    stat = nc.dram_tensor("stat", [ns, 64, 128], BF16, kind="ExternalInput").ap()
    rhs = nc.dram_tensor("rhs", [64, 256], BF16, kind="ExternalInput").ap()
    out = nc.dram_tensor("out", [ns, 128, 256], BF16, kind="ExternalOutput").ap()
    sq = Seq()
    with contextlib.ExitStack() as cx:
        E = cx.enter_context
        GRP = 16
        stb = E(nc.sbuf_tensor("stb", [64, GRP * 128], BF16))
        rb = E(nc.sbuf_tensor("rb", [64, 256], BF16))
        ob = E(nc.sbuf_tensor("ob", [128, GRP * 256], BF16))
        ps = E(nc.psum_tensor([128, 512], F32))

        @sq.dma("sync")
        def _(q):
            return [q.dma_start(out=rb[:, :], in_=rhs[:, :])]

        for g in range(ns // GRP):
            @sq.dma("sync")
            def _(q, g=g):
                return [q.dma_start(out=stb[:, :].rearrange("p (i n) -> p i n", i=GRP),
                                    in_=stat[g * GRP:(g + 1) * GRP, :, :].rearrange("i p n -> p i n"))]

            for i in range(GRP):
                @sq.ph("tensor")
                def _(t, i=i):
                    return t.matmul(ps[:, 0:256], stb[:, i * 128:(i + 1) * 128], rb[:, :],
                                    start=True, stop=True)

                @sq.ph("scalar")
                def _(s, i=i):
                    return s.copy(out=ob[:, i * 256:(i + 1) * 256], in_=ps[:, 0:256])

            @sq.dma("sync")
            def _(q, g=g):
                return [q.dma_start(out=out[g * GRP:(g + 1) * GRP, :, :].rearrange("i p n -> p i n"),
                                    in_=ob[:, :].rearrange("p (i n) -> p i n", i=GRP))]

        sq.build(nc)
    return nc


def build_bmm_perk(nk, p, m, n):
    nc = new_nc()
    A_ = nc.dram_tensor("A", [nk, p, m], BF16, kind="ExternalInput").ap()
    B_ = nc.dram_tensor("B", [nk, p, m], BF16, kind="ExternalInput").ap()
    C_ = nc.dram_tensor("C", [nk, p, m], BF16, kind="ExternalInput").ap()
    D_ = nc.dram_tensor("Dm", [nk, p, m], BF16, kind="ExternalInput").ap()
    U_ = nc.dram_tensor("U", [nk, p, n], BF16, kind="ExternalInput").ap()
    V_ = nc.dram_tensor("V", [nk, p, n], BF16, kind="ExternalInput").ap()
    ore = nc.dram_tensor("ore", [nk, m, n], BF16, kind="ExternalOutput").ap()
    oim = nc.dram_tensor("oim", [nk, m, n], BF16, kind="ExternalOutput").ap()
    sq = Seq()
    nsl = (n + 511) // 512
    with contextlib.ExitStack() as cx:
        E = cx.enter_context
        ab = E(nc.sbuf_tensor("ab", [p, 4 * m], BF16))
        ub = E(nc.sbuf_tensor("ub", [p, 2 * n], BF16))
        ob = E(nc.sbuf_tensor("ob", [m, 2 * n], BF16))
        ps = E(nc.psum_tensor([m, 1024], F32))

        for k in range(nk):
            @sq.dma("sync")
            def _(q, k=k):
                return [q.dma_start(out=ab[:, 0:m], in_=A_[k, :, :]),
                        q.dma_start(out=ab[:, m:2 * m], in_=B_[k, :, :]),
                        q.dma_start(out=ab[:, 2 * m:3 * m], in_=C_[k, :, :]),
                        q.dma_start(out=ab[:, 3 * m:4 * m], in_=D_[k, :, :]),
                        q.dma_start(out=ub[:, 0:n], in_=U_[k, :, :]),
                        q.dma_start(out=ub[:, n:2 * n], in_=V_[k, :, :])]

            for sl in range(nsl):
                ss = sl * 512
                ww = min(512, n - ss)

                @sq.ph("tensor")
                def _(t, k=k, ss=ss, ww=ww):
                    t.matmul(ps[:, 0:ww], ab[:, 0:m], ub[:, ss:ss + ww], start=True, stop=False)
                    t.matmul(ps[:, 0:ww], ab[:, m:2 * m], ub[:, n + ss:n + ss + ww],
                             start=False, stop=True)
                    t.matmul(ps[:, 512:512 + ww], ab[:, 2 * m:3 * m], ub[:, ss:ss + ww],
                             start=True, stop=False)
                    return t.matmul(ps[:, 512:512 + ww], ab[:, 3 * m:4 * m],
                                    ub[:, n + ss:n + ss + ww], start=False, stop=True)

                @sq.ph("scalar")
                def _(s, k=k, ss=ss, ww=ww):
                    s.copy(out=ob[:, ss:ss + ww], in_=ps[:, 0:ww])
                    return s.copy(out=ob[:, n + ss:n + ss + ww], in_=ps[:, 512:512 + ww])

            @sq.dma("sync")
            def _(q, k=k):
                return [q.dma_start(out=ore[k, :, :], in_=ob[:, 0:n]),
                        q.dma_start(out=oim[k, :, :], in_=ob[:, n:2 * n])]

        sq.build(nc)
    return nc


def build_cmul(nk, nch, nb):
    n = nb * nch
    nc = new_nc()
    xre = nc.dram_tensor("xre", [nk, 128, n], BF16, kind="ExternalInput").ap()
    xim = nc.dram_tensor("xim", [nk, 128, n], BF16, kind="ExternalInput").ap()
    kre = nc.dram_tensor("kre", [nk, 128, nch], BF16, kind="ExternalInput").ap()
    kim = nc.dram_tensor("kim", [nk, 128, nch], BF16, kind="ExternalInput").ap()
    yre = nc.dram_tensor("yre", [nk, 128, n], BF16, kind="ExternalOutput").ap()
    yim = nc.dram_tensor("yim", [nk, 128, n], BF16, kind="ExternalOutput").ap()
    sq = Seq()
    with contextlib.ExitStack() as cx:
        E = cx.enter_context
        xb = E(nc.sbuf_tensor("xb", [128, 2 * n], F32))
        kb = E(nc.sbuf_tensor("kb", [128, 2 * nch], F32))
        xbb = E(nc.sbuf_tensor("xbb", [128, 2 * n], BF16))
        kbb = E(nc.sbuf_tensor("kbb", [128, 2 * nch], BF16))
        t1 = E(nc.sbuf_tensor("t1", [128, n], F32))
        t2 = E(nc.sbuf_tensor("t2", [128, n], F32))
        ob = E(nc.sbuf_tensor("ob", [128, 2 * n], BF16))

        for k in range(nk):
            @sq.dma("sync")
            def _(q, k=k):
                return [q.dma_start(out=xbb[:, 0:n], in_=xre[k, :, :]),
                        q.dma_start(out=xbb[:, n:2 * n], in_=xim[k, :, :]),
                        q.dma_start(out=kbb[:, 0:nch], in_=kre[k, :, :]),
                        q.dma_start(out=kbb[:, nch:2 * nch], in_=kim[k, :, :])]

            @sq.ph("vector")
            def _(v, k=k):
                v.tensor_copy(xb[:, :], xbb[:, :])
                return v.tensor_copy(kb[:, :], kbb[:, :])

            @sq.ph("vector")
            def _(v, k=k):
                krev = kb[:, 0:nch].rearrange("p (u c) -> p u c", u=1).broadcast_to([128, nb, nch])
                kimv = kb[:, nch:2 * nch].rearrange("p (u c) -> p u c", u=1).broadcast_to([128, nb, nch])
                xrv = xb[:, 0:n].rearrange("p (b c) -> p b c", b=nb)
                xiv = xb[:, n:2 * n].rearrange("p (b c) -> p b c", b=nb)
                v.tensor_tensor(out=t1[:, :].rearrange("p (b c) -> p b c", b=nb), in0=xrv, in1=krev, op=ALU.mult)
                v.tensor_tensor(out=t2[:, :].rearrange("p (b c) -> p b c", b=nb), in0=xiv, in1=kimv, op=ALU.mult)
                v.tensor_sub(ob[:, 0:n], t1[:, :], t2[:, :])
                v.tensor_tensor(out=t1[:, :].rearrange("p (b c) -> p b c", b=nb), in0=xrv, in1=kimv, op=ALU.mult)
                v.tensor_tensor(out=t2[:, :].rearrange("p (b c) -> p b c", b=nb), in0=xiv, in1=krev, op=ALU.mult)
                return v.tensor_add(ob[:, n:2 * n], t1[:, :], t2[:, :])

            @sq.dma("sync")
            def _(q, k=k):
                return [q.dma_start(out=yre[k, :, :], in_=ob[:, 0:n]),
                        q.dma_start(out=yim[k, :, :], in_=ob[:, n:2 * n])]

        sq.build(nc)
    return nc


def build_bmm_acc2(p, m, nwide):
    nc = new_nc()
    L1 = nc.dram_tensor("L1", [p, m], BF16, kind="ExternalInput").ap()
    L2 = nc.dram_tensor("L2", [p, m], BF16, kind="ExternalInput").ap()
    R1 = nc.dram_tensor("R1", [p, nwide], BF16, kind="ExternalInput").ap()
    R2 = nc.dram_tensor("R2", [p, nwide], BF16, kind="ExternalInput").ap()
    out = nc.dram_tensor("out", [m, nwide], BF16, kind="ExternalOutput").ap()
    sq = Seq()
    with contextlib.ExitStack() as cx:
        E = cx.enter_context
        lb = E(nc.sbuf_tensor("lb", [p, 2 * m], BF16))
        rb = E(nc.sbuf_tensor("rb", [p, 1024], BF16))
        ob = E(nc.sbuf_tensor("ob", [m, 512], BF16))
        ps = E(nc.psum_tensor([m, 512], F32))

        @sq.dma("sync")
        def _(q):
            return [q.dma_start(out=lb[:, 0:m], in_=L1[:, :]),
                    q.dma_start(out=lb[:, m:2 * m], in_=L2[:, :])]

        for sl in range((nwide + 511) // 512):
            ss = sl * 512
            ww = min(512, nwide - ss)

            @sq.dma("sync")
            def _(q, ss=ss, ww=ww):
                return [q.dma_start(out=rb[:, 0:ww], in_=R1[:, ss:ss + ww]),
                        q.dma_start(out=rb[:, 512:512 + ww], in_=R2[:, ss:ss + ww])]

            @sq.ph("tensor")
            def _(t, ss=ss, ww=ww):
                t.matmul(ps[:, 0:ww], lb[:, 0:m], rb[:, 0:ww], start=True, stop=False)
                return t.matmul(ps[:, 0:ww], lb[:, m:2 * m], rb[:, 512:512 + ww],
                                start=False, stop=True)

            @sq.ph("scalar")
            def _(s, ss=ss, ww=ww):
                return s.copy(out=ob[:, 0:ww], in_=ps[:, 0:ww])

            @sq.dma("sync")
            def _(q, ss=ss, ww=ww):
                return [q.dma_start(out=out[:, ss:ss + ww], in_=ob[:, 0:ww])]

        sq.build(nc)
    return nc



def build_l1a_xz():
    TT_ = 2 * S
    nc = new_nc()
    xln = nc.dram_tensor("xln", [D, TT_], BF16, kind="ExternalInput").ap()
    wsl = nc.dram_tensor("wsl", [D, 512], F32, kind="ExternalInput").ap()
    o_xz = nc.dram_tensor("o_xz", [512, TT_], F32, kind="ExternalOutput").ap()
    sq = Seq()
    with contextlib.ExitStack() as cx:
        E = cx.enter_context
        wb = E(nc.sbuf_tensor("wb", [128, NCH * 512], BF16))
        xb = E(nc.sbuf_tensor("xb", [128, NCH * 512], BF16))
        ob = E(nc.sbuf_tensor("ob", [128, 4 * 512], F32))
        scr = E(nc.sbuf_tensor("scr", [128, 512], F32))
        ps = E(nc.psum_tensor([128, 2048], F32))

        for k in range(NCH):
            @sq.dma("sync")
            def _(q, k=k):
                return [q.dma_start(out=scr[:, :], in_=wsl[k * 128:(k + 1) * 128, :])]

            @sq.ph("vector")
            def _(v, k=k):
                return v.tensor_copy(wb[:, k * 512:(k + 1) * 512], scr[:, :])

        for tc in range(TT_ // 512):
            @sq.dma("sync")
            def _(q, tc=tc):
                return [q.dma_start(out=xb[:, k * 512:(k + 1) * 512],
                                    in_=xln[k * 128:(k + 1) * 128, tc * 512:(tc + 1) * 512])
                        for k in range(NCH)]

            @sq.ph("tensor")
            def _(t, tc=tc):
                for o in range(4):
                    for k in range(NCH):
                        last = t.matmul(ps[:, o * 512:(o + 1) * 512],
                                        wb[:, k * 512 + o * 128:k * 512 + o * 128 + 128],
                                        xb[:, k * 512:(k + 1) * 512],
                                        start=(k == 0), stop=(k == NCH - 1))
                return last

            @sq.ph("vector")
            def _(v, tc=tc):
                for o in range(4):
                    last = v.tensor_copy(ob[:, o * 512:(o + 1) * 512], ps[:, o * 512:(o + 1) * 512])
                return last

            @sq.dma("sync")
            def _(q, tc=tc):
                return [q.dma_start(out=o_xz[o * 128:(o + 1) * 128, tc * 512:(tc + 1) * 512],
                                    in_=ob[:, o * 512:(o + 1) * 512]) for o in range(4)]

        sq.build(nc)
    return nc



def build_l1a_conv():
    TT_ = 2 * S
    nc = new_nc()
    xz = nc.dram_tensor("xz", [512, TT_], F32, kind="ExternalInput").ap()
    convw = nc.dram_tensor("convw", [256, 4], F32, kind="ExternalInput").ap()
    convb = nc.dram_tensor("convb", [256, 1], F32, kind="ExternalInput").ap()
    o_xc = nc.dram_tensor("o_xc", [256, TT_], F32, kind="ExternalOutput").ap()
    o_xcb = nc.dram_tensor("o_xcb", [256, TT_], BF16, kind="ExternalOutput").ap()
    o_sz = nc.dram_tensor("o_sz", [256, TT_], BF16, kind="ExternalOutput").ap()
    kssm = nc.dram_tensor("kssm", [256, 4096], F32, kind="ExternalInput").ap()
    o_kb = nc.dram_tensor("o_kb", [256, 4096], BF16, kind="ExternalOutput").ap()
    sq = Seq()
    with contextlib.ExitStack() as cx:
        E = cx.enter_context
        xt = E(nc.sbuf_tensor("xt", [128, S + 3], F32))
        acc = E(nc.sbuf_tensor("acc", [128, S], F32))
        acc2 = E(nc.sbuf_tensor("acc2", [128, S], F32))
        res = E(nc.sbuf_tensor("res", [128, S], F32))
        resb = E(nc.sbuf_tensor("resb", [128, S], BF16))
        cw = E(nc.sbuf_tensor("cw", [128, 2 * 4], F32))
        cb = E(nc.sbuf_tensor("cb", [128, 2], F32))

        @sq.dma("sync")
        def _(q):
            ds = [q.dma_start(out=cw[:, 0:8].rearrange("p (c n) -> p c n", c=2),
                              in_=convw.rearrange("(c p) n -> p c n", p=128))]
            for c in range(2):
                ds.append(q.dma_start(out=cb[:, c:c + 1], in_=convb[c * 128:(c + 1) * 128, :]))
            return ds

        for cc in range(2):
            @sq.dma("sync")
            def _(q, cc=cc):
                return [q.dma_start(out=xt[:, 0:4096], in_=kssm[cc * 128:(cc + 1) * 128, :])]

            @sq.ph("vector")
            def _(v, cc=cc):
                return v.tensor_copy(resb[:, 0:4096], xt[:, 0:4096])

            @sq.dma("sync")
            def _(q, cc=cc):
                return [q.dma_start(out=o_kb[cc * 128:(cc + 1) * 128, :], in_=resb[:, 0:4096])]

        for cc in range(2):
            for bb in range(2):
                @sq.ph("vector")
                def _(v, cc=cc, bb=bb):
                    return v.memset(xt[:, 0:3], 0.0)

                @sq.dma("sync")
                def _(q, cc=cc, bb=bb):
                    return [q.dma_start(out=xt[:, 3:3 + S],
                                        in_=xz[cc * 128:(cc + 1) * 128, bb * S:(bb + 1) * S])]

                @sq.ph("vector")
                def _(v, cc=cc, bb=bb):
                    v.tensor_scalar(acc[:, :], xt[:, 0:S], cw[:, cc * 4 + 0:cc * 4 + 1], None, ALU.mult)
                    v.scalar_tensor_tensor(out=acc2[:, :], in0=xt[:, 1:1 + S],
                                           scalar=cw[:, cc * 4 + 1:cc * 4 + 2], in1=acc[:, :],
                                           op0=ALU.mult, op1=ALU.add)
                    v.scalar_tensor_tensor(out=acc[:, :], in0=xt[:, 2:2 + S],
                                           scalar=cw[:, cc * 4 + 2:cc * 4 + 3], in1=acc2[:, :],
                                           op0=ALU.mult, op1=ALU.add)
                    v.scalar_tensor_tensor(out=acc2[:, :], in0=xt[:, 3:3 + S],
                                           scalar=cw[:, cc * 4 + 3:cc * 4 + 4], in1=acc[:, :],
                                           op0=ALU.mult, op1=ALU.add)
                    return v.tensor_scalar(acc[:, :], acc2[:, :], cb[:, cc:cc + 1], None, ALU.add)

                @sq.ph("scalar")
                def _(s, cc=cc, bb=bb):
                    s.activation(out=res[:, :], in_=acc[:, :], func=AF.Silu)
                    return s.activation(out=resb[:, :], in_=acc[:, :], func=AF.Silu)

                @sq.dma("sync")
                def _(q, cc=cc, bb=bb):
                    return [q.dma_start(out=o_xc[cc * 128:(cc + 1) * 128, bb * S:(bb + 1) * S],
                                        in_=res[:, :]),
                            q.dma_start(out=o_xcb[cc * 128:(cc + 1) * 128, bb * S:(bb + 1) * S],
                                        in_=resb[:, :])]

                @sq.dma("sync")
                def _(q, cc=cc, bb=bb):
                    return [q.dma_start(out=xt[:, 3:3 + S],
                                        in_=xz[256 + cc * 128:256 + (cc + 1) * 128, bb * S:(bb + 1) * S])]

                @sq.ph("scalar")
                def _(s, cc=cc, bb=bb):
                    return s.activation(out=resb[:, :], in_=xt[:, 3:3 + S], func=AF.Silu)

                @sq.dma("sync")
                def _(q, cc=cc, bb=bb):
                    return [q.dma_start(out=o_sz[cc * 128:(cc + 1) * 128, bb * S:(bb + 1) * S],
                                        in_=resb[:, :])]

        sq.build(nc)
    return nc



def build_l1a_gate():
    TT_ = 2 * S
    nc = new_nc()
    yy = nc.dram_tensor("yy", [256, TT_], BF16, kind="ExternalInput").ap()
    xc = nc.dram_tensor("xc", [256, TT_], F32, kind="ExternalInput").ap()
    sz = nc.dram_tensor("sz", [256, TT_], BF16, kind="ExternalInput").ap()
    dsk = nc.dram_tensor("dsk", [256, 1], F32, kind="ExternalInput").ap()
    o_yg = nc.dram_tensor("o_yg", [256, TT_], BF16, kind="ExternalOutput").ap()
    sq = Seq()
    with contextlib.ExitStack() as cx:
        E = cx.enter_context
        ybt = E(nc.sbuf_tensor("ybt", [128, 4096], BF16))
        yft = E(nc.sbuf_tensor("yft", [128, 4096], F32))
        xct = E(nc.sbuf_tensor("xct", [128, 4096], F32))
        szt = E(nc.sbuf_tensor("szt", [128, 4096], BF16))
        szf = E(nc.sbuf_tensor("szf", [128, 4096], F32))
        ot = E(nc.sbuf_tensor("ot", [128, 4096], BF16))
        dct = E(nc.sbuf_tensor("dct", [128, 2], F32))

        @sq.dma("sync")
        def _(q):
            return [q.dma_start(out=dct[:, c:c + 1], in_=dsk[c * 128:(c + 1) * 128, :])
                    for c in range(2)]

        for cc in range(2):
            for tb in range(TT_ // 4096):
                @sq.dma("sync")
                def _(q, cc=cc, tb=tb):
                    sl = slice(tb * 4096, (tb + 1) * 4096)
                    return [q.dma_start(out=ybt[:, :], in_=yy[cc * 128:(cc + 1) * 128, sl]),
                            q.dma_start(out=xct[:, :], in_=xc[cc * 128:(cc + 1) * 128, sl]),
                            q.dma_start(out=szt[:, :], in_=sz[cc * 128:(cc + 1) * 128, sl])]

                @sq.ph("vector")
                def _(v, cc=cc, tb=tb):
                    v.tensor_copy(yft[:, :], ybt[:, :])
                    v.tensor_copy(szf[:, :], szt[:, :])
                    v.scalar_tensor_tensor(out=xct[:, :], in0=xct[:, :], scalar=dct[:, cc:cc + 1],
                                           in1=yft[:, :], op0=ALU.mult, op1=ALU.add)
                    return v.tensor_mul(ot[:, :], xct[:, :], szf[:, :])

                @sq.dma("sync")
                def _(q, cc=cc, tb=tb):
                    sl = slice(tb * 4096, (tb + 1) * 4096)
                    return [q.dma_start(out=o_yg[cc * 128:(cc + 1) * 128, sl], in_=ot[:, :])]

        sq.build(nc)
    return nc



def build_matres(KD):
    T = TSH
    nkc = KD // 128
    nc = new_nc()
    x_fm = nc.dram_tensor("x_fm", [D, T], F32, kind="ExternalInput").ap()
    act = nc.dram_tensor("act", [KD, T], BF16, kind="ExternalInput").ap()
    wT = nc.dram_tensor("wT", [KD, D], F32, kind="ExternalInput").ap()
    o_x = nc.dram_tensor("o_x", [D, T], F32, kind="ExternalOutput").ap()
    sq = Seq()
    with contextlib.ExitStack() as cx:
        E = cx.enter_context
        A = E(nc.sbuf_tensor("A", [128, NCH * T], F32))
        U = E(nc.sbuf_tensor("U", [128, nkc * T], BF16))
        wb = E(nc.sbuf_tensor("wb", [128, nkc * D], BF16))
        scr = E(nc.sbuf_tensor("scr", [128, 2048], F32))
        psM = E(nc.psum_tensor([128, 2048], F32))

        @sq.dma("sync")
        def _(q):
            ins = [q.dma_start(out=A[:, k * T:(k + 1) * T], in_=x_fm[k * 128:(k + 1) * 128, :])
                   for k in range(NCH)]
            ins += [q.dma_start(out=U[:, k * T:(k + 1) * T], in_=act[k * 128:(k + 1) * 128, :])
                    for k in range(nkc)]
            return ins

        for k in range(nkc):
            @sq.dma("sync")
            def _(q, k=k):
                return [q.dma_start(out=scr[:, 0:1024], in_=wT[k * 128:(k + 1) * 128, :])]

            @sq.ph("vector")
            def _(v, k=k):
                return v.tensor_copy(wb[:, k * D:(k + 1) * D], scr[:, 0:1024])

        for o in range(NCH):
            @sq.ph("tensor")
            def _(t, o=o):
                for tc in range(4):
                    for k in range(nkc):
                        last = t.matmul(psM[:, tc * 512:tc * 512 + 512],
                                        wb[:, k * D + o * 128:k * D + o * 128 + 128],
                                        U[:, k * T + tc * 512:k * T + tc * 512 + 512],
                                        start=(k == 0), stop=(k == nkc - 1))
                return last

            @sq.ph("vector")
            def _(v, o=o):
                return v.tensor_add(A[:, o * T:(o + 1) * T], A[:, o * T:(o + 1) * T], psM[:, 0:T])

        @sq.dma("sync")
        def _(q):
            return [q.dma_start(out=o_x[k * 128:(k + 1) * 128, :], in_=A[:, k * T:(k + 1) * T])
                    for k in range(NCH)]

        sq.build(nc)
    return nc



def build_l1ffn():
    T = TSH
    nc = new_nc()
    x_fm = nc.dram_tensor("x_fm", [D, T], F32, kind="ExternalInput").ap()
    w1T = nc.dram_tensor("w1T", [D, 4096], F32, kind="ExternalInput").ap()
    w2T = nc.dram_tensor("w2T", [4096, D], F32, kind="ExternalInput").ap()
    b1c = nc.dram_tensor("b1c", [4096, 1], F32, kind="ExternalInput").ap()
    b2c = nc.dram_tensor("b2c", [D, 1], F32, kind="ExternalInput").ap()
    hbuf = nc.dram_tensor("hbuf", [4096, T], BF16, kind="Internal").ap()
    o_x = nc.dram_tensor("o_x", [D, T], F32, kind="ExternalOutput").ap()
    sq = Seq()
    with contextlib.ExitStack() as cx:
        E = cx.enter_context
        A = E(nc.sbuf_tensor("A", [128, NCH * T], F32))
        Bt = E(nc.sbuf_tensor("Bt", [128, NCH * T], BF16))
        wqb = E(nc.sbuf_tensor("wqb", [128, 32768], BF16))
        scr = E(nc.sbuf_tensor("scr", [128, 2048], F32))
        sc2 = E(nc.sbuf_tensor("sc2", [128, 4096], F32))
        hstg = E(nc.sbuf_tensor("hstg", [128, T], BF16))
        hh = Bt
        bcol = E(nc.sbuf_tensor("bcol", [128, 48], F32))
        gcol = E(nc.sbuf_tensor("gcol", [128, 8], F32))
        rowsu = E(nc.sbuf_tensor("rowsu", [1, T], F32))
        rowva = E(nc.sbuf_tensor("rowva", [1, T], F32))
        ones_col = E(nc.sbuf_tensor("ones_col", [128, 1], F32))
        ones_row = E(nc.sbuf_tensor("ones_row", [1, 128], F32))
        psA = E(nc.psum_tensor([1, 512], F32))
        psR = E(nc.psum_tensor([128, 1024], F32))
        psM = E(nc.psum_tensor([128, 2048], F32))

        @sq.dma("sync")
        def _(q):
            ins = [q.dma_start(out=A[:, k * T:(k + 1) * T], in_=x_fm[k * 128:(k + 1) * 128, :])
                   for k in range(NCH)]
            for c in range(32):
                ins.append(q.dma_start(out=bcol[:, c:c + 1], in_=b1c[c * 128:(c + 1) * 128, :]))
            for c in range(8):
                ins.append(q.dma_start(out=bcol[:, 32 + c:33 + c], in_=b2c[c * 128:(c + 1) * 128, :]))
            return ins

        @sq.ph("vector")
        def _(v):
            v.memset(ones_col[:, :], 1.0)
            v.memset(ones_row[:, :], 1.0)
            return v.memset(gcol[:, :], 0.0)

        h = dict(xs=A, xln=Bt, scr=scr, row_su=rowsu, row_va=rowva,
                 psA=psA, psR=psR, ones_col=ones_col, ones_row=ones_row)
        add_ln_phases(sq, h, T)

        def quant_weight(src, nchunks, ncols, per_elem):
            @sq.ph("vector")
            def _(v):
                return v.memset(gcol[:, 0:1], 0.0)

            for k in range(nchunks):
                for cs in range(0, ncols, 2048):
                    ce = min(ncols, cs + 2048)

                    @sq.dma("sync")
                    def _(q, k=k, cs=cs, ce=ce):
                        return [q.dma_start(out=scr[:, 0:ce - cs],
                                            in_=src[k * 128:(k + 1) * 128, cs:ce])]

                    @sq.ph("scalar")
                    def _(s, k=k, cs=cs, ce=ce):
                        return s.activation(out=sc2[:, 0:ce - cs], in_=scr[:, 0:ce - cs], func=AF.Abs)

                    @sq.ph("vector")
                    def _(v, k=k, cs=cs, ce=ce):
                        v.reduce_sum(gcol[:, 1:2], sc2[:, 0:ce - cs], AX.X)
                        return v.tensor_add(gcol[:, 0:1], gcol[:, 0:1], gcol[:, 1:2])

            @sq.ph("tensor")
            def _(t):
                return t.matmul(psA[0:1, 0:1], ones_col[:, :], gcol[:, 0:1], start=True, stop=True)

            @sq.ph("vector")
            def _(v):
                return v.tensor_copy(rowsu[0:1, 0:1], psA[0:1, 0:1])

            @sq.ph("tensor")
            def _(t):
                return t.matmul(psR[:, 0:1], ones_row[:, :], rowsu[0:1, 0:1], start=True, stop=True)

            @sq.ph("vector")
            def _(v):
                v.tensor_scalar(gcol[:, 2:3], psR[:, 0:1], 1.0 / per_elem, 1e-5, ALU.mult, ALU.add)
                return v.reciprocal(gcol[:, 3:4], gcol[:, 2:3])

            for k in range(nchunks):
                for cs in range(0, ncols, 2048):
                    ce = min(ncols, cs + 2048)
                    w_ = ce - cs

                    @sq.dma("sync")
                    def _(q, k=k, cs=cs, ce=ce):
                        return [q.dma_start(out=scr[:, 0:ce - cs],
                                            in_=src[k * 128:(k + 1) * 128, cs:ce])]

                    @sq.ph("vector")
                    def _(v, k=k, cs=cs, ce=ce, w_=w_):
                        v.tensor_scalar(sc2[:, 0:w_], scr[:, 0:w_], gcol[:, 3:4], None, ALU.mult)
                        v.tensor_scalar(sc2[:, 2048:2048 + w_], sc2[:, 0:w_], 1.0, 12582912.0, ALU.mult, ALU.add)
                        v.tensor_scalar(sc2[:, 0:w_], sc2[:, 2048:2048 + w_], 1.0, -12582912.0, ALU.mult, ALU.add)
                        v.tensor_scalar_max(sc2[:, 2048:2048 + w_], sc2[:, 0:w_], -1.0)
                        v.tensor_scalar_min(sc2[:, 0:w_], sc2[:, 2048:2048 + w_], 1.0)
                        return v.tensor_copy(wqb[:, k * ncols + cs:k * ncols + ce], sc2[:, 0:w_])

        quant_weight(w1T, NCH, 4096, 4096.0 * 1024.0)

        for o in range(32):
            @sq.ph("tensor")
            def _(t, o=o):
                for tc in range(4):
                    for k in range(NCH):
                        last = t.matmul(psM[:, tc * 512:tc * 512 + 512],
                                        wqb[:, k * 4096 + o * 128:k * 4096 + o * 128 + 128],
                                        Bt[:, k * T + tc * 512:k * T + tc * 512 + 512],
                                        start=(k == 0), stop=(k == NCH - 1))
                return last

            @sq.ph("scalar")
            def _(s, o=o):
                return s.activation(out=hstg[:, :], in_=psM[:, 0:T], func=AF.Relu,
                                    bias=bcol[:, o:o + 1])

            @sq.dma("sync")
            def _(q, o=o):
                return [q.dma_start(out=hbuf[o * 128:(o + 1) * 128, :], in_=hstg[:, :])]

        quant_weight(w2T, 32, 1024, 4096.0 * 1024.0)

        for tc in range(4):
            @sq.dma("sync")
            def _(q, tc=tc):
                return [q.dma_start(out=hh[:, k * 512:(k + 1) * 512],
                                    in_=hbuf[k * 128:(k + 1) * 128, tc * 512:(tc + 1) * 512])
                        for k in range(32)]

            for o in range(NCH):
                @sq.ph("tensor")
                def _(t, tc=tc, o=o):
                    for k in range(32):
                        last = t.matmul(psM[:, 0:512],
                                        wqb[:, k * 1024 + o * 128:k * 1024 + o * 128 + 128],
                                        hh[:, k * 512:(k + 1) * 512],
                                        start=(k == 0), stop=(k == 31))
                    return last

                @sq.ph("scalar")
                def _(s, tc=tc, o=o):
                    return s.activation(out=scr[:, 0:512], in_=psM[:, 0:512], func=AF.Identity,
                                        bias=bcol[:, 32 + o:33 + o])

                @sq.ph("vector")
                def _(v, tc=tc, o=o):
                    sl = slice(o * T + tc * 512, o * T + tc * 512 + 512)
                    return v.tensor_add(A[:, sl], A[:, sl], scr[:, 0:512])

        @sq.dma("sync")
        def _(q):
            return [q.dma_start(out=o_x[k * 128:(k + 1) * 128, :], in_=A[:, k * T:(k + 1) * T])
                    for k in range(NCH)]

        sq.build(nc)
    return nc



def build_lb2():
    SQ = S
    nc = new_nc()
    xln = nc.dram_tensor("xln", [D, SQ], BF16, kind="ExternalInput").ap()
    wq_in = nc.dram_tensor("wq", [D, 256], F32, kind="ExternalInput").ap()
    wk_in = nc.dram_tensor("wk", [D, 256], F32, kind="ExternalInput").ap()
    wv_in = nc.dram_tensor("wv", [D, 256], F32, kind="ExternalInput").ap()
    mask_in = nc.dram_tensor("mask", [128, 128], F32, kind="ExternalInput").ap()
    invf_in = nc.dram_tensor("invf", [32, 1], F32, kind="ExternalInput").ap()
    o_ao = nc.dram_tensor("o_ao", [256, SQ], BF16, kind="ExternalOutput").ap()
    sq = Seq()
    NT = SQ // 128
    NQC = SQ // 512
    with contextlib.ExitStack() as cx:
        E = cx.enter_context
        QT = E(nc.sbuf_tensor("QT", [128, 2 * SQ], BF16))
        KT = E(nc.sbuf_tensor("KT", [128, 2 * SQ], BF16))
        VA = E(nc.sbuf_tensor("VA", [128, NT * 260], BF16))
        wqb = E(nc.sbuf_tensor("wqb", [128, NCH * 256], BF16))
        wkb = E(nc.sbuf_tensor("wkb", [128, NCH * 256], BF16))
        wvb = E(nc.sbuf_tensor("wvb", [128, NCH * 256], BF16))
        xch = E(nc.sbuf_tensor("xch", [128, NCH * 512], BF16))
        CW = 2048
        cosT = E(nc.sbuf_tensor("cosT", [32, CW], F32))
        sinT = E(nc.sbuf_tensor("sinT", [32, CW], F32))
        EB = E(nc.sbuf_tensor("EB", [32, CW], F32))
        OB = E(nc.sbuf_tensor("OB", [32, CW], F32))
        T1 = E(nc.sbuf_tensor("T1", [32, CW], F32))
        T2 = E(nc.sbuf_tensor("T2", [32, CW], F32))
        maskb = E(nc.sbuf_tensor("maskb", [128, 128], F32))
        invfb = E(nc.sbuf_tensor("invfb", [32, 1], F32))
        invfr = E(nc.sbuf_tensor("invfr", [1, 32], F32))
        posr = E(nc.sbuf_tensor("posr", [1, CW], F32))
        nrow = E(nc.sbuf_tensor("nrow", [1, CW], F32))
        ncol = E(nc.sbuf_tensor("ncol", [128, 8], F32))
        PTb = E(nc.sbuf_tensor("PTb", [128, 512], BF16))
        AOt = E(nc.sbuf_tensor("AOt", [64, 512], F32))
        rdrow = E(nc.sbuf_tensor("rdrow", [1, 512], F32))
        scr = E(nc.sbuf_tensor("scr", [128, 2048], F32))
        ones_col = E(nc.sbuf_tensor("ones_col", [128, 1], F32))
        ones_row = E(nc.sbuf_tensor("ones_row", [1, 128], F32))
        psS = E(nc.psum_tensor([128, 512], F32))
        psO = E(nc.psum_tensor([65, 512], F32))
        psR = E(nc.psum_tensor([128, 512], F32))
        psA = E(nc.psum_tensor([32, 512], F32))

        @sq.dma("sync")
        def _(q):
            return [q.dma_start(out=maskb[:, :], in_=mask_in[:, :]),
                    q.dma_start(out=invfb[:, :], in_=invf_in[:, :]),
                    q.dma_start(out=invfr[0:1, :], in_=invf_in.rearrange("a b -> b a")),
                    q.dma_start(out=scr[:, 0:1024].rearrange("p (c n) -> p c n", c=4),
                                in_=wq_in.rearrange("(c p) n -> p c n", p=128)[:, 0:4, :]),
                    q.dma_start(out=scr[:, 1024:2048].rearrange("p (c n) -> p c n", c=4),
                                in_=wq_in.rearrange("(c p) n -> p c n", p=128)[:, 4:8, :])]

        @sq.ph("vector")
        def _(v):
            v.memset(ones_col[:, :], 1.0)
            v.memset(ones_row[:, :], 1.0)
            v.tensor_copy(wqb[:, 0:1024], scr[:, 0:1024])
            return v.tensor_copy(wqb[:, 1024:2048], scr[:, 1024:2048])

        @sq.dma("sync")
        def _(q):
            return [q.dma_start(out=scr[:, 0:1024].rearrange("p (c n) -> p c n", c=4),
                                in_=wk_in.rearrange("(c p) n -> p c n", p=128)[:, 0:4, :]),
                    q.dma_start(out=scr[:, 1024:2048].rearrange("p (c n) -> p c n", c=4),
                                in_=wk_in.rearrange("(c p) n -> p c n", p=128)[:, 4:8, :])]

        @sq.ph("vector")
        def _(v):
            v.tensor_copy(wkb[:, 0:1024], scr[:, 0:1024])
            return v.tensor_copy(wkb[:, 1024:2048], scr[:, 1024:2048])

        @sq.dma("sync")
        def _(q):
            return [q.dma_start(out=scr[:, 0:1024].rearrange("p (c n) -> p c n", c=4),
                                in_=wv_in.rearrange("(c p) n -> p c n", p=128)[:, 0:4, :]),
                    q.dma_start(out=scr[:, 1024:2048].rearrange("p (c n) -> p c n", c=4),
                                in_=wv_in.rearrange("(c p) n -> p c n", p=128)[:, 4:8, :])]

        @sq.ph("vector")
        def _(v):
            v.tensor_copy(wvb[:, 0:1024], scr[:, 0:1024])
            v.tensor_copy(wvb[:, 1024:2048], scr[:, 1024:2048])
            last = v.memset(VA[:, :], 0.0)
            return last

        @sq.ph("vector")
        def _(v):
            for tl in range(NT):
                for hh in range(4):
                    last = v.memset(VA[:, tl * 260 + hh * 65 + 64:tl * 260 + hh * 65 + 65], 1.0)
            return last

        for tc in range(NQC):
            @sq.dma("sync")
            def _(q, tc=tc):
                return [q.dma_start(out=xch[:, k * 512:(k + 1) * 512],
                                    in_=xln[k * 128:(k + 1) * 128, tc * 512:(tc + 1) * 512])
                        for k in range(NCH)]

            for o in range(2):
                @sq.ph("tensor")
                def _(t, tc=tc, o=o):
                    for k in range(NCH):
                        last = t.matmul(psS[:, 0:512],
                                        wqb[:, k * 256 + o * 128:k * 256 + o * 128 + 128],
                                        xch[:, k * 512:(k + 1) * 512],
                                        start=(k == 0), stop=(k == NCH - 1))
                    return last

                @sq.ph("scalar")
                def _(s, tc=tc, o=o):
                    return s.copy(out=QT[:, o * SQ + tc * 512:o * SQ + (tc + 1) * 512],
                                  in_=psS[:, 0:512])

                @sq.ph("tensor")
                def _(t, tc=tc, o=o):
                    for k in range(NCH):
                        last = t.matmul(psS[:, 0:512],
                                        wkb[:, k * 256 + o * 128:k * 256 + o * 128 + 128],
                                        xch[:, k * 512:(k + 1) * 512],
                                        start=(k == 0), stop=(k == NCH - 1))
                    return last

                @sq.ph("scalar")
                def _(s, tc=tc, o=o):
                    return s.copy(out=KT[:, o * SQ + tc * 512:o * SQ + (tc + 1) * 512],
                                  in_=psS[:, 0:512])

            for j in range(4):
                tl = tc * 4 + j

                @sq.ph("tensor")
                def _(t, tc=tc, j=j, tl=tl):
                    for k in range(NCH):
                        last = t.matmul(psO[0:65, 0:256] if False else psS[:, 0:256],
                                        xch[:, k * 512 + j * 128:k * 512 + j * 128 + 128],
                                        wvb[:, k * 256:(k + 1) * 256],
                                        start=(k == 0), stop=(k == NCH - 1))
                    return last

                @sq.ph("vector")
                def _(v, tl=tl):
                    for hh in range(4):
                        last = v.tensor_copy(VA[:, tl * 260 + hh * 65:tl * 260 + hh * 65 + 64],
                                             psS[:, hh * 64:(hh + 1) * 64])
                    return last

        negB = E(nc.sbuf_tensor("negB", [128, 4], F32))
        TWO_PI = 2.0 * math.pi

        @sq.ph("vector")
        def _(v):
            return v.memset(ncol[:, :], 0.0)

        for cb in range(SQ // CW):
            @sq.ph("gpsimd")
            def _(g, cb=cb):
                return g.iota(posr[0:1, :], pattern=[[1, CW]], base=cb * CW,
                              channel_multiplier=0, allow_small_or_imprecise_dtypes=True)

            for sl4 in range(CW // 512):
                @sq.ph("tensor")
                def _(t, sl4=sl4):
                    return t.matmul(psA[:, 0:512], invfr[0:1, :],
                                    posr[0:1, sl4 * 512:(sl4 + 1) * 512], start=True, stop=True)

                @sq.ph("vector")
                def _(v, sl4=sl4):
                    sl = slice(sl4 * 512, (sl4 + 1) * 512)
                    v.tensor_scalar(T1[:, 0:512], psA[0:32, 0:512], 1.0 / TWO_PI, 12582912.0,
                                    ALU.mult, ALU.add)
                    v.tensor_scalar(EB[:, 0:512], T1[:, 0:512], 1.0, -12582912.0, ALU.mult, ALU.add)
                    v.scalar_tensor_tensor(out=cosT[:, sl], in0=EB[:, 0:512], scalar=-TWO_PI,
                                           in1=psA[0:32, 0:512], op0=ALU.mult, op1=ALU.add)
                    v.tensor_scalar(T1[:, 0:512], psA[0:32, 0:512], 1.0 / TWO_PI, 12582912.25,
                                    ALU.mult, ALU.add)
                    v.tensor_scalar(EB[:, 0:512], T1[:, 0:512], 1.0, -12582912.0, ALU.mult, ALU.add)
                    v.scalar_tensor_tensor(out=T1[:, 0:512], in0=EB[:, 0:512], scalar=-TWO_PI,
                                           in1=psA[0:32, 0:512], op0=ALU.mult, op1=ALU.add)
                    return v.tensor_scalar(sinT[:, sl], T1[:, 0:512], 1.0, 0.5 * math.pi,
                                           ALU.mult, ALU.add)

            @sq.ph("scalar")
            def _(s, cb=cb):
                s.activation(out=EB[:, :], in_=cosT[:, :], func=AF.Sin)
                return s.activation(out=OB[:, :], in_=sinT[:, :], func=AF.Sin)

            @sq.ph("vector")
            def _(v, cb=cb):
                v.tensor_copy(sinT[:, :], EB[:, :])
                return v.tensor_copy(cosT[:, :], OB[:, :])

            for mi, mat in enumerate((QT, KT)):
                for hh in range(4):
                    base = (hh // 2) * SQ + cb * CW
                    roff = (hh % 2) * 64
                    mcol = mi * 4 + hh

                    @sq.ph("vector")
                    def _(v, mat=mat, base=base, roff=roff):
                        v.tensor_copy(EB[:, :], mat[roff:roff + 32, base:base + CW])
                        return v.tensor_copy(OB[:, :], mat[roff + 32:roff + 64, base:base + CW])

                    @sq.ph("vector")
                    def _(v, mat=mat, base=base, roff=roff):
                        erow = mat[roff:roff + 32, base:base + CW]
                        orow = mat[roff + 32:roff + 64, base:base + CW]
                        v.tensor_mul(T1[:, :], EB[:, :], cosT[:, :])
                        v.tensor_mul(T2[:, :], OB[:, :], sinT[:, :])
                        v.tensor_sub(erow, T1[:, :], T2[:, :])
                        v.tensor_mul(T1[:, :], EB[:, :], sinT[:, :])
                        v.tensor_mul(T2[:, :], OB[:, :], cosT[:, :])
                        return v.tensor_add(orow, T1[:, :], T2[:, :])

                    @sq.ph("vector")
                    def _(v, mat=mat, base=base, roff=roff):
                        erow = mat[roff:roff + 32, base:base + CW]
                        orow = mat[roff + 32:roff + 64, base:base + CW]
                        v.tensor_mul(T1[:, :], erow, erow)
                        v.tensor_mul(T2[:, :], orow, orow)
                        return v.tensor_add(T1[:, :], T1[:, :], T2[:, :])

                    for sl4 in range(CW // 512):
                        @sq.ph("tensor")
                        def _(t, sl4=sl4):
                            return t.matmul(psA[0:1, 0:512], ones_col[0:32, 0:1],
                                            T1[:, sl4 * 512:(sl4 + 1) * 512],
                                            start=True, stop=True)

                        @sq.ph("vector")
                        def _(v, sl4=sl4):
                            return v.tensor_copy(nrow[0:1, sl4 * 512:(sl4 + 1) * 512],
                                                 psA[0:1, 0:512])

                    @sq.ph("vector")
                    def _(v, mcol=mcol):
                        v.reduce_max(nrow[0:1, 0:1], nrow[0:1, :], AX.X)
                        return v.tensor_max(ncol[0:1, mcol:mcol + 1], ncol[0:1, mcol:mcol + 1],
                                            nrow[0:1, 0:1])

        for hh in range(4):
            @sq.ph("vector")
            def _(v, hh=hh):
                return v.tensor_add(nrow[0:1, 0:1], ncol[0:1, hh:hh + 1],
                                    ncol[0:1, 4 + hh:5 + hh])

            @sq.ph("tensor")
            def _(t, hh=hh):
                return t.matmul(psR[:, 0:1], ones_row[:, :], nrow[0:1, 0:1], start=True, stop=True)

            @sq.ph("vector")
            def _(v, hh=hh):
                return v.tensor_scalar(negB[:, hh:hh + 1], psR[:, 0:1], -1.0 / 16.0, None, ALU.mult)

        for hh in range(4):
            base = (hh // 2) * SQ
            roff = (hh % 2) * 64
            for qc in range(NQC):
                qs = qc * 512
                ktmax = 4 * qc + 4
                for kt in range(ktmax):
                    @sq.ph("tensor")
                    def _(t, hh=hh, base=base, roff=roff, qs=qs, kt=kt):
                        return t.matmul(psS[:, 0:512],
                                        KT[roff:roff + 64, base + kt * 128:base + (kt + 1) * 128],
                                        QT[roff:roff + 64, base + qs:base + qs + 512],
                                        start=True, stop=True)

                    if kt >= 4 * qc:
                        @sq.ph("vector")
                        def _(v, qc=qc, kt=kt):
                            j = kt - 4 * qc
                            return v.tensor_add(psS[:, j * 128:(j + 1) * 128],
                                                psS[:, j * 128:(j + 1) * 128], maskb[:, :])

                        @sq.ph("vector")
                        def _(v, qc=qc, kt=kt):
                            j = kt - 4 * qc
                            if j >= 1:
                                return v.memset(psS[:, 0:j * 128], -1e30)
                            return v.tensor_copy(scr[0:1, 0:1], psS[0:1, 0:1])

                    @sq.ph("scalar")
                    def _(s, hh=hh, kt=kt):
                        return s.activation(out=PTb[:, :], in_=psS[:, 0:512], func=AF.Exp,
                                            scale=0.125, bias=negB[:, hh:hh + 1])

                    @sq.ph("tensor")
                    def _(t, hh=hh, qs=qs, kt=kt, qc=qc):
                        return t.matmul(psO[0:65, 0:512], VA[:, kt * 260 + hh * 65:kt * 260 + (hh + 1) * 65],
                                        PTb[:, :], start=(kt == 0), stop=(kt == 4 * qc + 3))

                @sq.ph("vector")
                def _(v, hh=hh, qs=qs):
                    v.reciprocal(rdrow[0:1, 0:512], psO[64:65, 0:512])
                    return v.tensor_copy(AOt[:, :], psO[0:64, 0:512])

                @sq.ph("tensor")
                def _(t, hh=hh, qs=qs):
                    return t.matmul(psR[0:64, 0:512], ones_row[0:1, 0:64], rdrow[0:1, 0:512],
                                    start=True, stop=True)

                @sq.ph("vector")
                def _(v, hh=hh, qs=qs):
                    return v.tensor_mul(PTb[0:64, 0:512], AOt[:, :], psR[0:64, 0:512])

                @sq.dma("sync")
                def _(q, hh=hh, qs=qs):
                    return [q.dma_start(out=o_ao[hh * 64:(hh + 1) * 64, qs:qs + 512],
                                        in_=PTb[0:64, 0:512])]

        sq.build(nc)
    return nc



def _run(nc, in_maps, cores=None):
    if cores is None:
        cores = list(range(NCORES))
    res = run_bass_kernel_spmd(nc, in_maps, core_ids=cores)
    return res.results


def _shards_fm(x):
    out = []
    for c in range(NCORES):
        b, sl = c // 4, c % 4
        out.append(np.ascontiguousarray(x[b, sl * TSH:(sl + 1) * TSH, :].T))
    return out


def _gather_fm(parts):
    x = np.empty((2, S, D), np.float32)
    for c in range(NCORES):
        b, sl = c // 4, c % 4
        x[b, sl * TSH:(sl + 1) * TSH, :] = parts[c].T
    return x


_IDENT = np.eye(128, dtype=np.float32)


def _sublayer_attn_a(x, P):
    nc = cached("a1", build_a1)
    wres_perm = np.ascontiguousarray(
        P["L0_attn_a_W_res"].reshape(D, 16, 2).transpose(0, 2, 1).reshape(D, 32))
    base = {
        "w_in": P["L0_attn_a_W_in"], "w_xp": P["L0_attn_a_W_xproj"],
        "w_dt": np.ascontiguousarray(np.concatenate(
            [P["L0_attn_a_W_dt"], P["L0_attn_a_b_dt"][None, :]], 0)),
        "w_res": wres_perm, "ident": _IDENT,
    }
    xs = _shards_fm(x)
    res = _run(nc, [dict(base, x_fm=xs[c]) for c in range(NCORES)])
    dt = {}; dtx = {}; bc = {}; xh = {}; sz = {}
    for b in range(2):
        dt[b] = np.concatenate([res[b * 4 + i]["o_dt"] for i in range(4)], axis=1)
        dtx[b] = np.concatenate([res[b * 4 + i]["o_dtx"] for i in range(4)], axis=1)
        bc[b] = np.concatenate([res[b * 4 + i]["o_bc"] for i in range(4)], axis=1)
        xh[b] = np.concatenate([res[b * 4 + i]["o_xhat"] for i in range(4)], axis=0)
        sz[b] = [res[b * 4 + i]["o_siluz"] for i in range(4)]

    NST = 1024
    nc2 = cached("a2", lambda: build_a2(NST))
    p = np.arange(128)
    nl = p // 32
    da = p % 32
    en4 = (nl[:, None] == np.arange(4)[None, :]).astype(np.float32)
    en4t = np.ascontiguousarray(en4.T)
    en32 = (da[:, None] == np.arange(32)[None, :]).astype(np.float32)
    emask = (np.arange(32)[:, None] == da[None, :]).astype(np.float32)
    phases = np.linspace(0.0, 2.0 * np.pi * (1.0 - 1.0 / 16.0), 16)
    ypart = [np.empty((32, S * 32), np.float32) for _ in range(NCORES)]
    state = []
    for c in range(NCORES):
        b, g = c // 4, c % 4
        ng = 4 * g + nl
        hre0 = np.ascontiguousarray(
            np.broadcast_to((0.01 * np.cos(phases))[ng][:, None], (128, 32))).astype(np.float32)
        him0 = np.ascontiguousarray(
            np.broadcast_to((0.01 * np.sin(phases))[ng][:, None], (128, 32))).astype(np.float32)
        th0 = np.full((128, 32), 0.01, np.float32)
        state.append([hre0, him0, th0])
    core_streams = []
    for c in range(NCORES):
        b, g = c // 4, c % 4
        ng4 = 4 * g + np.arange(4)
        ngp = 4 * g + nl
        dt_s = np.ascontiguousarray(dt[b].T.reshape(S, 32, 32).transpose(1, 0, 2).reshape(32, S * 32))
        dtx_s = np.ascontiguousarray(dtx[b].T.reshape(S, 32, 32).transpose(1, 0, 2).reshape(32, S * 32))
        bcb = bc[b]
        bre_s = np.ascontiguousarray(bcb[0 * 32 + ngp * 2 + 0, :])
        bim_s = np.ascontiguousarray(bcb[0 * 32 + ngp * 2 + 1, :])
        cre_s = np.ascontiguousarray(bcb[1 * 32 + ngp * 2 + 0, :])
        cim_s = np.ascontiguousarray(bcb[1 * 32 + ngp * 2 + 1, :])
        xr_s = np.ascontiguousarray(xh[b][:, ngp].T)
        xi_s = np.ascontiguousarray(xh[b][:, 16 + ngp].T)
        alogp = np.ascontiguousarray(P["L0_attn_a_A_log"][ngp][None, :]).astype(np.float32)
        core_streams.append((dt_s, dtx_s, bre_s, bim_s, cre_s, cim_s, xr_s, xi_s, alogp))
    for lk in range(S // NST):
        t0 = lk * NST
        maps = []
        for c in range(NCORES):
            dt_s, dtx_s, bre_s, bim_s, cre_s, cim_s, xr_s, xi_s, alogp = core_streams[c]
            maps.append({
                "h_re0": state[c][0], "h_im0": state[c][1], "th0": state[c][2],
                "dt_s": np.ascontiguousarray(dt_s[:, t0 * 32:(t0 + NST) * 32]),
                "dtx_s": np.ascontiguousarray(dtx_s[:, t0 * 32:(t0 + NST) * 32]),
                "bre_s": np.ascontiguousarray(bre_s[:, t0:t0 + NST]),
                "bim_s": np.ascontiguousarray(bim_s[:, t0:t0 + NST]),
                "cre_s": np.ascontiguousarray(cre_s[:, t0:t0 + NST]),
                "cim_s": np.ascontiguousarray(cim_s[:, t0:t0 + NST]),
                "xr_s": np.ascontiguousarray(xr_s[:, t0:t0 + NST]),
                "xi_s": np.ascontiguousarray(xi_s[:, t0:t0 + NST]),
                "en4": en4, "en4t": en4t, "en32": en32, "emask": emask, "alogp": alogp,
            })
        res2 = _run(nc2, maps)
        for c in range(NCORES):
            state[c] = [res2[c]["o_hre"], res2[c]["o_him"], res2[c]["o_th"]]
            ypart[c][:, t0 * 32:(t0 + NST) * 32] = res2[c]["o_y"]

    nc3 = cached("a3", build_a3)
    woaug = np.ascontiguousarray(np.concatenate(
        [P["L0_attn_a_W_out"], P["L0_attn_a_b_out"][None, :]], 0))
    maps = []
    for c in range(NCORES):
        b, sl = c // 4, c % 4
        m = {"x_fm": _shards_fm(x)[c], "siluz": sz[b][sl], "w_out": woaug}
        for i in range(4):
            yp = ypart[b * 4 + i]
            yps = yp.reshape(32, S, 32)[:, sl * TSH:(sl + 1) * TSH, :]
            m[f"yp{i}"] = np.ascontiguousarray(yps.transpose(0, 2, 1).reshape(D, TSH))
        maps.append(m)
    res3 = _run(nc3, maps)
    return _gather_fm([res3[c]["o_x"] for c in range(NCORES)])


def _sublayer_ffn(x, P):
    nc1 = cached("f1", build_f1)
    w1 = P["L0_ffn_w1"]
    base = {
        "w1t": np.ascontiguousarray(w1[0:D, :]), "w1bot": np.ascontiguousarray(w1[D:2 * D, :]),
        "b1r": np.ascontiguousarray(P["L0_ffn_b1"][None, :]),
        "w2": P["L0_ffn_w2"], "slot_init": P["L0_ffn_slot_init"],
        "taub": np.full((128, 1), float(np.asarray(P["L0_ffn_tau"]).reshape(-1)[0]), np.float32),
        "b2r": np.ascontiguousarray(P["L0_ffn_b2"][None, :]),
        "ident": _IDENT,
    }
    xs = _shards_fm(x)
    res1 = _run(nc1, [dict(base, x_fm=xs[c]) for c in range(NCORES)])

    nc2 = cached("f2", build_f2)
    wihT = np.ascontiguousarray(np.concatenate(
        [P["L0_ffn_Wih"].T, P["L0_ffn_bih"][None, :]], 0))
    whhT = np.ascontiguousarray(np.concatenate(
        [P["L0_ffn_Whh"].T, P["L0_ffn_bhh"][None, :]], 0))
    base2 = {
        "slot_init": P["L0_ffn_slot_init"], "wihT": wihT, "whhT": whhT,
        "wq": P["L0_ffn_wq"], "wk": P["L0_ffn_wk"], "wv": P["L0_ffn_wv"],
        "wvp": P["L0_ffn_wvp"],
        "bq": np.ascontiguousarray(P["L0_ffn_bq"][:, None]),
        "bk": np.ascontiguousarray(P["L0_ffn_bk"][:, None]),
        "bv": np.ascontiguousarray(P["L0_ffn_bv"][:, None]),
        "bvp": np.ascontiguousarray(P["L0_ffn_bvp"][None, :]),
        "lng": np.ascontiguousarray(np.broadcast_to(P["L0_ffn_ln_g"][None, :], (64, D))),
        "lnb": np.ascontiguousarray(np.broadcast_to(P["L0_ffn_ln_b"][None, :], (64, D))),
        "ident": _IDENT,
    }
    maps2 = []
    for c in range(NCORES):
        b = c // 4
        m = dict(base2)
        for i in range(4):
            m[f"sp{i}"] = res1[b * 4 + i]["o_spart"]
        maps2.append(m)
    res2 = _run(nc2, maps2)

    nc3 = cached("f3", build_f3)
    woaug = np.ascontiguousarray(np.concatenate(
        [P["L0_ffn_wo"], P["L0_ffn_bo"][None, :]], 0))
    maps3 = []
    for c in range(NCORES):
        maps3.append({"x_fm": xs[c], "alphaT": np.ascontiguousarray(res1[c]["o_alpha"].T),
                      "sv": res2[c]["o_sv"], "wo": woaug})
    res3 = _run(nc3, maps3)
    return _gather_fm([res3[c]["o_x"] for c in range(NCORES)])


def _sublayer_attn_b(x, P):
    nc1 = cached("c1", build_c1)
    waug = np.ascontiguousarray(np.concatenate(
        [P["L0_attn_b_attn_W"], P["L0_attn_b_attn_b"][None, :]], 0))
    xs = _shards_fm(x)
    res1 = _run(nc1, [{"x_fm": xs[c], "w_in": waug} for c in range(NCORES)])
    import ml_dtypes
    q0 = {c: res1[c]["o_q"] for c in range(NCORES)}
    vv = {c: res1[c]["o_v"] for c in range(NCORES)}
    idcol = P["L0_attn_b_identity"].astype(ml_dtypes.bfloat16)
    idpad = np.ascontiguousarray(np.broadcast_to(idcol[:, None], (D, 2047)))
    wfaug = np.ascontiguousarray(np.concatenate(
        [P["L0_attn_b_freq_W"], P["L0_attn_b_freq_b"][None, :]], 0))
    woutaug = np.ascontiguousarray(np.concatenate(
        [P["L0_attn_b_out_W"], P["L0_attn_b_out_b"][None, :]], 0))

    ncA = cached("c2A", lambda: build_c2("A"))
    ncB = cached("c2B", lambda: build_c2("B"))
    mapsA, mapsB, coresA, coresB = [], [], [], []
    for c in range(NCORES):
        b, sl = c // 4, c % 4
        if sl == 0:
            halo = idpad
        else:
            halo = q0[c - 1][:, TSH - 2047:]
        qext = np.ascontiguousarray(np.concatenate([halo, q0[c]], axis=1))
        m = {"qext": qext, "wf": wfaug, "wout": woutaug, "ident": _IDENT}
        if sl == 0:
            mapsA.append(m); coresA.append(c)
        else:
            mapsB.append(m); coresB.append(c)
    resA = _run(ncA, mapsA, coresA)
    resB = _run(ncB, mapsB, coresB)
    q11 = {}
    for i, c in enumerate(coresA):
        q11[c] = resA[i]["o_q"]
    for i, c in enumerate(coresB):
        q11[c] = resB[i]["o_q"]

    nc34 = cached("c34", build_c34)
    maps = []
    for c in range(NCORES):
        sl = c % 4
        prev = idpad[:, 0:TSH] if sl == 0 else q11[c - 1]
        if sl == 0:
            prev = np.ascontiguousarray(np.broadcast_to(idcol[:, None], (D, TSH)))
        maps.append({"qcur": q11[c], "qprev": prev, "wf": wfaug, "wout": woutaug, "ident": _IDENT})
    res12 = _run(nc34, maps)
    q12 = {c: res12[c]["o_q"] for c in range(NCORES)}
    maps = []
    for c in range(NCORES):
        sl = c % 4
        if sl <= 1:
            prev = np.ascontiguousarray(np.broadcast_to(idcol[:, None], (D, TSH)))
        else:
            prev = q12[c - 2]
        maps.append({"qcur": q12[c], "qprev": prev, "wf": wfaug, "wout": woutaug, "ident": _IDENT})
    res13 = _run(nc34, maps)

    nc5 = cached("c5", build_c5)
    projaug = np.ascontiguousarray(np.concatenate(
        [P["L0_attn_b_proj_W"], P["L0_attn_b_proj_b"][None, :]], 0))
    maps = [{"x_fm": xs[c], "q13": res13[c]["o_q"], "vv": vv[c], "proj": projaug}
            for c in range(NCORES)]
    res5 = _run(nc5, maps)
    return _gather_fm([res5[c]["o_x"] for c in range(NCORES)])


def _fft_consts():
    import ml_dtypes
    bf = ml_dtypes.bfloat16
    r = np.arange(64)[:, None]
    k0 = np.arange(128)[None, :]
    W1 = np.concatenate([np.cos(2 * np.pi * r * k0 / 128.0),
                         -np.sin(2 * np.pi * r * k0 / 128.0)], axis=1).astype(bf)
    c = np.arange(128)
    k1 = np.arange(128)
    th = 2 * np.pi * c[None, :, None] * (np.arange(128)[:, None, None] + 128 * k1[None, None, :]) / 16384.0
    W2C = np.cos(th).astype(bf)
    W2S = np.sin(th).astype(bf)
    thT = np.transpose(th, (0, 2, 1))
    V2C = np.cos(thT).astype(bf)
    V2S = np.sin(thT).astype(bf)
    ph = 2 * np.pi * np.arange(128)[:, None] * np.arange(64)[None, :] / 128.0
    WIC = (np.cos(ph) / 16384.0).astype(bf)
    WIS = (-np.sin(ph) / 16384.0).astype(bf)
    return W1, W2C, W2S, V2C, V2S, WIC, WIS


def _sublayer_l1_attn_a(x, P):
    import ml_dtypes
    bf = ml_dtypes.bfloat16
    ncln = cached("lnonly", build_lnonly)
    xs = _shards_fm(x)
    resln = _run(ncln, [{"x_fm": xs[c]} for c in range(NCORES)])
    xlnb = {b: np.concatenate([resln[b * 4 + i]["o_xln"] for i in range(4)], axis=1) for b in range(2)}
    xln_full = np.concatenate([xlnb[0], xlnb[1]], axis=1)

    ncxz = cached("l1axz", build_l1a_xz)
    WinT = np.ascontiguousarray(P["L1_attn_a_W_in"].T)
    maps = []
    for cr in range(NCORES):
        idx = np.concatenate([np.arange(cr * 256, cr * 256 + 256),
                              2048 + np.arange(cr * 256, cr * 256 + 256)])
        maps.append({"xln": xln_full, "wsl": np.ascontiguousarray(WinT[:, idx])})
    resxz = _run(ncxz, maps)

    nccv = cached("l1aconv", build_l1a_conv)
    maps = []
    for cr in range(NCORES):
        chs = slice(cr * 256, cr * 256 + 256)
        maps.append({"xz": resxz[cr]["o_xz"],
                     "convw": np.ascontiguousarray(P["L1_attn_a_conv_w"][chs]),
                     "convb": np.ascontiguousarray(P["L1_attn_a_conv_b"][chs][:, None]),
                     "kssm": np.ascontiguousarray(P["L1_attn_a_k_ssm"][chs])})
    rescv = _run(nccv, maps)

    W1, W2C, W2S, V2C, V2S, WIC, WIS = _fft_consts()

    ncs1 = cached("fft1", lambda: build_bmm_sigstat(768))
    maps = []
    for cr in range(NCORES):
        xcb = rescv[cr]["o_xcb"]
        kb = rescv[cr]["o_kb"]
        stat = np.zeros((768, 64, 128), bf)
        stat[0:256] = xcb[:, 0:S].reshape(256, 64, 128)
        stat[256:512] = xcb[:, S:2 * S].reshape(256, 64, 128)
        stat[512:768, 0:32, :] = kb.reshape(256, 32, 128)
        maps.append({"stat": stat, "rhs": W1})
    res1 = _run(ncs1, maps)

    ncs2 = cached("fft2", lambda: build_bmm_perk(128, 128, 128, 768))
    maps = []
    for cr in range(NCORES):
        G = res1[cr]["out"]
        Gre = np.ascontiguousarray(np.transpose(G[:, :, 0:128], (2, 1, 0)))
        Gim = np.ascontiguousarray(np.transpose(G[:, :, 128:256], (2, 1, 0)))
        maps.append({"A": W2C, "B": W2S, "C": (-W2S).astype(bf), "Dm": W2C,
                     "U": Gre, "V": Gim})
    res2 = _run(ncs2, maps)

    nccm = cached("fftc", lambda: build_cmul(128, 256, 2))
    maps = []
    for cr in range(NCORES):
        Xre = res2[cr]["ore"]
        Xim = res2[cr]["oim"]
        maps.append({"xre": np.ascontiguousarray(Xre[:, :, 0:512]),
                     "xim": np.ascontiguousarray(Xim[:, :, 0:512]),
                     "kre": np.ascontiguousarray(Xre[:, :, 512:768]),
                     "kim": np.ascontiguousarray(Xim[:, :, 512:768])})
    rescm = _run(nccm, maps)

    nci1 = cached("ffti1", lambda: build_bmm_perk(128, 128, 128, 512))
    maps = []
    for cr in range(NCORES):
        maps.append({"A": V2C, "B": (-V2S).astype(bf), "C": V2S, "Dm": V2C,
                     "U": rescm[cr]["yre"], "V": rescm[cr]["yim"]})
    resi1 = _run(nci1, maps)

    nci2 = cached("ffti2", lambda: build_bmm_acc2(128, 64, 128 * 512))
    maps = []
    for cr in range(NCORES):
        Pre = resi1[cr]["ore"]
        Pim = resi1[cr]["oim"]
        maps.append({"L1": WIC, "L2": WIS,
                     "R1": np.ascontiguousarray(Pre.reshape(128, 128 * 512)),
                     "R2": np.ascontiguousarray(Pim.reshape(128, 128 * 512))})
    resi2 = _run(nci2, maps)

    ncg = cached("l1agate", build_l1a_gate)
    maps = []
    for cr in range(NCORES):
        yo = resi2[cr]["out"].reshape(64, 128, 512)
        yt = np.ascontiguousarray(np.transpose(yo, (2, 1, 0)))
        yb16 = np.ascontiguousarray(np.transpose(yt, (0, 2, 1)).reshape(512, S))
        yy = np.empty((256, 2 * S), bf)
        yy[:, 0:S] = yb16[0:256]
        yy[:, S:2 * S] = yb16[256:512]
        chs = slice(cr * 256, cr * 256 + 256)
        maps.append({"yy": yy, "xc": rescv[cr]["o_xc"], "sz": rescv[cr]["o_sz"],
                     "dsk": np.ascontiguousarray(P["L1_attn_a_D_skip"][chs][:, None])})
    resg = _run(ncg, maps)

    ncmr = cached("matres2048", lambda: build_matres(2048))
    WoutT = np.ascontiguousarray(P["L1_attn_a_W_out"].T)
    maps = []
    for c in range(NCORES):
        b, sl = c // 4, c % 4
        ts = slice(b * 2 * 0 + 0, 0)
        ygall = np.empty((2048, TSH), bf)
        for j in range(NCORES):
            ygall[j * 256:(j + 1) * 256, :] = resg[j]["o_yg"][:, b * S + sl * TSH: b * S + (sl + 1) * TSH]
        maps.append({"x_fm": xs[c], "act": ygall, "wT": WoutT})
    resmr = _run(ncmr, maps)
    return _gather_fm([resmr[c]["o_x"] for c in range(NCORES)])


def _sublayer_l1_ffn(x, P):
    ncf = cached("l1ffn", build_l1ffn)
    xs = _shards_fm(x)
    w1T = np.ascontiguousarray(P["L1_ffn_W1"].T)
    w2T = np.ascontiguousarray(P["L1_ffn_W2"].T)
    maps = [{"x_fm": xs[c], "w1T": w1T, "w2T": w2T,
             "b1c": np.ascontiguousarray(P["L1_ffn_b1"][:, None]),
             "b2c": np.ascontiguousarray(P["L1_ffn_b2"][:, None])} for c in range(NCORES)]
    res = _run(ncf, maps)
    return _gather_fm([res[c]["o_x"] for c in range(NCORES)])


def _sublayer_l1_attn_b(x, P):
    import ml_dtypes
    ncln = cached("lnonly", build_lnonly)
    xs = _shards_fm(x)
    resln = _run(ncln, [{"x_fm": xs[c]} for c in range(NCORES)])
    xlnb = {b: np.concatenate([resln[b * 4 + i]["o_xln"] for i in range(4)], axis=1) for b in range(2)}

    perm = np.concatenate([np.arange(0, 64, 2), np.arange(1, 64, 2)])
    invf = (10000.0 ** (-np.arange(0, 64, 2) / 64.0)).astype(np.float32)[:, None]
    mask = np.where(np.arange(128)[:, None] > np.arange(128)[None, :], -1e30, 0.0).astype(np.float32)

    nc2 = cached("lb2", build_lb2)
    WqT = P["L1_attn_b_Wq"].T
    WkT = P["L1_attn_b_Wk"].T
    WvT = P["L1_attn_b_Wv"].T
    maps = []
    for c in range(NCORES):
        b, hg = c // 4, c % 4
        hsl = []
        for i in range(4):
            h = 4 * hg + i
            hsl.append(h * 64 + perm)
        hperm = np.concatenate(hsl)
        hplain = np.concatenate([np.arange(h * 64, h * 64 + 64) for h in (4 * hg + np.arange(4))])
        maps.append({"xln": xlnb[b],
                     "wq": np.ascontiguousarray(WqT[:, hperm]),
                     "wk": np.ascontiguousarray(WkT[:, hperm]),
                     "wv": np.ascontiguousarray(WvT[:, hplain]),
                     "mask": mask, "invf": invf})
    res2 = _run(nc2, maps)

    ncmr = cached("matres1024", lambda: build_matres(1024))
    WoT = np.ascontiguousarray(P["L1_attn_b_Wo"].T)
    maps = []
    for c in range(NCORES):
        b, sl = c // 4, c % 4
        ao = np.empty((1024, TSH), WoT.dtype)
        import ml_dtypes as _m
        ao = np.empty((1024, TSH), _m.bfloat16)
        for i in range(4):
            ao[i * 256:(i + 1) * 256, :] = res2[b * 4 + i]["o_ao"][:, sl * TSH:(sl + 1) * TSH]
        maps.append({"x_fm": xs[c], "act": ao, "wT": WoT})
    resmr = _run(ncmr, maps)
    return _gather_fm([resmr[c]["o_x"] for c in range(NCORES)])


def kernel(**inputs):
    P = {k: np.asarray(v, np.float32) for k, v in inputs.items()}
    x = P["L0_attn_a_x"]
    x = _sublayer_attn_a(x, P)
    x = _sublayer_ffn(x, P)
    x = _sublayer_attn_b(x, P)
    x = _sublayer_ffn(x, P)
    x = _sublayer_l1_attn_a(x, P)
    x = _sublayer_l1_ffn(x, P)
    x = _sublayer_l1_attn_b(x, P)
    x = _sublayer_l1_ffn(x, P)
    return x.astype(np.float32)


# version 03 (final)
# speedup vs baseline: 1.1036x; 1.1036x over previous
import math
import contextlib
import numpy as np
import concourse.bass as bass
import concourse.mybir as mybir
from concourse.bass_utils import run_bass_kernel_spmd

F32 = mybir.dt.float32
BF16 = mybir.dt.bfloat16
AF = mybir.ActivationFunctionType
ALU = mybir.AluOpType
AX = mybir.AxisListType

NCORES = 8
B = 2
S = 8192
D = 1024
TSH = S // 4
NCH = D // 128

_PROG_CACHE = {}


def cached(key, fn):
    if key not in _PROG_CACHE:
        _PROG_CACHE[key] = fn()
    return _PROG_CACHE[key]


class Seq:

    def __init__(self):
        self.phases = []

    def ph(self, engine):
        def deco(fn):
            self.phases.append((engine, "op", fn))
        return deco

    def dma(self, engine="sync"):
        def deco(fn):
            self.phases.append((engine, "dma", fn))
        return deco

    _NOFENCE = {"wait_ge", "nop", "dma_start", "then_inc", "sem_inc", "memzero"}

    class _Fenced:
        def __init__(self, eng, sem):
            self._eng = eng
            self._sem = sem
            self._n = 0

        def __getattr__(self, name):
            attr = getattr(self._eng, name)
            if not callable(attr) or name.startswith("_") or name in Seq._NOFENCE:
                return attr

            def wrapper(*a, **k):
                inst = attr(*a, **k)
                try:
                    self._n += 1
                    inst.then_inc(self._sem, 1)
                    self._eng.wait_ge(self._sem, self._n)
                except Exception:
                    raise
                return inst
            return wrapper

    def build(self, nc):
        with contextlib.ExitStack() as st:
            g = st.enter_context(nc.semaphore("g_seq"))
            dsems = {e: st.enter_context(nc.semaphore(f"d_{e}"))
                     for e in ("sync", "vector", "scalar", "tensor", "gpsimd")}
            csems = {e: st.enter_context(nc.semaphore(f"c_{e}"))
                     for e in ("vector", "scalar", "gpsimd")}
            block = st.enter_context(nc.Block())

            def run_engine(eng, name):
                dma_cum = 0
                fenced = Seq._Fenced(eng, csems[name]) if name in csems else None
                for i, (ename, kind, fn) in enumerate(self.phases):
                    if ename != name:
                        continue
                    if i > 0:
                        eng.wait_ge(g, i)
                    if kind == "op":
                        last = fn(fenced if fenced is not None else eng)
                        assert last is not None, f"phase {i} ({name}) returned None"
                        if fenced is not None:
                            eng.nop().then_inc(g, 1)
                        else:
                            last.then_inc(g, 1)
                    else:
                        insts = fn(eng)
                        assert insts, f"dma phase {i} empty"
                        for inst in insts:
                            inst.then_inc(dsems[name], 16)
                        dma_cum += 16 * len(insts)
                        eng.wait_ge(dsems[name], dma_cum)
                        eng.nop().then_inc(g, 1)

            @block.sync
            def _(e):
                run_engine(e, "sync")

            @block.vector
            def _(e):
                run_engine(e, "vector")

            @block.scalar
            def _(e):
                run_engine(e, "scalar")

            @block.tensor
            def _(e):
                run_engine(e, "tensor")

            @block.gpsimd
            def _(e):
                run_engine(e, "gpsimd")
        return nc


def new_nc():
    return bass.Bass("TRN2", target_bir_lowering=False)



def add_ln_phases(sq, h, T=TSH, eps=1e-5):
    ntc = T // 512
    for tc in range(ntc):
        @sq.ph("tensor")
        def _(t, tc=tc):
            for k in range(NCH):
                last = t.matmul(h["psA"][0:1, 0:512], h["ones_col"][:, :],
                                h["xs"][:, k * T + tc * 512:k * T + tc * 512 + 512],
                                start=(k == 0), stop=(k == NCH - 1))
            return last

        @sq.ph("vector")
        def _(v, tc=tc):
            return v.tensor_copy(h["row_su"][0:1, tc * 512:tc * 512 + 512], h["psA"][0:1, 0:512])

        @sq.ph("scalar")
        def _(s, tc=tc):
            for k in range(NCH):
                last = s.activation(out=h["scr"][:, k * 256:k * 256 + 256],
                                    in_=h["xs"][:, k * T + tc * 512:k * T + tc * 512 + 256],
                                    func=AF.Square)
            return last

        @sq.ph("tensor")
        def _(t, tc=tc):
            for k in range(NCH):
                last = t.matmul(h["psA"][0:1, 0:256], h["ones_col"][:, :],
                                h["scr"][:, k * 256:k * 256 + 256],
                                start=(k == 0), stop=(k == NCH - 1))
            return last

        @sq.ph("scalar")
        def _(s, tc=tc):
            for k in range(NCH):
                last = s.activation(out=h["scr"][:, k * 256:k * 256 + 256],
                                    in_=h["xs"][:, k * T + tc * 512 + 256:k * T + tc * 512 + 512],
                                    func=AF.Square)
            return last

        @sq.ph("tensor")
        def _(t, tc=tc):
            for k in range(NCH):
                last = t.matmul(h["psA"][0:1, 256:512], h["ones_col"][:, :],
                                h["scr"][:, k * 256:k * 256 + 256],
                                start=(k == 0), stop=(k == NCH - 1))
            return last

        @sq.ph("vector")
        def _(v, tc=tc):
            return v.tensor_copy(h["row_va"][0:1, tc * 512:tc * 512 + 512], h["psA"][0:1, 0:512])

    @sq.ph("vector")
    def _(v):
        su = h["row_su"][0:1, 0:T]
        va = h["row_va"][0:1, 0:T]
        v.tensor_scalar_mul(su, su, 1.0 / D)
        v.tensor_mul(h["scr"][0:1, 0:T], su, su)
        v.scalar_tensor_tensor(out=va, in0=va, scalar=1.0 / D,
                               in1=h["scr"][0:1, 0:T], op0=ALU.mult, op1=ALU.subtract)
        return v.tensor_scalar_add(va, va, eps)

    @sq.ph("scalar")
    def _(s):
        va = h["row_va"][0:1, 0:T]
        s.activation(out=h["scr"][0:1, 0:T], in_=va, func=AF.Ln)
        return s.activation(out=va, in_=h["scr"][0:1, 0:T], func=AF.Exp, scale=-0.5)

    for tc in range(ntc):
        @sq.ph("tensor")
        def _(t, tc=tc):
            t.matmul(h["psR"][:, 0:512], h["ones_row"][:, :],
                     h["row_va"][0:1, tc * 512:tc * 512 + 512], start=True, stop=True)
            return t.matmul(h["psR"][:, 512:1024], h["ones_row"][:, :],
                            h["row_su"][0:1, tc * 512:tc * 512 + 512], start=True, stop=True)

        @sq.ph("vector")
        def _(v, tc=tc):
            for k in range(NCH):
                sl = slice(k * T + tc * 512, k * T + tc * 512 + 512)
                v.tensor_sub(h["scr"][:, 0:512], h["xs"][:, sl], h["psR"][:, 512:1024])
                last = v.tensor_mul(h["xln"][:, sl], h["scr"][:, 0:512], h["psR"][:, 0:512])
                if "xln2" in h:
                    last = v.tensor_mul(h["xln2"][:, sl], h["scr"][:, 0:512], h["psR"][:, 0:512])
            return last



def build_a1():
    T = TSH
    nc = new_nc()
    x_fm = nc.dram_tensor("x_fm", [D, T], F32, kind="ExternalInput").ap()
    w_in = nc.dram_tensor("w_in", [D, 2 * D], F32, kind="ExternalInput").ap()
    w_xp = nc.dram_tensor("w_xp", [D, 128], F32, kind="ExternalInput").ap()
    w_dt = nc.dram_tensor("w_dt", [65, D], F32, kind="ExternalInput").ap()
    w_res = nc.dram_tensor("w_res", [D, 32], F32, kind="ExternalInput").ap()
    ident = nc.dram_tensor("ident", [128, 128], F32, kind="ExternalInput").ap()
    o_siluz = nc.dram_tensor("o_siluz", [D, T], F32, kind="ExternalOutput").ap()
    o_dt = nc.dram_tensor("o_dt", [D, T], F32, kind="ExternalOutput").ap()
    o_dtx = nc.dram_tensor("o_dtx", [D, T], F32, kind="ExternalOutput").ap()
    o_bc = nc.dram_tensor("o_bc", [64, T], F32, kind="ExternalOutput").ap()
    o_xhat = nc.dram_tensor("o_xhat", [T, 32], F32, kind="ExternalOutput").ap()

    sq = Seq()
    with contextlib.ExitStack() as cx:
        E = cx.enter_context
        A = E(nc.sbuf_tensor("A", [128, NCH * T], F32))
        Bt = E(nc.sbuf_tensor("Bt", [128, NCH * T], BF16))
        Ct = E(nc.sbuf_tensor("Ct", [128, NCH * T], BF16))
        wb = E(nc.sbuf_tensor("wb", [128, NCH * 2048], BF16))
        wxb = E(nc.sbuf_tensor("wxb", [128, NCH * 128], BF16))
        wdtb = E(nc.sbuf_tensor("wdtb", [65, D], BF16))
        wresb = E(nc.sbuf_tensor("wresb", [128, NCH * 32], F32))
        idb = E(nc.sbuf_tensor("idb", [128, 128], F32))
        scr = E(nc.sbuf_tensor("scr", [128, 4096], F32))
        scr2 = E(nc.sbuf_tensor("scr2", [128, 2048], F32))
        xpb = E(nc.sbuf_tensor("xpb", [65, T], BF16))
        dtt = E(nc.sbuf_tensor("dtt", [128, T], F32))
        xhat = E(nc.sbuf_tensor("xhat", [128, 16 * 32], F32))
        ones_col = E(nc.sbuf_tensor("ones_col", [128, 1], F32))
        ones_row = E(nc.sbuf_tensor("ones_row", [1, 128], F32))
        cst = E(nc.sbuf_tensor("cst", [128, 4], F32))
        psA = E(nc.psum_tensor([1, 512], F32))
        psR = E(nc.psum_tensor([128, 1024], F32))
        psM = E(nc.psum_tensor([128, 2048], F32))
        psT = E(nc.psum_tensor([128, 128], F32))

        @sq.dma("sync")
        def _(q):
            ins = [q.dma_start(out=A[:, k * T:(k + 1) * T], in_=x_fm[k * 128:(k + 1) * 128, :])
                   for k in range(NCH)]
            ins.append(q.dma_start(out=idb[:, :], in_=ident[:, :]))
            return ins

        @sq.ph("vector")
        def _(v):
            v.memset(ones_col[:, :], 1.0)
            v.memset(ones_row[:, :], 1.0)
            v.memset(cst[:, 0:1], 1.0)
            v.memset(cst[:, 1:2], 1e-8)
            return v.memset(cst[:, 2:3], 1e-10)

        h = dict(xs=A, xln=Bt, scr=scr,
                 row_su=dtt[0:1, :], row_va=scr2[0:1, 0:T],
                 psA=psA, psR=psR, ones_col=ones_col, ones_row=ones_row)
        add_ln_phases(sq, h, T)

        for k in range(NCH):
            @sq.dma("sync")
            def _(q, k=k):
                return [q.dma_start(out=scr[:, 0:2048], in_=w_in[k * 128:(k + 1) * 128, :])]

            @sq.ph("vector")
            def _(v, k=k):
                return v.tensor_copy(wb[:, k * 2048:(k + 1) * 2048], scr[:, 0:2048])

        @sq.dma("sync")
        def _(q):
            return [q.dma_start(out=scr[:, 0:1024].rearrange("p (c n) -> p c n", c=8),
                                in_=w_xp.rearrange("(c p) n -> p c n", p=128)),
                    q.dma_start(out=scr[0:65, 1024:2048], in_=w_dt[:, :]),
                    q.dma_start(out=scr[0:128, 2048:2048 + 256].rearrange("p (c n) -> p c n", c=8),
                                in_=w_res.rearrange("(c p) n -> p c n", p=128))]

        @sq.ph("vector")
        def _(v):
            v.tensor_copy(wxb[:, :], scr[:, 0:1024])
            v.tensor_copy(wdtb[:, :], scr[0:65, 1024:2048])
            return v.tensor_copy(wresb[:, :], scr[:, 2048:2048 + 256])

        for o in range(16):
            @sq.ph("tensor")
            def _(t, o=o):
                for tc in range(4):
                    for k in range(NCH):
                        last = t.matmul(psM[:, tc * 512:tc * 512 + 512],
                                        wb[:, k * 2048 + o * 128:k * 2048 + o * 128 + 128],
                                        Bt[:, k * T + tc * 512:k * T + tc * 512 + 512],
                                        start=(k == 0), stop=(k == NCH - 1))
                return last

            if o < 8:
                @sq.ph("scalar")
                def _(s, o=o):
                    s.activation(out=A[:, o * T:(o + 1) * T], in_=psM[:, 0:T], func=AF.Silu)
                    return s.activation(out=Ct[:, o * T:(o + 1) * T], in_=psM[:, 0:T], func=AF.Silu)
            else:
                @sq.ph("scalar")
                def _(s, o=o):
                    return s.activation(out=scr2[:, 0:T], in_=psM[:, 0:T], func=AF.Silu)

                @sq.dma("sync")
                def _(q, o=o):
                    return [q.dma_start(out=o_siluz[(o - 8) * 128:(o - 7) * 128, :], in_=scr2[:, 0:T])]

        @sq.ph("tensor")
        def _(t):
            for tc in range(4):
                for k in range(NCH):
                    last = t.matmul(psM[:, tc * 512:tc * 512 + 512],
                                    wxb[:, k * 128:(k + 1) * 128],
                                    Ct[:, k * T + tc * 512:k * T + tc * 512 + 512],
                                    start=(k == 0), stop=(k == NCH - 1))
            return last

        @sq.ph("vector")
        def _(v):
            return v.tensor_copy(scr2[:, 0:T], psM[:, 0:T])

        @sq.dma("sync")
        def _(q):
            return [q.dma_start(out=o_bc[:, :], in_=scr2[64:128, 0:T])]

        @sq.ph("vector")
        def _(v):
            v.tensor_copy(xpb[0:64, :], scr2[0:64, 0:T])
            return v.memset(xpb[64:65, :], 1.0)

        for o in range(NCH):
            @sq.ph("tensor")
            def _(t, o=o):
                for tc in range(4):
                    last = t.matmul(psM[:, tc * 512:tc * 512 + 512],
                                    wdtb[:, o * 128:(o + 1) * 128],
                                    xpb[:, tc * 512:tc * 512 + 512],
                                    start=True, stop=True)
                return last

            @sq.ph("scalar")
            def _(s, o=o):
                s.activation(out=scr[:, 0:T], in_=psM[:, 0:T], func=AF.Abs)
                s.activation(out=scr[:, 0:T], in_=scr[:, 0:T], func=AF.Exp, scale=-1.0)
                return s.activation(out=scr[:, 0:T], in_=scr[:, 0:T], func=AF.Ln, bias=cst[:, 0:1])

            @sq.ph("vector")
            def _(v, o=o):
                v.tensor_scalar_max(scr2[:, 0:T], psM[:, 0:T], 0.0)
                v.tensor_add(dtt[:, :], scr2[:, 0:T], scr[:, 0:T])
                return v.tensor_mul(scr2[:, 0:T], dtt[:, :], A[:, o * T:(o + 1) * T])

            @sq.dma("sync")
            def _(q, o=o):
                return [q.dma_start(out=o_dt[o * 128:(o + 1) * 128, :], in_=dtt[:, :]),
                        q.dma_start(out=o_dtx[o * 128:(o + 1) * 128, :], in_=scr2[:, 0:T])]

        @sq.ph("tensor")
        def _(t):
            for tc in range(4):
                for k in range(NCH):
                    last = t.matmul(psM[0:32, tc * 512:tc * 512 + 512],
                                    wresb[:, k * 32:(k + 1) * 32],
                                    A[:, k * T + tc * 512:k * T + tc * 512 + 512],
                                    start=(k == 0), stop=(k == NCH - 1))
            return last

        xres = scr[0:32, 2048:4096]

        @sq.ph("vector")
        def _(v):
            return v.tensor_copy(xres, psM[0:32, 0:T])

        for tl in range(16):
            @sq.ph("tensor")
            def _(t, tl=tl):
                t.transpose(psT[:, 0:32], xres[:, tl * 128:(tl + 1) * 128], idb[0:32, 0:32])
                return t.matmul(psT[0:1, 32:33], ones_col[0:1, 0:1], ones_col[0:1, 0:1],
                                start=True, stop=True)

            @sq.ph("vector")
            def _(v, tl=tl):
                v.tensor_copy(scr[:, 256:288], psT[:, 0:32])
                re = scr[:, 256:272]
                return v.tensor_add(scr[:, 0:16], re, cst[:, 2:3].broadcast_to([128, 16]))

            @sq.ph("scalar")
            def _(sc, tl=tl):
                sc.activation(out=scr[:, 64:80], in_=scr[:, 0:16], func=AF.Square)
                return sc.activation(out=scr[:, 128:144], in_=scr[:, 272:288], func=AF.Square)

            @sq.ph("vector")
            def _(v, tl=tl):
                return v.tensor_add(scr[:, 64:80], scr[:, 64:80], scr[:, 128:144])

            @sq.ph("scalar")
            def _(sc, tl=tl):
                sc.activation(out=scr[:, 80:96], in_=scr[:, 64:80], func=AF.Ln)
                return sc.activation(out=scr[:, 80:96], in_=scr[:, 80:96], func=AF.Exp, scale=-0.5)

            @sq.ph("vector")
            def _(v, tl=tl):
                dst = xhat[:, tl * 32:(tl + 1) * 32]
                v.tensor_mul(dst[:, 0:16], scr[:, 0:16], scr[:, 80:96])
                return v.tensor_mul(dst[:, 16:32], scr[:, 272:288], scr[:, 80:96])

        @sq.dma("sync")
        def _(q):
            return [q.dma_start(out=o_xhat[tl * 128:(tl + 1) * 128, :],
                                in_=xhat[:, tl * 32:(tl + 1) * 32]) for tl in range(16)]

        sq.build(nc)
    return nc



def build_a2(nstep, chunk=128):
    nc = new_nc()
    nchunk = nstep // chunk
    CH = chunk * 32
    h_re0 = nc.dram_tensor("h_re0", [128, 32], F32, kind="ExternalInput").ap()
    h_im0 = nc.dram_tensor("h_im0", [128, 32], F32, kind="ExternalInput").ap()
    th0 = nc.dram_tensor("th0", [128, 32], F32, kind="ExternalInput").ap()
    dt_s = nc.dram_tensor("dt_s", [32, nstep * 32], F32, kind="ExternalInput").ap()
    dtx_s = nc.dram_tensor("dtx_s", [32, nstep * 32], F32, kind="ExternalInput").ap()
    bre_s = nc.dram_tensor("bre_s", [128, nstep], F32, kind="ExternalInput").ap()
    bim_s = nc.dram_tensor("bim_s", [128, nstep], F32, kind="ExternalInput").ap()
    cre_s = nc.dram_tensor("cre_s", [128, nstep], F32, kind="ExternalInput").ap()
    cim_s = nc.dram_tensor("cim_s", [128, nstep], F32, kind="ExternalInput").ap()
    xr_s = nc.dram_tensor("xr_s", [128, nstep], F32, kind="ExternalInput").ap()
    xi_s = nc.dram_tensor("xi_s", [128, nstep], F32, kind="ExternalInput").ap()
    en4 = nc.dram_tensor("en4", [128, 4], F32, kind="ExternalInput").ap()
    en32 = nc.dram_tensor("en32", [128, 32], F32, kind="ExternalInput").ap()
    en4t = nc.dram_tensor("en4t", [4, 128], F32, kind="ExternalInput").ap()
    emask = nc.dram_tensor("emask", [32, 128], F32, kind="ExternalInput").ap()
    alogp = nc.dram_tensor("alogp", [1, 128], F32, kind="ExternalInput").ap()
    o_y = nc.dram_tensor("o_y", [32, nstep * 32], F32, kind="ExternalOutput").ap()
    o_hre = nc.dram_tensor("o_hre", [128, 32], F32, kind="ExternalOutput").ap()
    o_him = nc.dram_tensor("o_him", [128, 32], F32, kind="ExternalOutput").ap()
    o_th = nc.dram_tensor("o_th", [128, 32], F32, kind="ExternalOutput").ap()

    sq = Seq()
    with contextlib.ExitStack() as cx:
        E = cx.enter_context
        hre = E(nc.sbuf_tensor("hre", [128, 32], F32))
        him = E(nc.sbuf_tensor("him", [128, 32], F32))
        th = E(nc.sbuf_tensor("th", [128, 32], F32))
        dtc = E(nc.sbuf_tensor("dtc", [32, CH], F32))
        dtxc = E(nc.sbuf_tensor("dtxc", [32, CH], F32))
        dtcb = E(nc.sbuf_tensor("dtcb", [32, CH], BF16))
        dtxcb = E(nc.sbuf_tensor("dtxcb", [32, CH], BF16))
        gt = E(nc.sbuf_tensor("gt", [128, CH], F32))
        ga = E(nc.sbuf_tensor("ga", [128, CH], F32))
        dtxr = E(nc.sbuf_tensor("dtxr", [128, CH], F32))
        bxre = E(nc.sbuf_tensor("bxre", [128, CH], F32))
        bxim = E(nc.sbuf_tensor("bxim", [128, CH], F32))
        brc = E(nc.sbuf_tensor("brc", [128, chunk], F32))
        bic = E(nc.sbuf_tensor("bic", [128, chunk], F32))
        crc = E(nc.sbuf_tensor("crc", [128, chunk], F32))
        cic = E(nc.sbuf_tensor("cic", [128, chunk], F32))
        xrc = E(nc.sbuf_tensor("xrc", [128, chunk], F32))
        xic = E(nc.sbuf_tensor("xic", [128, chunk], F32))
        w2 = E(nc.sbuf_tensor("w2", [128, 12], F32))
        yb = E(nc.sbuf_tensor("yb", [128, CH], F32))
        en4b = E(nc.sbuf_tensor("en4b", [128, 4], F32))
        en32b = E(nc.sbuf_tensor("en32b", [128, 32], F32))
        yred = E(nc.sbuf_tensor("yred", [32, 4096], F32))
        en4tb = E(nc.sbuf_tensor("en4tb", [4, 128], F32))
        eab = E(nc.sbuf_tensor("eab", [32, 128], BF16))
        erb = E(nc.sbuf_tensor("erb", [32, 128], BF16))
        emk = E(nc.sbuf_tensor("emk", [32, 128], F32))
        alg = E(nc.sbuf_tensor("alg", [1, 256], F32))
        cst = E(nc.sbuf_tensor("cst", [128, 8], F32))
        sc = E(nc.sbuf_tensor("sc", [128, 64], F32))
        th2 = E(nc.sbuf_tensor("th2", [128, 32], F32))
        Lt = E(nc.sbuf_tensor("Lt", [128, 32], F32))
        glt = E(nc.sbuf_tensor("glt", [128, 32], F32))
        ut = E(nc.sbuf_tensor("ut", [128, 32], F32))
        Ft = E(nc.sbuf_tensor("Ft", [128, 32], F32))
        sqre = E(nc.sbuf_tensor("sqre", [128, 32], F32))
        sqim = E(nc.sbuf_tensor("sqim", [128, 32], F32))
        m2t = E(nc.sbuf_tensor("m2t", [128, 32], F32))
        L2t = E(nc.sbuf_tensor("L2t", [128, 32], F32))
        mg2 = E(nc.sbuf_tensor("mg2", [128, 32], F32))
        rm2 = E(nc.sbuf_tensor("rm2", [128, 32], F32))
        F2t = E(nc.sbuf_tensor("F2t", [128, 32], F32))
        yt1 = E(nc.sbuf_tensor("yt1", [128, 32], F32))
        tiny = E(nc.sbuf_tensor("tiny", [4, 16], F32))
        wq128 = E(nc.sbuf_tensor("wq128", [128, 1], F32))
        scr32 = E(nc.sbuf_tensor("scr32", [32, 128], F32))
        psC = E(nc.psum_tensor([4, 64], F32))
        psW = E(nc.psum_tensor([128, 2], F32))
        psB = E(nc.psum_tensor([128, 2048], F32))
        psE = E(nc.psum_tensor([32, 128], F32))

        @sq.dma("sync")
        def _(q):
            return [q.dma_start(out=hre[:, :], in_=h_re0[:, :]),
                    q.dma_start(out=him[:, :], in_=h_im0[:, :]),
                    q.dma_start(out=th[:, :], in_=th0[:, :]),
                    q.dma_start(out=en4b[:, :], in_=en4[:, :]),
                    q.dma_start(out=en32b[:, :], in_=en32[:, :]),
                    q.dma_start(out=en4tb[:, :], in_=en4t[:, :]),
                    q.dma_start(out=emk[:, :], in_=emask[:, :]),
                    q.dma_start(out=alg[0:1, 0:128], in_=alogp[:, :])]

        @sq.ph("vector")
        def _(v):
            v.memset(cst[:, 0:1], 1.0)
            v.memset(cst[:, 1:2], 1e-8)
            v.memset(cst[:, 2:3], 1.024e-7)
            v.memset(tiny[:, :], 0.0)
            v.memset(w2[:, :], 0.0)
            v.memset(sc[:, :], 0.0)
            return v.memset(cst[:, 3:4], 0.0)

        @sq.ph("scalar")
        def _(s):
            return s.activation(out=alg[0:1, 128:256], in_=alg[0:1, 0:128], func=AF.Exp)

        @sq.ph("tensor")
        def _(t):
            return t.matmul(psE[:, :], cst[0:1, 0:1].broadcast_to([1, 32]),
                            alg[0:1, 128:256], start=True, stop=True)

        @sq.ph("vector")
        def _(v):
            v.tensor_mul(scr32[:, :], psE[:, :], emk[:, :])
            v.tensor_scalar_mul(scr32[:, :], scr32[:, :], -1.0)
            v.tensor_copy(eab[:, :], scr32[:, :])
            return v.tensor_copy(erb[:, :], emk[:, :])

        for ck in range(nchunk):
            t0 = ck * chunk

            @sq.dma("sync")
            def _(q, t0=t0):
                return [q.dma_start(out=dtc[:, :], in_=dt_s[:, t0 * 32:(t0 + chunk) * 32]),
                        q.dma_start(out=dtxc[:, :], in_=dtx_s[:, t0 * 32:(t0 + chunk) * 32]),
                        q.dma_start(out=brc[:, :], in_=bre_s[:, t0:t0 + chunk]),
                        q.dma_start(out=bic[:, :], in_=bim_s[:, t0:t0 + chunk]),
                        q.dma_start(out=crc[:, :], in_=cre_s[:, t0:t0 + chunk]),
                        q.dma_start(out=cic[:, :], in_=cim_s[:, t0:t0 + chunk]),
                        q.dma_start(out=xrc[:, :], in_=xr_s[:, t0:t0 + chunk]),
                        q.dma_start(out=xic[:, :], in_=xi_s[:, t0:t0 + chunk])]

            @sq.ph("vector")
            def _(v):
                v.tensor_copy(dtcb[:, :], dtc[:, :])
                return v.tensor_copy(dtxcb[:, :], dtxc[:, :])

            for half in range(CH // 2048):
                hsl = slice(half * 2048, (half + 1) * 2048)

                @sq.ph("tensor")
                def _(t, half=half):
                    for j in range(4):
                        sl = slice(half * 2048 + j * 512, half * 2048 + j * 512 + 512)
                        last = t.matmul(psB[:, j * 512:(j + 1) * 512], eab[:, :],
                                        dtcb[:, sl], start=True, stop=True)
                    return last

                @sq.ph("scalar")
                def _(s, half=half, hsl=hsl):
                    return s.activation(out=gt[:, hsl], in_=psB[:, 0:2048], func=AF.Exp)

                @sq.ph("tensor")
                def _(t, half=half):
                    for j in range(4):
                        sl = slice(half * 2048 + j * 512, half * 2048 + j * 512 + 512)
                        last = t.matmul(psB[:, j * 512:(j + 1) * 512], erb[:, :],
                                        dtxcb[:, sl], start=True, stop=True)
                    return last

                @sq.ph("scalar")
                def _(s, half=half, hsl=hsl):
                    return s.copy(out=dtxr[:, hsl], in_=psB[:, 0:2048])

            @sq.ph("vector")
            def _(v):
                v.tensor_scalar(ga[:, :], gt[:, :], -0.25, 0.25, ALU.mult, ALU.add)
                dtv = dtxr[:, :].rearrange("p (t d) -> p t d", d=32)
                brv = brc[:, :].rearrange("p (t u) -> p t u", u=1).broadcast_to([128, chunk, 32])
                biv = bic[:, :].rearrange("p (t u) -> p t u", u=1).broadcast_to([128, chunk, 32])
                v.tensor_tensor(out=bxre[:, :].rearrange("p (t d) -> p t d", d=32),
                                in0=dtv, in1=brv, op=ALU.mult)
                return v.tensor_tensor(out=bxim[:, :].rearrange("p (t d) -> p t d", d=32),
                                       in0=dtv, in1=biv, op=ALU.mult)

            for ct in range(chunk):
                ssl = slice(ct * 32, ct * 32 + 32)

                @sq.ph("tensor")
                def _(t, ct=ct):
                    t.matmul(psC[0:4, 0:32], en4b[:, :], hre[:, :], start=True, stop=True)
                    return t.matmul(psC[0:4, 32:64], en4b[:, :], him[:, :], start=True, stop=True)

                @sq.ph("vector")
                def _(v, ct=ct):
                    v.tensor_reduce(tiny[0:4, 0:1], psC[0:4, 0:32], AX.X, ALU.add)
                    v.tensor_reduce(tiny[0:4, 1:2], psC[0:4, 32:64], AX.X, ALU.add)
                    return v.tensor_add(tiny[0:4, 2:3], tiny[0:4, 0:1], cst[0:4, 2:3])

                @sq.ph("tensor")
                def _(t, ct=ct):
                    t.matmul(psW[:, 0:1], en4tb[:, :], tiny[0:4, 2:3], start=True, stop=True)
                    return t.matmul(psW[:, 1:2], en4tb[:, :], tiny[0:4, 1:2], start=True, stop=True)

                @sq.ph("vector")
                def _(v, ct=ct):
                    v.tensor_copy(w2[:, 0:2], psW[:, 0:2])
                    v.tensor_mul(w2[:, 2:3], xrc[:, ct:ct + 1], w2[:, 0:1])
                    v.scalar_tensor_tensor(out=w2[:, 3:4], in0=xic[:, ct:ct + 1],
                                           scalar=w2[:, 1:2], in1=w2[:, 2:3],
                                           op0=ALU.mult, op1=ALU.add)
                    v.tensor_mul(w2[:, 4:5], w2[:, 0:1], w2[:, 0:1])
                    v.tensor_mul(w2[:, 5:6], w2[:, 1:2], w2[:, 1:2])
                    return v.tensor_mul(th2[:, :], th[:, :], th[:, :])

                @sq.ph("scalar")
                def _(s, ct=ct):
                    return s.activation(out=Lt[:, :], in_=th2[:, :], func=AF.Ln, bias=cst[:, 1:2])

                @sq.ph("vector")
                def _(v, ct=ct, ssl=ssl):
                    v.tensor_add(w2[:, 4:5], w2[:, 4:5], w2[:, 5:6])
                    return v.tensor_mul(glt[:, :], ga[:, ssl], Lt[:, :])

                @sq.ph("scalar")
                def _(s, ct=ct):
                    s.activation(out=w2[:, 5:6], in_=w2[:, 4:5], func=AF.Ln)
                    return s.activation(out=w2[:, 6:7], in_=w2[:, 5:6], func=AF.Exp, scale=-0.5)

                @sq.ph("vector")
                def _(v, ct=ct):
                    v.tensor_mul(w2[:, 7:8], w2[:, 3:4], w2[:, 6:7])
                    v.tensor_scalar(w2[:, 8:9], w2[:, 7:8], -0.5, 0.5, ALU.mult, ALU.add)
                    return v.tensor_scalar(ut[:, :], glt[:, :], w2[:, 8:9], None, ALU.mult)

                @sq.ph("scalar")
                def _(s, ct=ct):
                    return s.activation(out=Ft[:, :], in_=ut[:, :], func=AF.Exp)

                @sq.ph("vector")
                def _(v, ct=ct, ssl=ssl):
                    v.tensor_mul(hre[:, :], Ft[:, :], hre[:, :])
                    v.tensor_mul(him[:, :], Ft[:, :], him[:, :])
                    v.tensor_add(hre[:, :], hre[:, :], bxre[:, ssl])
                    v.tensor_add(him[:, :], him[:, :], bxim[:, ssl])
                    v.tensor_mul(sqre[:, :], hre[:, :], hre[:, :])
                    v.tensor_mul(sqim[:, :], him[:, :], him[:, :])
                    return v.tensor_add(m2t[:, :], sqre[:, :], sqim[:, :])

                @sq.ph("scalar")
                def _(s, ct=ct):
                    s.activation(out=L2t[:, :], in_=m2t[:, :], func=AF.Ln, bias=cst[:, 1:2])
                    s.activation(out=mg2[:, :], in_=L2t[:, :], func=AF.Exp, scale=0.5)
                    s.activation(out=m2t[:, :], in_=mg2[:, :], func=AF.Exp, scale=2.0)
                    return s.activation(out=rm2[:, :], in_=L2t[:, :], func=AF.Exp, scale=-0.5)

                @sq.ph("vector")
                def _(v, ct=ct, ssl=ssl):
                    v.tensor_scalar_add(sqre[:, :], m2t[:, :], 1.0)
                    v.reciprocal(sqim[:, :], sqre[:, :])
                    v.tensor_scalar(th[:, :], sqim[:, :], -2.0, 1.0, ALU.mult, ALU.add)
                    v.tensor_mul(F2t[:, :], th[:, :], rm2[:, :])
                    v.tensor_mul(hre[:, :], F2t[:, :], hre[:, :])
                    v.tensor_mul(him[:, :], F2t[:, :], him[:, :])
                    v.tensor_scalar(yt1[:, :], him[:, :], cic[:, ct:ct + 1], None, ALU.mult)
                    return v.scalar_tensor_tensor(out=yb[:, ssl], in0=hre[:, :],
                                                  scalar=crc[:, ct:ct + 1], in1=yt1[:, :],
                                                  op0=ALU.mult, op1=ALU.add)

            for half in range(CH // 2048):
                @sq.ph("tensor")
                def _(t, half=half):
                    for j in range(4):
                        sl = slice(half * 2048 + j * 512, half * 2048 + j * 512 + 512)
                        last = t.matmul(psB[0:32, j * 512:(j + 1) * 512], en32b[:, :],
                                        yb[:, sl], start=True, stop=True)
                    return last

                @sq.ph("scalar")
                def _(s, half=half):
                    return s.copy(out=yred[0:32, half * 2048:(half + 1) * 2048], in_=psB[0:32, 0:2048])

            @sq.dma("sync")
            def _(q, t0=t0):
                return [q.dma_start(out=o_y[:, t0 * 32:(t0 + chunk) * 32], in_=yred[0:32, 0:CH])]

        @sq.dma("sync")
        def _(q):
            return [q.dma_start(out=o_hre[:, :], in_=hre[:, :]),
                    q.dma_start(out=o_him[:, :], in_=him[:, :]),
                    q.dma_start(out=o_th[:, :], in_=th[:, :])]

        sq.build(nc)
    return nc



def build_a3():
    T = TSH
    nc = new_nc()
    x_fm = nc.dram_tensor("x_fm", [D, T], F32, kind="ExternalInput").ap()
    siluz = nc.dram_tensor("siluz", [D, T], F32, kind="ExternalInput").ap()
    yps = [nc.dram_tensor(f"yp{i}", [D, T], F32, kind="ExternalInput").ap() for i in range(4)]
    w_out = nc.dram_tensor("w_out", [1025, D], F32, kind="ExternalInput").ap()
    o_x = nc.dram_tensor("o_x", [D, T], F32, kind="ExternalOutput").ap()

    sq = Seq()
    with contextlib.ExitStack() as cx:
        E = cx.enter_context
        A = E(nc.sbuf_tensor("A", [128, NCH * T], F32))
        Yg = E(nc.sbuf_tensor("Yg", [128, NCH * T], BF16))
        wob = E(nc.sbuf_tensor("wob", [128, NCH * D], BF16))
        wbias = E(nc.sbuf_tensor("wbias", [1, D], BF16))
        onesr = E(nc.sbuf_tensor("onesr", [1, 512], BF16))
        scr = E(nc.sbuf_tensor("scr", [128, 2048], F32))
        sc2 = E(nc.sbuf_tensor("sc2", [128, 2048], F32))
        sc3 = E(nc.sbuf_tensor("sc3", [128, 2048], F32))
        psM = E(nc.psum_tensor([128, 2048], F32))

        @sq.dma("sync")
        def _(q):
            return [q.dma_start(out=A[:, k * T:(k + 1) * T], in_=x_fm[k * 128:(k + 1) * 128, :])
                    for k in range(NCH)]

        @sq.ph("vector")
        def _(v):
            return v.memset(onesr[:, :], 1.0)

        for k in range(NCH):
            @sq.dma("sync")
            def _(q, k=k):
                return [q.dma_start(out=scr[:, 0:1024], in_=w_out[k * 128:(k + 1) * 128, :])]

            @sq.ph("vector")
            def _(v, k=k):
                return v.tensor_copy(wob[:, k * D:(k + 1) * D], scr[:, 0:1024])

        @sq.dma("sync")
        def _(q):
            return [q.dma_start(out=scr[0:1, 0:1024], in_=w_out[1024:1025, :])]

        @sq.ph("vector")
        def _(v):
            return v.tensor_copy(wbias[:, :], scr[0:1, 0:1024])

        for k in range(NCH):
            @sq.dma("sync")
            def _(q, k=k):
                return [q.dma_start(out=scr[:, :], in_=yps[0][k * 128:(k + 1) * 128, :]),
                        q.dma_start(out=sc2[:, :], in_=yps[1][k * 128:(k + 1) * 128, :])]

            @sq.ph("vector")
            def _(v, k=k):
                return v.tensor_add(sc3[:, :], scr[:, :], sc2[:, :])

            @sq.dma("sync")
            def _(q, k=k):
                return [q.dma_start(out=scr[:, :], in_=yps[2][k * 128:(k + 1) * 128, :]),
                        q.dma_start(out=sc2[:, :], in_=yps[3][k * 128:(k + 1) * 128, :])]

            @sq.ph("vector")
            def _(v, k=k):
                v.tensor_add(sc3[:, :], sc3[:, :], scr[:, :])
                return v.tensor_add(sc3[:, :], sc3[:, :], sc2[:, :])

            @sq.dma("sync")
            def _(q, k=k):
                return [q.dma_start(out=scr[:, :], in_=siluz[k * 128:(k + 1) * 128, :])]

            @sq.ph("vector")
            def _(v, k=k):
                return v.tensor_mul(Yg[:, k * T:(k + 1) * T], sc3[:, :], scr[:, :])

        for o in range(NCH):
            @sq.ph("tensor")
            def _(t, o=o):
                for tc in range(4):
                    for k in range(NCH):
                        t.matmul(psM[:, tc * 512:tc * 512 + 512],
                                 wob[:, k * D + o * 128:k * D + o * 128 + 128],
                                 Yg[:, k * T + tc * 512:k * T + tc * 512 + 512],
                                 start=(k == 0), stop=False)
                    last = t.matmul(psM[:, tc * 512:tc * 512 + 512],
                                    wbias[0:1, o * 128:(o + 1) * 128],
                                    onesr[0:1, :], start=False, stop=True)
                return last

            @sq.ph("vector")
            def _(v, o=o):
                return v.tensor_add(A[:, o * T:(o + 1) * T], A[:, o * T:(o + 1) * T], psM[:, 0:T])

        @sq.dma("sync")
        def _(q):
            return [q.dma_start(out=o_x[k * 128:(k + 1) * 128, :], in_=A[:, k * T:(k + 1) * T])
                    for k in range(NCH)]

        sq.build(nc)
    return nc



def build_f1():
    T = TSH
    nc = new_nc()
    x_fm = nc.dram_tensor("x_fm", [D, T], F32, kind="ExternalInput").ap()
    w1t = nc.dram_tensor("w1t", [D, 512], F32, kind="ExternalInput").ap()
    w1bot = nc.dram_tensor("w1bot", [D, 512], F32, kind="ExternalInput").ap()
    b1r = nc.dram_tensor("b1r", [1, 512], F32, kind="ExternalInput").ap()
    w2 = nc.dram_tensor("w2", [512, 64], F32, kind="ExternalInput").ap()
    slot_init = nc.dram_tensor("slot_init", [64, D], F32, kind="ExternalInput").ap()
    taub = nc.dram_tensor("taub", [128, 1], F32, kind="ExternalInput").ap()
    b2r = nc.dram_tensor("b2r", [1, 64], F32, kind="ExternalInput").ap()
    ident = nc.dram_tensor("ident", [128, 128], F32, kind="ExternalInput").ap()
    o_alpha = nc.dram_tensor("o_alpha", [T, 64], F32, kind="ExternalOutput").ap()
    o_spart = nc.dram_tensor("o_spart", [64, 1025], F32, kind="ExternalOutput").ap()

    sq = Seq()
    with contextlib.ExitStack() as cx:
        E = cx.enter_context
        A = E(nc.sbuf_tensor("A", [128, NCH * T], F32))
        Bt = E(nc.sbuf_tensor("Bt", [128, NCH * T], BF16))
        hb = E(nc.sbuf_tensor("hb", [128, 4 * T], BF16))
        w1b = E(nc.sbuf_tensor("w1b", [128, NCH * 512], BF16))
        w2b = E(nc.sbuf_tensor("w2b", [128, 4 * 64], BF16))
        sib = E(nc.sbuf_tensor("sib", [64, D], F32))
        idb = E(nc.sbuf_tensor("idb", [128, 128], F32))
        hcol = E(nc.sbuf_tensor("hcol", [128, 8], F32))
        xtm = E(nc.sbuf_tensor("xtm", [128, 1025], BF16))
        alb = E(nc.sbuf_tensor("alb", [128, 64], BF16))
        scr = E(nc.sbuf_tensor("scr", [128, 2048], F32))
        lg0 = E(nc.sbuf_tensor("lg0", [128, 64], F32))
        lgw = E(nc.sbuf_tensor("lgw", [128, 64], F32))
        ecol = E(nc.sbuf_tensor("ecol", [128, 16], F32))
        ee = E(nc.sbuf_tensor("ee", [128, 64], F32))
        msk = E(nc.sbuf_tensor("msk", [128, 64], F32))
        alphaf = E(nc.sbuf_tensor("alphaf", [128, 64], F32))
        smrow = E(nc.sbuf_tensor("smrow", [1, D], F32))
        smcol = E(nc.sbuf_tensor("smcol", [128, NCH], F32))
        ones_col = E(nc.sbuf_tensor("ones_col", [128, 1], F32))
        ones_row = E(nc.sbuf_tensor("ones_row", [1, 128], F32))
        b2s = E(nc.sbuf_tensor("b2s", [1, 64], F32))
        psA = E(nc.psum_tensor([1, 512], F32))
        psR = E(nc.psum_tensor([128, 1024], F32))
        psM = E(nc.psum_tensor([128, 2048], F32))
        psT = E(nc.psum_tensor([128, 128], F32))

        @sq.dma("sync")
        def _(q):
            ins = [q.dma_start(out=A[:, k * T:(k + 1) * T], in_=x_fm[k * 128:(k + 1) * 128, :])
                   for k in range(NCH)]
            ins += [q.dma_start(out=sib[:, :], in_=slot_init[:, :]),
                    q.dma_start(out=idb[:, :], in_=ident[:, :]),
                    q.dma_start(out=ecol[:, 1:2], in_=taub[:, :]),
                    q.dma_start(out=b2s[:, :], in_=b2r[:, :])]
            return ins

        @sq.ph("vector")
        def _(v):
            v.memset(ones_col[:, :], 1.0)
            v.memset(ones_row[:, :], 1.0)
            v.memset(xtm[:, 1024:1025], 1.0)
            return v.memset(scr[:, :], 0.0)

        @sq.ph("scalar")
        def _(s):
            return s.activation(out=ecol[:, 2:3], in_=ecol[:, 1:2], func=AF.Abs)

        @sq.ph("vector")
        def _(v):
            v.tensor_scalar_add(ecol[:, 3:4], ecol[:, 2:3], 0.1)
            return v.reciprocal(ecol[:, 1:2], ecol[:, 3:4])

        h = dict(xs=A, xln=Bt, scr=scr,
                 row_su=lg0[0:1, 0:64].tensor[0:1, 0:64] if False else None,
                 psA=psA, psR=psR, ones_col=ones_col, ones_row=ones_row)
        rowsu = E(nc.sbuf_tensor("rowsu", [1, T], F32))
        rowva = E(nc.sbuf_tensor("rowva", [1, T], F32))
        h["row_su"] = rowsu
        h["row_va"] = rowva
        h["xln2"] = A
        add_ln_phases(sq, h, T)

        @sq.dma("sync")
        def _(q):
            return [q.dma_start(out=scr[:, 0:512].rearrange("p (c n) -> p c n", c=1),
                                in_=w2.rearrange("(c p) n -> p c n", p=128)[:, 0:1, :]) if False else
                    q.dma_start(out=scr[:, 0:256].rearrange("p (c n) -> p c n", c=4),
                                in_=w2.rearrange("(c p) n -> p c n", p=128))]

        @sq.ph("vector")
        def _(v):
            return v.tensor_copy(w2b[:, :], scr[:, 0:256])

        for k in range(NCH):
            @sq.dma("sync")
            def _(q, k=k):
                return [q.dma_start(out=scr[:, 0:512], in_=w1t[k * 128:(k + 1) * 128, :])]

            @sq.ph("vector")
            def _(v, k=k):
                return v.tensor_copy(w1b[:, k * 512:(k + 1) * 512], scr[:, 0:512])

        @sq.ph("tensor")
        def _(t):
            for j in range(2):
                last = t.matmul(psA[0:1, j * 512:j * 512 + 512] if False else psA[0:1, 0:512],
                                ones_col[0:64, 0:1], sib[:, j * 512:(j + 1) * 512],
                                start=True, stop=True)
            return last

        sq.phases.pop()
        for j in range(2):
            @sq.ph("tensor")
            def _(t, j=j):
                return t.matmul(psA[0:1, 0:512], ones_col[0:64, 0:1],
                                sib[:, j * 512:(j + 1) * 512], start=True, stop=True)

            @sq.ph("vector")
            def _(v, j=j):
                return v.tensor_scalar(smrow[0:1, j * 512:(j + 1) * 512], psA[0:1, 0:512],
                                       1.0 / 64.0, None, ALU.mult)

        for k in range(NCH):
            @sq.ph("tensor")
            def _(t, k=k):
                t.transpose(psT[:, 0:1], smrow[0:1, k * 128:(k + 1) * 128], idb[0:1, 0:1])
                return t.matmul(psT[0:1, 2:3], ones_col[0:1, 0:1], ones_col[0:1, 0:1],
                                start=True, stop=True)

            @sq.ph("vector")
            def _(v, k=k):
                return v.tensor_copy(smcol[:, k:k + 1], psT[:, 0:1])

        for o in range(4):
            @sq.dma("sync")
            def _(q, o=o):
                return [q.dma_start(out=scr[:, 0:1024].rearrange("p (c n) -> p c n", c=8),
                                    in_=w1bot.rearrange("(c p) n -> p c n", p=128)[:, :, o * 128:(o + 1) * 128])]

            @sq.ph("vector")
            def _(v, o=o):
                return v.tensor_copy(scr[:, 1024:2048], scr[:, 0:1024])

            @sq.ph("tensor")
            def _(t, o=o):
                for k in range(NCH):
                    last = t.matmul(psT[:, 4:5], scr[:, 1024 + k * 128:1024 + (k + 1) * 128],
                                    smcol[:, k:k + 1], start=(k == 0), stop=(k == NCH - 1))
                return last

            @sq.ph("vector")
            def _(v, o=o):
                return v.tensor_copy(hcol[:, o:o + 1], psT[:, 4:5])

        @sq.dma("sync")
        def _(q):
            return [q.dma_start(out=scr[0:1, 0:512], in_=b1r[:, :])]

        for o in range(4):
            @sq.ph("tensor")
            def _(t, o=o):
                t.transpose(psT[:, 0:1], scr[0:1, o * 128:(o + 1) * 128], idb[0:1, 0:1])
                return t.matmul(psT[0:1, 2:3], ones_col[0:1, 0:1], ones_col[0:1, 0:1],
                                start=True, stop=True)

            @sq.ph("vector")
            def _(v, o=o):
                return v.tensor_add(hcol[:, o:o + 1], hcol[:, o:o + 1], psT[:, 0:1])

        for o in range(4):
            @sq.ph("tensor")
            def _(t, o=o):
                for tc in range(4):
                    for k in range(NCH):
                        last = t.matmul(psM[:, tc * 512:tc * 512 + 512],
                                        w1b[:, k * 512 + o * 128:k * 512 + o * 128 + 128],
                                        Bt[:, k * T + tc * 512:k * T + tc * 512 + 512],
                                        start=(k == 0), stop=(k == NCH - 1))
                return last

            @sq.ph("scalar")
            def _(s, o=o):
                return s.activation(out=hb[:, o * T:(o + 1) * T], in_=psM[:, 0:T],
                                    func=AF.Gelu, bias=hcol[:, o:o + 1])

        for tl in range(16):
            tsl = slice(tl * 128, (tl + 1) * 128)

            @sq.ph("tensor")
            def _(t, tl=tl, tsl=tsl):
                for k in range(4):
                    t.matmul(psT[:, 0:64], hb[:, k * T + tl * 128:k * T + tl * 128 + 128],
                             w2b[:, k * 64:(k + 1) * 64], start=(k == 0), stop=False)
                return t.matmul(psT[:, 0:64], ones_row[:, :], b2s[:, :],
                                start=False, stop=True)

            @sq.ph("vector")
            def _(v, tl=tl):
                v.tensor_copy(lg0[:, :], psT[:, 0:64])
                return v.tensor_copy(lgw[:, :], psT[:, 0:64])

            for it in range(8):
                @sq.ph("vector")
                def _(v, tl=tl, it=it):
                    src = lgw if it % 2 == 0 else ee
                    dst = ee if it % 2 == 0 else lgw
                    mx = ecol[:, 4:5] if it == 0 else ecol[:, 5:6]
                    v.reduce_max(mx, src[:, :], AX.X)
                    v.tensor_scalar(msk[:, :], src[:, :], mx, None, ALU.is_ge)
                    return v.scalar_tensor_tensor(out=dst[:, :], in0=msk[:, :], scalar=-1e30,
                                                  in1=src[:, :], op0=ALU.mult, op1=ALU.add)

            @sq.ph("vector")
            def _(v, tl=tl):
                v.tensor_mul(ecol[:, 6:7], ecol[:, 4:5], ecol[:, 1:2])
                return v.tensor_scalar(ecol[:, 7:8], ecol[:, 6:7], -1.0, None, ALU.mult)

            @sq.ph("scalar")
            def _(s, tl=tl):
                return s.activation(out=ee[:, :], in_=lg0[:, :], func=AF.Exp,
                                    scale=ecol[:, 1:2], bias=ecol[:, 7:8])

            @sq.ph("vector")
            def _(v, tl=tl, tsl=tsl):
                v.tensor_scalar(msk[:, :], lgw[:, :], -1e29, None, ALU.is_lt)
                v.tensor_mul(alphaf[:, :], ee[:, :], msk[:, :])
                v.reduce_sum(ecol[:, 8:9], alphaf[:, :], AX.X)
                v.reciprocal(ecol[:, 9:10], ecol[:, 8:9])
                v.tensor_scalar(ee[:, :], alphaf[:, :], ecol[:, 9:10], None, ALU.mult)
                v.tensor_copy(alphaf[:, :], ee[:, :])
                return v.tensor_copy(alb[:, :], ee[:, :])

            @sq.dma("sync")
            def _(q, tl=tl, tsl=tsl):
                return [q.dma_start(out=o_alpha[tsl, :], in_=alphaf[:, :])]

            for k in range(NCH):
                @sq.ph("tensor")
                def _(t, tl=tl, k=k):
                    t.transpose(psT[:, 0:128], A[:, k * T + tl * 128:k * T + tl * 128 + 128], idb[:, :])
                    return t.matmul(psT[0:1, 2:3], ones_col[0:1, 0:1], ones_col[0:1, 0:1],
                                    start=True, stop=True)

                @sq.ph("vector")
                def _(v, tl=tl, k=k):
                    return v.tensor_copy(xtm[:, k * 128:(k + 1) * 128], psT[:, 0:128])

            @sq.ph("tensor")
            def _(t, tl=tl):
                t.matmul(psM[0:64, 0:512], alb[:, :], xtm[:, 0:512], start=(tl == 0), stop=(tl == 15))
                t.matmul(psM[0:64, 512:1024], alb[:, :], xtm[:, 512:1024], start=(tl == 0), stop=(tl == 15))
                return t.matmul(psM[0:64, 1024:1025], alb[:, :], xtm[:, 1024:1025],
                                start=(tl == 0), stop=(tl == 15))

        @sq.ph("vector")
        def _(v):
            v.tensor_copy(scr[0:64, 0:1024], psM[0:64, 0:1024])
            return v.tensor_copy(scr[0:64, 1024:1025], psM[0:64, 1024:1025])

        @sq.dma("sync")
        def _(q):
            return [q.dma_start(out=o_spart[:, :], in_=scr[0:64, 0:1025])]

        sq.build(nc)
    return nc



def build_f2():
    nc = new_nc()
    sps = [nc.dram_tensor(f"sp{i}", [64, 1025], F32, kind="ExternalInput").ap() for i in range(4)]
    slot_init = nc.dram_tensor("slot_init", [64, D], F32, kind="ExternalInput").ap()
    wihT = nc.dram_tensor("wihT", [1025, 3072], F32, kind="ExternalInput").ap()
    whhT = nc.dram_tensor("whhT", [1025, 3072], F32, kind="ExternalInput").ap()
    wq = nc.dram_tensor("wq", [D, 256], F32, kind="ExternalInput").ap()
    wk = nc.dram_tensor("wk", [D, 256], F32, kind="ExternalInput").ap()
    wv = nc.dram_tensor("wv", [D, D], F32, kind="ExternalInput").ap()
    wvp = nc.dram_tensor("wvp", [D, D], F32, kind="ExternalInput").ap()
    bq = nc.dram_tensor("bq", [256, 1], F32, kind="ExternalInput").ap()
    bk = nc.dram_tensor("bk", [256, 1], F32, kind="ExternalInput").ap()
    bv = nc.dram_tensor("bv", [D, 1], F32, kind="ExternalInput").ap()
    bvp = nc.dram_tensor("bvp", [1, D], F32, kind="ExternalInput").ap()
    lng = nc.dram_tensor("lng", [64, D], F32, kind="ExternalInput").ap()
    lnb = nc.dram_tensor("lnb", [64, D], F32, kind="ExternalInput").ap()
    ident = nc.dram_tensor("ident", [128, 128], F32, kind="ExternalInput").ap()
    o_sv = nc.dram_tensor("o_sv", [64, D], F32, kind="ExternalOutput").ap()

    sq = Seq()
    with contextlib.ExitStack() as cx:
        E = cx.enter_context
        spt = [E(nc.sbuf_tensor(f"spt{i}", [64, 1025], F32)) for i in range(4)]
        S0 = E(nc.sbuf_tensor("S0", [64, D], F32))
        si = E(nc.sbuf_tensor("si", [64, D], F32))
        gi = E(nc.sbuf_tensor("gi", [64, 3072], F32))
        gh = E(nc.sbuf_tensor("gh", [64, 3072], F32))
        wbig = E(nc.sbuf_tensor("wbig", [128, NCH * 3072], BF16))
        brow = E(nc.sbuf_tensor("brow", [1, 3072], BF16))
        wqb = E(nc.sbuf_tensor("wqb", [128, NCH * 256], BF16))
        wkb = E(nc.sbuf_tensor("wkb", [128, NCH * 256], BF16))
        wvb = E(nc.sbuf_tensor("wvb", [128, NCH * D], BF16))
        sT = E(nc.sbuf_tensor("sT", [128, NCH * 64], BF16))
        qf = E(nc.sbuf_tensor("qf", [128, 2 * 64], BF16))
        kf = E(nc.sbuf_tensor("kf", [128, 2 * 64], BF16))
        vf = E(nc.sbuf_tensor("vf", [128, NCH * 64], F32))
        vtm = E(nc.sbuf_tensor("vtm", [64, NCH * 128], BF16))
        Pb = E(nc.sbuf_tensor("Pb", [64, 64], F32))
        PTb = E(nc.sbuf_tensor("PTb", [64, 64], BF16))
        yy = E(nc.sbuf_tensor("yy", [64, D], F32))
        lngt = E(nc.sbuf_tensor("lngt", [64, D], F32))
        lnbt = E(nc.sbuf_tensor("lnbt", [64, D], F32))
        colz = E(nc.sbuf_tensor("colz", [64, 16], F32))
        bcol = E(nc.sbuf_tensor("bcol", [128, 12], F32))
        scr = E(nc.sbuf_tensor("scr", [128, 3072], F32))
        ones_col = E(nc.sbuf_tensor("ones_col", [128, 1], F32))
        ones_rowb = E(nc.sbuf_tensor("ones_rowb", [1, 64], BF16))
        idb = E(nc.sbuf_tensor("idb", [128, 128], F32))
        psM = E(nc.psum_tensor([128, 1024], F32))
        psT = E(nc.psum_tensor([128, 144], F32))

        @sq.dma("sync")
        def _(q):
            ins = [q.dma_start(out=spt[i][:, :], in_=sps[i][:, :]) for i in range(4)]
            ins += [q.dma_start(out=S0[:, :], in_=slot_init[:, :]),
                    q.dma_start(out=lngt[:, :], in_=lng[:, :]),
                    q.dma_start(out=lnbt[:, :], in_=lnb[:, :]),
                    q.dma_start(out=idb[:, :], in_=ident[:, :])]
            return ins

        @sq.ph("vector")
        def _(v):
            v.memset(ones_col[:, :], 1.0)
            return v.memset(ones_rowb[:, :], 1.0)

        @sq.ph("vector")
        def _(v):
            v.tensor_add(spt[0][:, :], spt[0][:, :], spt[1][:, :])
            v.tensor_add(spt[2][:, :], spt[2][:, :], spt[3][:, :])
            v.tensor_add(spt[0][:, :], spt[0][:, :], spt[2][:, :])
            v.tensor_scalar_add(colz[:, 0:1], spt[0][:, 1024:1025], 1e-8)
            v.reciprocal(colz[:, 1:2], colz[:, 0:1])
            return v.tensor_scalar(si[:, :], spt[0][:, 0:1024], colz[:, 1:2], None, ALU.mult)

        def add_transpose_to(sq, src, dst, nchunks, width=64, cast=True):
            for k in range(nchunks):
                @sq.ph("tensor")
                def _(t, k=k):
                    t.transpose(psT[:, 0:width], src[0:width, k * 128:(k + 1) * 128], idb[0:width, 0:width])
                    return t.matmul(psT[0:1, 128:129], ones_col[0:1, 0:1], ones_col[0:1, 0:1],
                                    start=True, stop=True)

                @sq.ph("vector")
                def _(v, k=k):
                    return v.tensor_copy(dst[:, k * width:(k + 1) * width], psT[:, 0:width])

        add_transpose_to(sq, si, sT, NCH)

        for k in range(NCH):
            @sq.dma("sync")
            def _(q, k=k):
                return [q.dma_start(out=scr[:, :], in_=wihT[k * 128:(k + 1) * 128, :])]

            @sq.ph("vector")
            def _(v, k=k):
                return v.tensor_copy(wbig[:, k * 3072:(k + 1) * 3072], scr[:, :])

        @sq.dma("sync")
        def _(q):
            return [q.dma_start(out=scr[0:1, 0:3072], in_=wihT[1024:1025, :])]

        @sq.ph("vector")
        def _(v):
            return v.tensor_copy(brow[:, :], scr[0:1, 0:3072])

        def add_gru_matmul(sq, out_t):
            for sl in range(6):
                @sq.ph("tensor")
                def _(t, sl=sl):
                    for k in range(NCH):
                        t.matmul(psM[0:64, 0:512], sT[:, k * 64:(k + 1) * 64],
                                 wbig[:, k * 3072 + sl * 512:k * 3072 + sl * 512 + 512],
                                 start=(k == 0), stop=False)
                    return t.matmul(psM[0:64, 0:512], ones_rowb[0:1, :],
                                    brow[0:1, sl * 512:(sl + 1) * 512], start=False, stop=True)

                @sq.ph("vector")
                def _(v, sl=sl):
                    return v.tensor_copy(out_t[:, sl * 512:(sl + 1) * 512], psM[0:64, 0:512])

        add_gru_matmul(sq, gi)

        add_transpose_to(sq, S0, sT, NCH)

        for k in range(NCH):
            @sq.dma("sync")
            def _(q, k=k):
                return [q.dma_start(out=scr[:, :], in_=whhT[k * 128:(k + 1) * 128, :])]

            @sq.ph("vector")
            def _(v, k=k):
                return v.tensor_copy(wbig[:, k * 3072:(k + 1) * 3072], scr[:, :])

        @sq.dma("sync")
        def _(q):
            return [q.dma_start(out=scr[0:1, 0:3072], in_=whhT[1024:1025, :])]

        @sq.ph("vector")
        def _(v):
            return v.tensor_copy(brow[:, :], scr[0:1, 0:3072])

        add_gru_matmul(sq, gh)

        @sq.ph("vector")
        def _(v):
            v.tensor_add(scr[0:64, 0:1024], gi[:, 0:1024], gh[:, 0:1024])
            return v.tensor_add(scr[0:64, 1024:2048], gi[:, 1024:2048], gh[:, 1024:2048])

        @sq.ph("scalar")
        def _(s):
            s.activation(out=scr[0:64, 0:1024], in_=scr[0:64, 0:1024], func=AF.Sigmoid)
            return s.activation(out=scr[0:64, 1024:2048], in_=scr[0:64, 1024:2048], func=AF.Sigmoid)

        @sq.ph("vector")
        def _(v):
            v.tensor_mul(scr[0:64, 2048:3072], scr[0:64, 0:1024], gh[:, 2048:3072])
            return v.tensor_add(scr[0:64, 2048:3072], gi[:, 2048:3072], scr[0:64, 2048:3072])

        @sq.ph("scalar")
        def _(s):
            return s.activation(out=scr[0:64, 2048:3072], in_=scr[0:64, 2048:3072], func=AF.Tanh)

        @sq.ph("vector")
        def _(v):
            z = scr[0:64, 1024:2048]
            n = scr[0:64, 2048:3072]
            v.tensor_mul(gi[:, 0:1024], z, n)
            v.tensor_sub(gi[:, 1024:2048], n, gi[:, 0:1024])
            v.tensor_mul(gi[:, 2048:3072], z, S0[:, :])
            return v.tensor_add(si[:, :], gi[:, 1024:2048], gi[:, 2048:3072])

        add_transpose_to(sq, si, sT, NCH)

        @sq.dma("sync")
        def _(q):
            return [q.dma_start(out=scr[:, 0:2048].rearrange("p (c n) -> p c n", c=8),
                                in_=wq.rearrange("(c p) n -> p c n", p=128))]

        @sq.ph("vector")
        def _(v):
            return v.tensor_copy(wqb[:, :], scr[:, 0:2048])

        @sq.dma("sync")
        def _(q):
            return [q.dma_start(out=scr[:, 0:2048].rearrange("p (c n) -> p c n", c=8),
                                in_=wk.rearrange("(c p) n -> p c n", p=128))]

        @sq.ph("vector")
        def _(v):
            return v.tensor_copy(wkb[:, :], scr[:, 0:2048])

        for k in range(NCH):
            @sq.dma("sync")
            def _(q, k=k):
                return [q.dma_start(out=scr[:, 0:1024], in_=wv[k * 128:(k + 1) * 128, :])]

            @sq.ph("vector")
            def _(v, k=k):
                return v.tensor_copy(wvb[:, k * D:(k + 1) * D], scr[:, 0:1024])

        @sq.dma("sync")
        def _(q):
            ds = []
            for c in range(2):
                ds.append(q.dma_start(out=bcol[:, c:c + 1], in_=bq[c * 128:(c + 1) * 128, :]))
                ds.append(q.dma_start(out=bcol[:, 2 + c:3 + c], in_=bk[c * 128:(c + 1) * 128, :]))
            for c in range(NCH):
                ds.append(q.dma_start(out=bcol[:, 4 + c:5 + c], in_=bv[c * 128:(c + 1) * 128, :]))
            return ds

        for o in range(2):
            @sq.ph("tensor")
            def _(t, o=o):
                for k in range(NCH):
                    last = t.matmul(psT[:, 0:64], wqb[:, k * 256 + o * 128:k * 256 + o * 128 + 128],
                                    sT[:, k * 64:(k + 1) * 64], start=(k == 0), stop=(k == NCH - 1))
                return last

            @sq.ph("scalar")
            def _(s, o=o):
                return s.activation(out=qf[:, o * 64:(o + 1) * 64], in_=psT[:, 0:64],
                                    func=AF.Identity, bias=bcol[:, o:o + 1])

            @sq.ph("tensor")
            def _(t, o=o):
                for k in range(NCH):
                    last = t.matmul(psT[:, 0:64], wkb[:, k * 256 + o * 128:k * 256 + o * 128 + 128],
                                    sT[:, k * 64:(k + 1) * 64], start=(k == 0), stop=(k == NCH - 1))
                return last

            @sq.ph("scalar")
            def _(s, o=o):
                return s.activation(out=kf[:, o * 64:(o + 1) * 64], in_=psT[:, 0:64],
                                    func=AF.Identity, bias=bcol[:, 2 + o:3 + o])

        for o in range(NCH):
            @sq.ph("tensor")
            def _(t, o=o):
                for k in range(NCH):
                    last = t.matmul(psT[:, 0:64], wvb[:, k * D + o * 128:k * D + o * 128 + 128],
                                    sT[:, k * 64:(k + 1) * 64], start=(k == 0), stop=(k == NCH - 1))
                return last

            @sq.ph("scalar")
            def _(s, o=o):
                return s.activation(out=vf[:, o * 64:(o + 1) * 64], in_=psT[:, 0:64],
                                    func=AF.Identity, bias=bcol[:, 4 + o:5 + o])

        @sq.ph("vector")
        def _(v):
            v.tensor_copy(scr[:, 0:64], qf[:, 0:64])
            return v.tensor_copy(scr[:, 64:128], qf[:, 64:128])

        @sq.ph("tensor")
        def _(t):
            t.matmul(psT[0:64, 0:64], qf[:, 0:64], kf[:, 0:64], start=True, stop=False)
            return t.matmul(psT[0:64, 0:64], qf[:, 64:128], kf[:, 64:128], start=False, stop=True)

        @sq.ph("vector")
        def _(v):
            v.tensor_copy(Pb[:, :], psT[0:64, 0:64])
            v.reduce_max(colz[:, 2:3], Pb[:, :], AX.X)
            return v.tensor_scalar(colz[:, 3:4], colz[:, 2:3], -0.0625, None, ALU.mult)

        @sq.ph("scalar")
        def _(s):
            return s.activation(out=Pb[:, :], in_=Pb[:, :], func=AF.Exp, scale=0.0625,
                                bias=colz[:, 3:4])

        @sq.ph("vector")
        def _(v):
            v.reduce_sum(colz[:, 4:5], Pb[:, :], AX.X)
            v.reciprocal(colz[:, 5:6], colz[:, 4:5])
            v.tensor_scalar(scr[0:64, 0:64], Pb[:, :], colz[:, 5:6], None, ALU.mult)
            return v.tensor_copy(Pb[:, :], scr[0:64, 0:64])

        @sq.ph("tensor")
        def _(t):
            t.transpose(psT[0:64, 64:128], Pb[:, :], idb[0:64, 0:64])
            return t.matmul(psT[0:1, 128:129], ones_col[0:1, 0:1], ones_col[0:1, 0:1],
                            start=True, stop=True)

        @sq.ph("vector")
        def _(v):
            return v.tensor_copy(PTb[:, :], psT[0:64, 64:128])

        for k in range(NCH):
            @sq.ph("vector")
            def _(v, k=k):
                return v.tensor_copy(scr[:, 0:64], vf[:, k * 64:(k + 1) * 64])

            @sq.ph("tensor")
            def _(t, k=k):
                t.transpose(psT[0:64, 0:128], scr[:, 0:64], idb[:, :])
                return t.matmul(psT[0:1, 128:129], ones_col[0:1, 0:1], ones_col[0:1, 0:1],
                                start=True, stop=True)

            @sq.ph("vector")
            def _(v, k=k):
                return v.tensor_copy(vtm[:, k * 128:(k + 1) * 128], psT[0:64, 0:128])

        for sl in range(2):
            @sq.ph("tensor")
            def _(t, sl=sl):
                return t.matmul(psM[0:64, 0:512], PTb[:, :], vtm[:, sl * 512:(sl + 1) * 512],
                                start=True, stop=True)

            @sq.ph("vector")
            def _(v, sl=sl):
                return v.tensor_add(yy[:, sl * 512:(sl + 1) * 512], si[:, sl * 512:(sl + 1) * 512],
                                    psM[0:64, 0:512])

        @sq.ph("vector")
        def _(v):
            return v.reduce_sum(colz[:, 6:7], yy[:, :], AX.X)

        @sq.ph("scalar")
        def _(s):
            return s.activation(out=scr[0:64, 0:1024], in_=yy[:, :], func=AF.Square)

        @sq.ph("vector")
        def _(v):
            v.reduce_sum(colz[:, 7:8], scr[0:64, 0:1024], AX.X)
            v.tensor_scalar(colz[:, 6:7], colz[:, 6:7], 1.0 / D, None, ALU.mult)
            return v.tensor_scalar(colz[:, 7:8], colz[:, 7:8], 1.0 / D, None, ALU.mult)

        @sq.ph("scalar")
        def _(s):
            return s.activation(out=colz[:, 8:9], in_=colz[:, 6:7], func=AF.Square)

        @sq.ph("vector")
        def _(v):
            v.tensor_sub(colz[:, 9:10], colz[:, 7:8], colz[:, 8:9])
            return v.tensor_scalar_add(colz[:, 9:10], colz[:, 9:10], 1e-5)

        @sq.ph("scalar")
        def _(s):
            s.activation(out=colz[:, 10:11], in_=colz[:, 9:10], func=AF.Ln)
            return s.activation(out=colz[:, 11:12], in_=colz[:, 10:11], func=AF.Exp, scale=-0.5)

        @sq.ph("vector")
        def _(v):
            v.tensor_scalar(scr[0:64, 0:1024], yy[:, :], colz[:, 6:7], None, ALU.subtract)
            v.tensor_scalar(scr[0:64, 1024:2048], scr[0:64, 0:1024], colz[:, 11:12], None, ALU.mult)
            v.tensor_mul(scr[0:64, 0:1024], scr[0:64, 1024:2048], lngt[:, :])
            return v.tensor_add(yy[:, :], scr[0:64, 0:1024], lnbt[:, :])

        add_transpose_to(sq, yy, sT, NCH)

        for k in range(NCH):
            @sq.dma("sync")
            def _(q, k=k):
                return [q.dma_start(out=scr[:, 0:1024], in_=wvp[k * 128:(k + 1) * 128, :])]

            @sq.ph("vector")
            def _(v, k=k):
                return v.tensor_copy(wvb[:, k * D:(k + 1) * D], scr[:, 0:1024])

        @sq.dma("sync")
        def _(q):
            return [q.dma_start(out=scr[0:1, 0:1024], in_=bvp[:, :])]

        @sq.ph("vector")
        def _(v):
            return v.tensor_copy(brow[:, 0:1024], scr[0:1, 0:1024])

        for sl in range(2):
            @sq.ph("tensor")
            def _(t, sl=sl):
                for k in range(NCH):
                    t.matmul(psM[0:64, 0:512], sT[:, k * 64:(k + 1) * 64],
                             wvb[:, k * D + sl * 512:k * D + sl * 512 + 512],
                             start=(k == 0), stop=False)
                return t.matmul(psM[0:64, 0:512], ones_rowb[0:1, :],
                                brow[0:1, sl * 512:(sl + 1) * 512], start=False, stop=True)

            @sq.ph("vector")
            def _(v, sl=sl):
                return v.tensor_copy(yy[:, sl * 512:(sl + 1) * 512], psM[0:64, 0:512])

        @sq.dma("sync")
        def _(q):
            return [q.dma_start(out=o_sv[:, :], in_=yy[:, :])]

        sq.build(nc)
    return nc



def build_f3():
    T = TSH
    nc = new_nc()
    x_fm = nc.dram_tensor("x_fm", [D, T], F32, kind="ExternalInput").ap()
    alphaT = nc.dram_tensor("alphaT", [64, T], F32, kind="ExternalInput").ap()
    sv = nc.dram_tensor("sv", [64, D], F32, kind="ExternalInput").ap()
    wo = nc.dram_tensor("wo", [1025, D], F32, kind="ExternalInput").ap()
    o_x = nc.dram_tensor("o_x", [D, T], F32, kind="ExternalOutput").ap()

    sq = Seq()
    with contextlib.ExitStack() as cx:
        E = cx.enter_context
        A = E(nc.sbuf_tensor("A", [128, NCH * T], F32))
        U = E(nc.sbuf_tensor("U", [128, NCH * T], BF16))
        alb = E(nc.sbuf_tensor("alb", [64, T], BF16))
        svb = E(nc.sbuf_tensor("svb", [64, D], BF16))
        wob = E(nc.sbuf_tensor("wob", [128, NCH * D], BF16))
        wbias = E(nc.sbuf_tensor("wbias", [1, D], BF16))
        onesr = E(nc.sbuf_tensor("onesr", [1, 512], BF16))
        scr = E(nc.sbuf_tensor("scr", [128, 2048], F32))
        psM = E(nc.psum_tensor([128, 2048], F32))

        @sq.dma("sync")
        def _(q):
            ins = [q.dma_start(out=A[:, k * T:(k + 1) * T], in_=x_fm[k * 128:(k + 1) * 128, :])
                   for k in range(NCH)]
            ins.append(q.dma_start(out=scr[0:64, 0:T], in_=alphaT[:, :]))
            return ins

        @sq.ph("vector")
        def _(v):
            v.memset(onesr[:, :], 1.0)
            return v.tensor_copy(alb[:, :], scr[0:64, 0:T])

        @sq.dma("sync")
        def _(q):
            return [q.dma_start(out=scr[0:64, 0:1024], in_=sv[:, :])]

        @sq.ph("vector")
        def _(v):
            return v.tensor_copy(svb[:, :], scr[0:64, 0:1024])

        for o in range(NCH):
            @sq.ph("tensor")
            def _(t, o=o):
                for tc in range(4):
                    last = t.matmul(psM[:, tc * 512:tc * 512 + 512],
                                    svb[:, o * 128:(o + 1) * 128],
                                    alb[:, tc * 512:tc * 512 + 512], start=True, stop=True)
                return last

            @sq.ph("vector")
            def _(v, o=o):
                return v.tensor_copy(U[:, o * T:(o + 1) * T], psM[:, 0:T])

        for k in range(NCH):
            @sq.dma("sync")
            def _(q, k=k):
                return [q.dma_start(out=scr[:, 0:1024], in_=wo[k * 128:(k + 1) * 128, :])]

            @sq.ph("vector")
            def _(v, k=k):
                return v.tensor_copy(wob[:, k * D:(k + 1) * D], scr[:, 0:1024])

        @sq.dma("sync")
        def _(q):
            return [q.dma_start(out=scr[0:1, 0:1024], in_=wo[1024:1025, :])]

        @sq.ph("vector")
        def _(v):
            return v.tensor_copy(wbias[:, :], scr[0:1, 0:1024])

        for o in range(NCH):
            @sq.ph("tensor")
            def _(t, o=o):
                for tc in range(4):
                    for k in range(NCH):
                        t.matmul(psM[:, tc * 512:tc * 512 + 512],
                                 wob[:, k * D + o * 128:k * D + o * 128 + 128],
                                 U[:, k * T + tc * 512:k * T + tc * 512 + 512],
                                 start=(k == 0), stop=False)
                    last = t.matmul(psM[:, tc * 512:tc * 512 + 512],
                                    wbias[0:1, o * 128:(o + 1) * 128],
                                    onesr[0:1, :], start=False, stop=True)
                return last

            @sq.ph("vector")
            def _(v, o=o):
                return v.tensor_add(A[:, o * T:(o + 1) * T], A[:, o * T:(o + 1) * T], psM[:, 0:T])

        @sq.dma("sync")
        def _(q):
            return [q.dma_start(out=o_x[k * 128:(k + 1) * 128, :], in_=A[:, k * T:(k + 1) * T])
                    for k in range(NCH)]

        sq.build(nc)
    return nc



def build_lnonly():
    T = TSH
    nc = new_nc()
    x_fm = nc.dram_tensor("x_fm", [D, T], F32, kind="ExternalInput").ap()
    o_xln = nc.dram_tensor("o_xln", [D, T], BF16, kind="ExternalOutput").ap()
    sq = Seq()
    with contextlib.ExitStack() as cx:
        E = cx.enter_context
        A = E(nc.sbuf_tensor("A", [128, NCH * T], F32))
        Bt = E(nc.sbuf_tensor("Bt", [128, NCH * T], BF16))
        scr = E(nc.sbuf_tensor("scr", [128, 2048], F32))
        rowsu = E(nc.sbuf_tensor("rowsu", [1, T], F32))
        rowva = E(nc.sbuf_tensor("rowva", [1, T], F32))
        ones_col = E(nc.sbuf_tensor("ones_col", [128, 1], F32))
        ones_row = E(nc.sbuf_tensor("ones_row", [1, 128], F32))
        psA = E(nc.psum_tensor([1, 512], F32))
        psR = E(nc.psum_tensor([128, 1024], F32))

        @sq.dma("sync")
        def _(q):
            return [q.dma_start(out=A[:, k * T:(k + 1) * T], in_=x_fm[k * 128:(k + 1) * 128, :])
                    for k in range(NCH)]

        @sq.ph("vector")
        def _(v):
            v.memset(ones_col[:, :], 1.0)
            return v.memset(ones_row[:, :], 1.0)

        h = dict(xs=A, xln=Bt, scr=scr, row_su=rowsu, row_va=rowva,
                 psA=psA, psR=psR, ones_col=ones_col, ones_row=ones_row)
        add_ln_phases(sq, h, T)

        @sq.dma("sync")
        def _(q):
            return [q.dma_start(out=o_xln[k * 128:(k + 1) * 128, :], in_=Bt[:, k * T:(k + 1) * T])
                    for k in range(NCH)]

        sq.build(nc)
    return nc



def build_c1():
    T = TSH
    nc = new_nc()
    x_fm = nc.dram_tensor("x_fm", [D, T], F32, kind="ExternalInput").ap()
    w_in = nc.dram_tensor("w_in", [1025, 2 * D], F32, kind="ExternalInput").ap()
    o_q = nc.dram_tensor("o_q", [D, T], BF16, kind="ExternalOutput").ap()
    o_v = nc.dram_tensor("o_v", [D, T], F32, kind="ExternalOutput").ap()

    sq = Seq()
    with contextlib.ExitStack() as cx:
        E = cx.enter_context
        A = E(nc.sbuf_tensor("A", [128, NCH * T], F32))
        Bt = E(nc.sbuf_tensor("Bt", [128, NCH * T], BF16))
        wb = E(nc.sbuf_tensor("wb", [128, NCH * 2048], BF16))
        wbias = E(nc.sbuf_tensor("wbias", [1, 2048], BF16))
        onesr = E(nc.sbuf_tensor("onesr", [1, 512], BF16))
        scr = E(nc.sbuf_tensor("scr", [128, 2048], F32))
        rowsu = E(nc.sbuf_tensor("rowsu", [1, T], F32))
        rowva = E(nc.sbuf_tensor("rowva", [1, T], F32))
        qtmp = E(nc.sbuf_tensor("qtmp", [128, T], BF16))
        ones_col = E(nc.sbuf_tensor("ones_col", [128, 1], F32))
        ones_row = E(nc.sbuf_tensor("ones_row", [1, 128], F32))
        psA = E(nc.psum_tensor([1, 512], F32))
        psR = E(nc.psum_tensor([128, 1024], F32))
        psM = E(nc.psum_tensor([128, 2048], F32))

        @sq.dma("sync")
        def _(q):
            return [q.dma_start(out=A[:, k * T:(k + 1) * T], in_=x_fm[k * 128:(k + 1) * 128, :])
                    for k in range(NCH)]

        @sq.ph("vector")
        def _(v):
            v.memset(ones_col[:, :], 1.0)
            v.memset(ones_row[:, :], 1.0)
            return v.memset(onesr[:, :], 1.0)

        h = dict(xs=A, xln=Bt, scr=scr, row_su=rowsu, row_va=rowva,
                 psA=psA, psR=psR, ones_col=ones_col, ones_row=ones_row)
        add_ln_phases(sq, h, T)

        for k in range(NCH):
            @sq.dma("sync")
            def _(q, k=k):
                return [q.dma_start(out=scr[:, :], in_=w_in[k * 128:(k + 1) * 128, :])]

            @sq.ph("vector")
            def _(v, k=k):
                return v.tensor_copy(wb[:, k * 2048:(k + 1) * 2048], scr[:, :])

        @sq.dma("sync")
        def _(q):
            return [q.dma_start(out=scr[0:1, 0:2048], in_=w_in[1024:1025, :])]

        @sq.ph("vector")
        def _(v):
            return v.tensor_copy(wbias[:, :], scr[0:1, 0:2048])

        for o in range(16):
            @sq.ph("tensor")
            def _(t, o=o):
                for tc in range(4):
                    for k in range(NCH):
                        t.matmul(psM[:, tc * 512:tc * 512 + 512],
                                 wb[:, k * 2048 + o * 128:k * 2048 + o * 128 + 128],
                                 Bt[:, k * T + tc * 512:k * T + tc * 512 + 512],
                                 start=(k == 0), stop=False)
                    last = t.matmul(psM[:, tc * 512:tc * 512 + 512],
                                    wbias[0:1, o * 128:(o + 1) * 128],
                                    onesr[0:1, :], start=False, stop=True)
                return last

            if o < 8:
                @sq.ph("vector")
                def _(v, o=o):
                    return v.tensor_copy(qtmp[:, :], psM[:, 0:T])

                @sq.dma("sync")
                def _(q, o=o):
                    return [q.dma_start(out=o_q[o * 128:(o + 1) * 128, :], in_=qtmp[:, :])]
            else:
                @sq.ph("vector")
                def _(v, o=o):
                    return v.tensor_copy(scr[:, 0:T], psM[:, 0:T])

                @sq.dma("sync")
                def _(q, o=o):
                    return [q.dma_start(out=o_v[(o - 8) * 128:(o - 7) * 128, :], in_=scr[:, 0:T])]

        sq.build(nc)
    return nc



def build_c2(variant):
    T = TSH
    HB = 2047
    W = HB + T
    nc = new_nc()
    qext = nc.dram_tensor("qext", [D, W], BF16, kind="ExternalInput").ap()
    wf = nc.dram_tensor("wf", [2049, 64], F32, kind="ExternalInput").ap()
    wout = nc.dram_tensor("wout", [129, D], F32, kind="ExternalInput").ap()
    o_q = nc.dram_tensor("o_q", [D, T], BF16, kind="ExternalOutput").ap()

    sq = Seq()
    with contextlib.ExitStack() as cx:
        E = cx.enter_context
        QA = E(nc.sbuf_tensor("QA", [128, NCH * W], BF16))
        QB = E(nc.sbuf_tensor("QB", [128, NCH * W], BF16))
        wfb = E(nc.sbuf_tensor("wfb", [128, 16 * 64], BF16))
        wfbias = E(nc.sbuf_tensor("wfbias", [1, 64], BF16))
        wob = E(nc.sbuf_tensor("wob", [128, NCH * 128], BF16))
        wobias = E(nc.sbuf_tensor("wobias", [1, D], BF16))
        onesr = E(nc.sbuf_tensor("onesr", [1, 512], BF16))
        sct = E(nc.sbuf_tensor("sct", [128, 512], BF16))
        fr = E(nc.sbuf_tensor("fr", [64, 512], F32))
        f2 = E(nc.sbuf_tensor("f2", [64, 512], F32))
        f3 = E(nc.sbuf_tensor("f3", [64, 512], F32))
        qraw = E(nc.sbuf_tensor("qraw", [128, NCH * 512], F32))
        qabs = E(nc.sbuf_tensor("qabs", [128, NCH * 512], F32))
        qm = E(nc.sbuf_tensor("qm", [128, 512], F32))
        rcol = E(nc.sbuf_tensor("rcol", [128, 8], F32))
        rrow = E(nc.sbuf_tensor("rrow", [1, 512], F32))
        idb = E(nc.sbuf_tensor("idb", [128, 128], F32))
        ident_in = nc.dram_tensor("ident", [128, 128], F32, kind="ExternalInput").ap()
        ones_col = E(nc.sbuf_tensor("ones_col", [128, 1], F32))
        ones_row = E(nc.sbuf_tensor("ones_row", [1, 128], F32))
        scr = E(nc.sbuf_tensor("scr", [128, 2048], F32))
        psF = E(nc.psum_tensor([64, 512], F32))
        psM = E(nc.psum_tensor([128, 512], F32))
        psT = E(nc.psum_tensor([128, 384], F32))
        psR = E(nc.psum_tensor([128, 512], F32))

        @sq.dma("sync")
        def _(q):
            ins = [q.dma_start(out=QA[:, k * W:(k + 1) * W], in_=qext[k * 128:(k + 1) * 128, :])
                   for k in range(NCH)]
            ins.append(q.dma_start(out=idb[:, :], in_=ident_in[:, :]))
            ins.append(q.dma_start(out=scr[:, 0:1024].rearrange("p (c n) -> p c n", c=16),
                                   in_=wf[0:2048, :].rearrange("(c p) n -> p c n", p=128)))
            ins.append(q.dma_start(out=scr[0:1, 1024:1088], in_=wf[2048:2049, :]))
            return ins

        @sq.ph("vector")
        def _(v):
            v.memset(ones_col[:, :], 1.0)
            v.memset(ones_row[:, :], 1.0)
            v.memset(onesr[:, :], 1.0)
            v.tensor_copy(wfb[:, :], scr[:, 0:1024])
            return v.tensor_copy(wfbias[:, :], scr[0:1, 1024:1088])

        @sq.dma("sync")
        def _(q):
            return [q.dma_start(out=scr[:, 0:1024], in_=wout[0:128, :]),
                    q.dma_start(out=scr[0:1, 1024:2048], in_=wout[128:129, :])]

        @sq.ph("vector")
        def _(v):
            v.tensor_copy(wob[:, :], scr[:, 0:1024])
            return v.tensor_copy(wobias[:, :], scr[0:1, 1024:2048])

        if variant == "A":
            @sq.ph("vector")
            def _(v):
                for k in range(NCH):
                    last = v.tensor_copy(QB[:, k * W:k * W + HB], QA[:, k * W:k * W + HB])
                return last

        TWO_PI = 2.0 * math.pi

        src, dst = QA, QB
        for it in range(1, 12):
            n = 1 << (it - 1)
            s_k = HB if variant == "A" else ((1 << it) - 1)
            width_total = W - s_k
            nchk = (width_total + 511) // 512
            for ci in range(nchk):
                cs = s_k + ci * 512
                wd = min(512, W - cs)

                @sq.ph("tensor")
                def _(t, src=src, cs=cs, wd=wd, n=n):
                    for k in range(NCH):
                        t.matmul(psF[:, 0:wd], wfb[:, k * 64:(k + 1) * 64],
                                 src[:, k * W + cs - n:k * W + cs - n + wd],
                                 start=(k == 0), stop=False)
                    for k in range(NCH):
                        t.matmul(psF[:, 0:wd], wfb[:, (8 + k) * 64:(9 + k) * 64],
                                 src[:, k * W + cs:k * W + cs + wd],
                                 start=False, stop=False)
                    return t.matmul(psF[:, 0:wd], wfbias[0:1, :], onesr[0:1, 0:wd],
                                    start=False, stop=True)

                @sq.ph("vector")
                def _(v, wd=wd):
                    v.tensor_scalar(fr[:, 0:wd], psF[0:64, 0:wd], 1.0 / TWO_PI, 12582912.0,
                                    ALU.mult, ALU.add)
                    v.tensor_scalar(f2[:, 0:wd], fr[:, 0:wd], 1.0, -12582912.0, ALU.mult, ALU.add)
                    v.scalar_tensor_tensor(out=f3[:, 0:wd], in0=f2[:, 0:wd], scalar=-TWO_PI,
                                           in1=psF[0:64, 0:wd], op0=ALU.mult, op1=ALU.add)
                    v.tensor_scalar(fr[:, 0:wd], psF[0:64, 0:wd], 1.0 / TWO_PI, 12582912.25,
                                    ALU.mult, ALU.add)
                    v.tensor_scalar(f2[:, 0:wd], fr[:, 0:wd], 1.0, -12582912.0, ALU.mult, ALU.add)
                    v.scalar_tensor_tensor(out=fr[:, 0:wd], in0=f2[:, 0:wd], scalar=-TWO_PI,
                                           in1=psF[0:64, 0:wd], op0=ALU.mult, op1=ALU.add)
                    return v.tensor_scalar(f2[:, 0:wd], fr[:, 0:wd], 1.0, 0.5 * math.pi,
                                           ALU.mult, ALU.add)

                @sq.ph("scalar")
                def _(s, wd=wd):
                    s.activation(out=sct[0:64, 0:wd], in_=f3[:, 0:wd], func=AF.Sin)
                    return s.activation(out=sct[64:128, 0:wd], in_=f2[:, 0:wd], func=AF.Sin)

                for o in range(NCH):
                    @sq.ph("tensor")
                    def _(t, o=o, wd=wd):
                        t.matmul(psM[:, 0:wd], wob[:, o * 128:(o + 1) * 128], sct[:, 0:wd],
                                 start=True, stop=False)
                        return t.matmul(psM[:, 0:wd], wobias[0:1, o * 128:(o + 1) * 128],
                                        onesr[0:1, 0:wd], start=False, stop=True)

                    @sq.ph("vector")
                    def _(v, o=o, wd=wd):
                        return v.tensor_copy(qraw[:, o * 512:o * 512 + wd], psM[:, 0:wd])

                @sq.ph("scalar")
                def _(s, wd=wd):
                    for o in range(NCH):
                        last = s.activation(out=qabs[:, o * 512:o * 512 + wd],
                                            in_=qraw[:, o * 512:o * 512 + wd], func=AF.Abs)
                    return last

                @sq.ph("vector")
                def _(v, wd=wd):
                    v.tensor_max(qm[:, 0:wd], qabs[:, 0:wd], qabs[:, 512:512 + wd])
                    for o in range(2, NCH):
                        last = v.tensor_max(qm[:, 0:wd], qm[:, 0:wd], qabs[:, o * 512:o * 512 + wd])
                    return last

                nsub = (wd + 127) // 128
                for j in range(nsub):
                    swd = min(128, wd - j * 128)

                    @sq.ph("tensor")
                    def _(t, j=j, swd=swd):
                        t.transpose(psT[:, 0:128][0:swd, 0:128] if False else psT[0:swd, 0:128],
                                    qm[:, j * 128:j * 128 + swd], idb[:, :])
                        return t.matmul(psT[0:1, 128:129], ones_col[0:1, 0:1], ones_col[0:1, 0:1],
                                        start=True, stop=True)

                    @sq.ph("vector")
                    def _(v, j=j, swd=swd):
                        v.tensor_copy(scr[0:swd, 0:128], psT[0:swd, 0:128])
                        v.reduce_max(rcol[0:swd, 0:1], scr[0:swd, 0:128], AX.X)
                        v.tensor_scalar_add(rcol[0:swd, 1:2], rcol[0:swd, 0:1], 1e-6)
                        return v.reciprocal(rcol[0:swd, 2:3], rcol[0:swd, 1:2])

                    @sq.ph("tensor")
                    def _(t, j=j, swd=swd):
                        t.transpose(psT[0:1, 129:129 + swd], rcol[0:swd, 2:3], idb[0:swd, 0:swd])
                        return t.matmul(psT[0:1, 128:129], ones_col[0:1, 0:1], ones_col[0:1, 0:1],
                                        start=True, stop=True)

                    @sq.ph("vector")
                    def _(v, j=j, swd=swd):
                        return v.tensor_copy(rrow[0:1, j * 128:j * 128 + swd], psT[0:1, 129:129 + swd])

                @sq.ph("tensor")
                def _(t, wd=wd):
                    return t.matmul(psR[:, 0:wd], ones_row[:, :], rrow[0:1, 0:wd],
                                    start=True, stop=True)

                @sq.ph("vector")
                def _(v, dst=dst, cs=cs, wd=wd):
                    for o in range(NCH):
                        last = v.tensor_mul(dst[:, o * W + cs:o * W + cs + wd],
                                            qraw[:, o * 512:o * 512 + wd], psR[:, 0:wd])
                    return last

            src, dst = dst, src

        @sq.dma("sync")
        def _(q, src=src):
            return [q.dma_start(out=o_q[k * 128:(k + 1) * 128, :], in_=src[:, k * W + HB:k * W + W])
                    for k in range(NCH)]

        sq.build(nc)
    return nc



def build_c34():
    T = TSH
    nc = new_nc()
    qcur = nc.dram_tensor("qcur", [D, T], BF16, kind="ExternalInput").ap()
    qprev = nc.dram_tensor("qprev", [D, T], BF16, kind="ExternalInput").ap()
    wf = nc.dram_tensor("wf", [2049, 64], F32, kind="ExternalInput").ap()
    wout = nc.dram_tensor("wout", [129, D], F32, kind="ExternalInput").ap()
    ident_in = nc.dram_tensor("ident", [128, 128], F32, kind="ExternalInput").ap()
    o_q = nc.dram_tensor("o_q", [D, T], BF16, kind="ExternalOutput").ap()

    sq = Seq()
    with contextlib.ExitStack() as cx:
        E = cx.enter_context
        QA = E(nc.sbuf_tensor("QA", [128, NCH * T], BF16))
        QP = E(nc.sbuf_tensor("QP", [128, NCH * T], BF16))
        QO = E(nc.sbuf_tensor("QO", [128, NCH * T], BF16))
        wfb = E(nc.sbuf_tensor("wfb", [128, 16 * 64], BF16))
        wfbias = E(nc.sbuf_tensor("wfbias", [1, 64], BF16))
        wob = E(nc.sbuf_tensor("wob", [128, NCH * 128], BF16))
        wobias = E(nc.sbuf_tensor("wobias", [1, D], BF16))
        onesr = E(nc.sbuf_tensor("onesr", [1, 512], BF16))
        sct = E(nc.sbuf_tensor("sct", [128, 512], BF16))
        fr = E(nc.sbuf_tensor("fr", [64, 512], F32))
        f2 = E(nc.sbuf_tensor("f2", [64, 512], F32))
        f3 = E(nc.sbuf_tensor("f3", [64, 512], F32))
        qraw = E(nc.sbuf_tensor("qraw", [128, NCH * 512], F32))
        qabs = E(nc.sbuf_tensor("qabs", [128, NCH * 512], F32))
        qm = E(nc.sbuf_tensor("qm", [128, 512], F32))
        rcol = E(nc.sbuf_tensor("rcol", [128, 8], F32))
        rrow = E(nc.sbuf_tensor("rrow", [1, 512], F32))
        idb = E(nc.sbuf_tensor("idb", [128, 128], F32))
        ones_col = E(nc.sbuf_tensor("ones_col", [128, 1], F32))
        ones_row = E(nc.sbuf_tensor("ones_row", [1, 128], F32))
        scr = E(nc.sbuf_tensor("scr", [128, 2048], F32))
        psF = E(nc.psum_tensor([64, 512], F32))
        psM = E(nc.psum_tensor([128, 512], F32))
        psT = E(nc.psum_tensor([128, 384], F32))
        psR = E(nc.psum_tensor([128, 512], F32))

        @sq.dma("sync")
        def _(q):
            ins = [q.dma_start(out=QA[:, k * T:(k + 1) * T], in_=qcur[k * 128:(k + 1) * 128, :])
                   for k in range(NCH)]
            ins += [q.dma_start(out=QP[:, k * T:(k + 1) * T], in_=qprev[k * 128:(k + 1) * 128, :])
                    for k in range(NCH)]
            ins.append(q.dma_start(out=idb[:, :], in_=ident_in[:, :]))
            ins.append(q.dma_start(out=scr[:, 0:1024].rearrange("p (c n) -> p c n", c=16),
                                   in_=wf[0:2048, :].rearrange("(c p) n -> p c n", p=128)))
            ins.append(q.dma_start(out=scr[0:1, 1024:1088], in_=wf[2048:2049, :]))
            return ins

        @sq.ph("vector")
        def _(v):
            v.memset(ones_col[:, :], 1.0)
            v.memset(ones_row[:, :], 1.0)
            v.memset(onesr[:, :], 1.0)
            v.tensor_copy(wfb[:, :], scr[:, 0:1024])
            return v.tensor_copy(wfbias[:, :], scr[0:1, 1024:1088])

        @sq.dma("sync")
        def _(q):
            return [q.dma_start(out=scr[:, 0:1024], in_=wout[0:128, :]),
                    q.dma_start(out=scr[0:1, 1024:2048], in_=wout[128:129, :])]

        @sq.ph("vector")
        def _(v):
            v.tensor_copy(wob[:, :], scr[:, 0:1024])
            return v.tensor_copy(wobias[:, :], scr[0:1, 1024:2048])

        TWO_PI = 2.0 * math.pi
        for ci in range(T // 512):
            cs = ci * 512
            wd = 512

            @sq.ph("tensor")
            def _(t, cs=cs, wd=wd):
                for k in range(NCH):
                    t.matmul(psF[:, 0:wd], wfb[:, k * 64:(k + 1) * 64],
                             QP[:, k * T + cs:k * T + cs + wd], start=(k == 0), stop=False)
                for k in range(NCH):
                    t.matmul(psF[:, 0:wd], wfb[:, (8 + k) * 64:(9 + k) * 64],
                             QA[:, k * T + cs:k * T + cs + wd], start=False, stop=False)
                return t.matmul(psF[:, 0:wd], wfbias[0:1, :], onesr[0:1, 0:wd],
                                start=False, stop=True)

            @sq.ph("vector")
            def _(v, wd=wd):
                v.tensor_scalar(fr[:, 0:wd], psF[0:64, 0:wd], 1.0 / TWO_PI, 12582912.0,
                                ALU.mult, ALU.add)
                v.tensor_scalar(f2[:, 0:wd], fr[:, 0:wd], 1.0, -12582912.0, ALU.mult, ALU.add)
                v.scalar_tensor_tensor(out=f3[:, 0:wd], in0=f2[:, 0:wd], scalar=-TWO_PI,
                                       in1=psF[0:64, 0:wd], op0=ALU.mult, op1=ALU.add)
                v.tensor_scalar(fr[:, 0:wd], psF[0:64, 0:wd], 1.0 / TWO_PI, 12582912.25,
                                ALU.mult, ALU.add)
                v.tensor_scalar(f2[:, 0:wd], fr[:, 0:wd], 1.0, -12582912.0, ALU.mult, ALU.add)
                v.scalar_tensor_tensor(out=fr[:, 0:wd], in0=f2[:, 0:wd], scalar=-TWO_PI,
                                       in1=psF[0:64, 0:wd], op0=ALU.mult, op1=ALU.add)
                return v.tensor_scalar(f2[:, 0:wd], fr[:, 0:wd], 1.0, 0.5 * math.pi,
                                       ALU.mult, ALU.add)

            @sq.ph("scalar")
            def _(s, wd=wd):
                s.activation(out=sct[0:64, 0:wd], in_=f3[:, 0:wd], func=AF.Sin)
                return s.activation(out=sct[64:128, 0:wd], in_=f2[:, 0:wd], func=AF.Sin)

            for o in range(NCH):
                @sq.ph("tensor")
                def _(t, o=o, wd=wd):
                    t.matmul(psM[:, 0:wd], wob[:, o * 128:(o + 1) * 128], sct[:, 0:wd],
                             start=True, stop=False)
                    return t.matmul(psM[:, 0:wd], wobias[0:1, o * 128:(o + 1) * 128],
                                    onesr[0:1, 0:wd], start=False, stop=True)

                @sq.ph("vector")
                def _(v, o=o, wd=wd):
                    return v.tensor_copy(qraw[:, o * 512:o * 512 + wd], psM[:, 0:wd])

            @sq.ph("scalar")
            def _(s, wd=wd):
                for o in range(NCH):
                    last = s.activation(out=qabs[:, o * 512:o * 512 + wd],
                                        in_=qraw[:, o * 512:o * 512 + wd], func=AF.Abs)
                return last

            @sq.ph("vector")
            def _(v, wd=wd):
                v.tensor_max(qm[:, 0:wd], qabs[:, 0:wd], qabs[:, 512:512 + wd])
                for o in range(2, NCH):
                    last = v.tensor_max(qm[:, 0:wd], qm[:, 0:wd], qabs[:, o * 512:o * 512 + wd])
                return last

            for j in range(4):
                @sq.ph("tensor")
                def _(t, j=j):
                    t.transpose(psT[0:128, 0:128], qm[:, j * 128:(j + 1) * 128], idb[:, :])
                    return t.matmul(psT[0:1, 128:129], ones_col[0:1, 0:1], ones_col[0:1, 0:1],
                                    start=True, stop=True)

                @sq.ph("vector")
                def _(v, j=j):
                    v.tensor_copy(scr[:, 0:128], psT[:, 0:128])
                    v.reduce_max(rcol[:, 0:1], scr[:, 0:128], AX.X)
                    v.tensor_scalar_add(rcol[:, 1:2], rcol[:, 0:1], 1e-6)
                    return v.reciprocal(rcol[:, 2:3], rcol[:, 1:2])

                @sq.ph("tensor")
                def _(t, j=j):
                    t.transpose(psT[0:1, 129:257], rcol[:, 2:3], idb[:, :])
                    return t.matmul(psT[0:1, 128:129], ones_col[0:1, 0:1], ones_col[0:1, 0:1],
                                    start=True, stop=True)

                @sq.ph("vector")
                def _(v, j=j):
                    return v.tensor_copy(rrow[0:1, j * 128:(j + 1) * 128], psT[0:1, 129:257])

            @sq.ph("tensor")
            def _(t, wd=wd):
                return t.matmul(psR[:, 0:wd], ones_row[:, :], rrow[0:1, 0:wd],
                                start=True, stop=True)

            @sq.ph("vector")
            def _(v, cs=cs, wd=wd):
                for o in range(NCH):
                    last = v.tensor_mul(QO[:, o * T + cs:o * T + cs + wd],
                                        qraw[:, o * 512:o * 512 + wd], psR[:, 0:wd])
                return last

        @sq.dma("sync")
        def _(q):
            return [q.dma_start(out=o_q[k * 128:(k + 1) * 128, :], in_=QO[:, k * T:(k + 1) * T])
                    for k in range(NCH)]

        sq.build(nc)
    return nc



def build_c5():
    T = TSH
    nc = new_nc()
    x_fm = nc.dram_tensor("x_fm", [D, T], F32, kind="ExternalInput").ap()
    q13 = nc.dram_tensor("q13", [D, T], BF16, kind="ExternalInput").ap()
    vv = nc.dram_tensor("vv", [D, T], F32, kind="ExternalInput").ap()
    proj = nc.dram_tensor("proj", [1025, D], F32, kind="ExternalInput").ap()
    o_x = nc.dram_tensor("o_x", [D, T], F32, kind="ExternalOutput").ap()

    sq = Seq()
    with contextlib.ExitStack() as cx:
        E = cx.enter_context
        A = E(nc.sbuf_tensor("A", [128, NCH * T], F32))
        Yb = E(nc.sbuf_tensor("Yb", [128, NCH * T], BF16))
        wob = E(nc.sbuf_tensor("wob", [128, NCH * D], BF16))
        wbias = E(nc.sbuf_tensor("wbias", [1, D], BF16))
        onesr = E(nc.sbuf_tensor("onesr", [1, 512], BF16))
        scr = E(nc.sbuf_tensor("scr", [128, 2048], F32))
        sc2 = E(nc.sbuf_tensor("sc2", [128, 2048], F32))
        psM = E(nc.psum_tensor([128, 2048], F32))

        @sq.dma("sync")
        def _(q):
            return [q.dma_start(out=A[:, k * T:(k + 1) * T], in_=x_fm[k * 128:(k + 1) * 128, :])
                    for k in range(NCH)]

        @sq.ph("vector")
        def _(v):
            return v.memset(onesr[:, :], 1.0)

        for k in range(NCH):
            @sq.dma("sync")
            def _(q, k=k):
                return [q.dma_start(out=scr[:, 0:T].rearrange("p t -> p t") if False else scr[:, 0:T], in_=vv[k * 128:(k + 1) * 128, :]),
                        q.dma_start(out=Yb[:, k * T:(k + 1) * T], in_=q13[k * 128:(k + 1) * 128, :])]

            @sq.ph("vector")
            def _(v, k=k):
                v.tensor_copy(sc2[:, 0:T], Yb[:, k * T:(k + 1) * T])
                v.tensor_mul(sc2[:, 0:T], sc2[:, 0:T], scr[:, 0:T])
                return v.tensor_copy(Yb[:, k * T:(k + 1) * T], sc2[:, 0:T])

        for k in range(NCH):
            @sq.dma("sync")
            def _(q, k=k):
                return [q.dma_start(out=scr[:, 0:1024], in_=proj[k * 128:(k + 1) * 128, :])]

            @sq.ph("vector")
            def _(v, k=k):
                return v.tensor_copy(wob[:, k * D:(k + 1) * D], scr[:, 0:1024])

        @sq.dma("sync")
        def _(q):
            return [q.dma_start(out=scr[0:1, 0:1024], in_=proj[1024:1025, :])]

        @sq.ph("vector")
        def _(v):
            return v.tensor_copy(wbias[:, :], scr[0:1, 0:1024])

        for o in range(NCH):
            @sq.ph("tensor")
            def _(t, o=o):
                for tc in range(4):
                    for k in range(NCH):
                        t.matmul(psM[:, tc * 512:tc * 512 + 512],
                                 wob[:, k * D + o * 128:k * D + o * 128 + 128],
                                 Yb[:, k * T + tc * 512:k * T + tc * 512 + 512],
                                 start=(k == 0), stop=False)
                    last = t.matmul(psM[:, tc * 512:tc * 512 + 512],
                                    wbias[0:1, o * 128:(o + 1) * 128],
                                    onesr[0:1, :], start=False, stop=True)
                return last

            @sq.ph("vector")
            def _(v, o=o):
                return v.tensor_add(A[:, o * T:(o + 1) * T], A[:, o * T:(o + 1) * T], psM[:, 0:T])

        @sq.dma("sync")
        def _(q):
            return [q.dma_start(out=o_x[k * 128:(k + 1) * 128, :], in_=A[:, k * T:(k + 1) * T])
                    for k in range(NCH)]

        sq.build(nc)
    return nc



def build_bmm_sigstat(ns):
    nc = new_nc()
    stat = nc.dram_tensor("stat", [ns, 64, 128], BF16, kind="ExternalInput").ap()
    rhs = nc.dram_tensor("rhs", [64, 256], BF16, kind="ExternalInput").ap()
    out = nc.dram_tensor("out", [ns, 128, 256], BF16, kind="ExternalOutput").ap()
    sq = Seq()
    with contextlib.ExitStack() as cx:
        E = cx.enter_context
        GRP = 16
        stb = E(nc.sbuf_tensor("stb", [64, GRP * 128], BF16))
        rb = E(nc.sbuf_tensor("rb", [64, 256], BF16))
        ob = E(nc.sbuf_tensor("ob", [128, GRP * 256], BF16))
        ps = E(nc.psum_tensor([128, 512], F32))

        @sq.dma("sync")
        def _(q):
            return [q.dma_start(out=rb[:, :], in_=rhs[:, :])]

        for g in range(ns // GRP):
            @sq.dma("sync")
            def _(q, g=g):
                return [q.dma_start(out=stb[:, :].rearrange("p (i n) -> p i n", i=GRP),
                                    in_=stat[g * GRP:(g + 1) * GRP, :, :].rearrange("i p n -> p i n"))]

            for i in range(GRP):
                @sq.ph("tensor")
                def _(t, i=i):
                    return t.matmul(ps[:, 0:256], stb[:, i * 128:(i + 1) * 128], rb[:, :],
                                    start=True, stop=True)

                @sq.ph("scalar")
                def _(s, i=i):
                    return s.copy(out=ob[:, i * 256:(i + 1) * 256], in_=ps[:, 0:256])

            @sq.dma("sync")
            def _(q, g=g):
                return [q.dma_start(out=out[g * GRP:(g + 1) * GRP, :, :].rearrange("i p n -> p i n"),
                                    in_=ob[:, :].rearrange("p (i n) -> p i n", i=GRP))]

        sq.build(nc)
    return nc


def build_bmm_perk(nk, p, m, n):
    nc = new_nc()
    A_ = nc.dram_tensor("A", [nk, p, m], BF16, kind="ExternalInput").ap()
    B_ = nc.dram_tensor("B", [nk, p, m], BF16, kind="ExternalInput").ap()
    C_ = nc.dram_tensor("C", [nk, p, m], BF16, kind="ExternalInput").ap()
    D_ = nc.dram_tensor("Dm", [nk, p, m], BF16, kind="ExternalInput").ap()
    U_ = nc.dram_tensor("U", [nk, p, n], BF16, kind="ExternalInput").ap()
    V_ = nc.dram_tensor("V", [nk, p, n], BF16, kind="ExternalInput").ap()
    ore = nc.dram_tensor("ore", [nk, m, n], BF16, kind="ExternalOutput").ap()
    oim = nc.dram_tensor("oim", [nk, m, n], BF16, kind="ExternalOutput").ap()
    sq = Seq()
    nsl = (n + 511) // 512
    with contextlib.ExitStack() as cx:
        E = cx.enter_context
        ab = E(nc.sbuf_tensor("ab", [p, 4 * m], BF16))
        ub = E(nc.sbuf_tensor("ub", [p, 2 * n], BF16))
        ob = E(nc.sbuf_tensor("ob", [m, 2 * n], BF16))
        ps = E(nc.psum_tensor([m, 1024], F32))

        for k in range(nk):
            @sq.dma("sync")
            def _(q, k=k):
                return [q.dma_start(out=ab[:, 0:m], in_=A_[k, :, :]),
                        q.dma_start(out=ab[:, m:2 * m], in_=B_[k, :, :]),
                        q.dma_start(out=ab[:, 2 * m:3 * m], in_=C_[k, :, :]),
                        q.dma_start(out=ab[:, 3 * m:4 * m], in_=D_[k, :, :]),
                        q.dma_start(out=ub[:, 0:n], in_=U_[k, :, :]),
                        q.dma_start(out=ub[:, n:2 * n], in_=V_[k, :, :])]

            for sl in range(nsl):
                ss = sl * 512
                ww = min(512, n - ss)

                @sq.ph("tensor")
                def _(t, k=k, ss=ss, ww=ww):
                    t.matmul(ps[:, 0:ww], ab[:, 0:m], ub[:, ss:ss + ww], start=True, stop=False)
                    t.matmul(ps[:, 0:ww], ab[:, m:2 * m], ub[:, n + ss:n + ss + ww],
                             start=False, stop=True)
                    t.matmul(ps[:, 512:512 + ww], ab[:, 2 * m:3 * m], ub[:, ss:ss + ww],
                             start=True, stop=False)
                    return t.matmul(ps[:, 512:512 + ww], ab[:, 3 * m:4 * m],
                                    ub[:, n + ss:n + ss + ww], start=False, stop=True)

                @sq.ph("scalar")
                def _(s, k=k, ss=ss, ww=ww):
                    s.copy(out=ob[:, ss:ss + ww], in_=ps[:, 0:ww])
                    return s.copy(out=ob[:, n + ss:n + ss + ww], in_=ps[:, 512:512 + ww])

            @sq.dma("sync")
            def _(q, k=k):
                return [q.dma_start(out=ore[k, :, :], in_=ob[:, 0:n]),
                        q.dma_start(out=oim[k, :, :], in_=ob[:, n:2 * n])]

        sq.build(nc)
    return nc


def build_cmul(nk, nch, nb):
    n = nb * nch
    nc = new_nc()
    xre = nc.dram_tensor("xre", [nk, 128, n], BF16, kind="ExternalInput").ap()
    xim = nc.dram_tensor("xim", [nk, 128, n], BF16, kind="ExternalInput").ap()
    kre = nc.dram_tensor("kre", [nk, 128, nch], BF16, kind="ExternalInput").ap()
    kim = nc.dram_tensor("kim", [nk, 128, nch], BF16, kind="ExternalInput").ap()
    yre = nc.dram_tensor("yre", [nk, 128, n], BF16, kind="ExternalOutput").ap()
    yim = nc.dram_tensor("yim", [nk, 128, n], BF16, kind="ExternalOutput").ap()
    sq = Seq()
    with contextlib.ExitStack() as cx:
        E = cx.enter_context
        xb = E(nc.sbuf_tensor("xb", [128, 2 * n], F32))
        kb = E(nc.sbuf_tensor("kb", [128, 2 * nch], F32))
        xbb = E(nc.sbuf_tensor("xbb", [128, 2 * n], BF16))
        kbb = E(nc.sbuf_tensor("kbb", [128, 2 * nch], BF16))
        t1 = E(nc.sbuf_tensor("t1", [128, n], F32))
        t2 = E(nc.sbuf_tensor("t2", [128, n], F32))
        ob = E(nc.sbuf_tensor("ob", [128, 2 * n], BF16))

        for k in range(nk):
            @sq.dma("sync")
            def _(q, k=k):
                return [q.dma_start(out=xbb[:, 0:n], in_=xre[k, :, :]),
                        q.dma_start(out=xbb[:, n:2 * n], in_=xim[k, :, :]),
                        q.dma_start(out=kbb[:, 0:nch], in_=kre[k, :, :]),
                        q.dma_start(out=kbb[:, nch:2 * nch], in_=kim[k, :, :])]

            @sq.ph("vector")
            def _(v, k=k):
                v.tensor_copy(xb[:, :], xbb[:, :])
                return v.tensor_copy(kb[:, :], kbb[:, :])

            @sq.ph("vector")
            def _(v, k=k):
                krev = kb[:, 0:nch].rearrange("p (u c) -> p u c", u=1).broadcast_to([128, nb, nch])
                kimv = kb[:, nch:2 * nch].rearrange("p (u c) -> p u c", u=1).broadcast_to([128, nb, nch])
                xrv = xb[:, 0:n].rearrange("p (b c) -> p b c", b=nb)
                xiv = xb[:, n:2 * n].rearrange("p (b c) -> p b c", b=nb)
                v.tensor_tensor(out=t1[:, :].rearrange("p (b c) -> p b c", b=nb), in0=xrv, in1=krev, op=ALU.mult)
                v.tensor_tensor(out=t2[:, :].rearrange("p (b c) -> p b c", b=nb), in0=xiv, in1=kimv, op=ALU.mult)
                v.tensor_sub(ob[:, 0:n], t1[:, :], t2[:, :])
                v.tensor_tensor(out=t1[:, :].rearrange("p (b c) -> p b c", b=nb), in0=xrv, in1=kimv, op=ALU.mult)
                v.tensor_tensor(out=t2[:, :].rearrange("p (b c) -> p b c", b=nb), in0=xiv, in1=krev, op=ALU.mult)
                return v.tensor_add(ob[:, n:2 * n], t1[:, :], t2[:, :])

            @sq.dma("sync")
            def _(q, k=k):
                return [q.dma_start(out=yre[k, :, :], in_=ob[:, 0:n]),
                        q.dma_start(out=yim[k, :, :], in_=ob[:, n:2 * n])]

        sq.build(nc)
    return nc


def build_bmm_acc2(p, m, nwide):
    nc = new_nc()
    L1 = nc.dram_tensor("L1", [p, m], BF16, kind="ExternalInput").ap()
    L2 = nc.dram_tensor("L2", [p, m], BF16, kind="ExternalInput").ap()
    R1 = nc.dram_tensor("R1", [p, nwide], BF16, kind="ExternalInput").ap()
    R2 = nc.dram_tensor("R2", [p, nwide], BF16, kind="ExternalInput").ap()
    out = nc.dram_tensor("out", [m, nwide], BF16, kind="ExternalOutput").ap()
    sq = Seq()
    with contextlib.ExitStack() as cx:
        E = cx.enter_context
        lb = E(nc.sbuf_tensor("lb", [p, 2 * m], BF16))
        rb = E(nc.sbuf_tensor("rb", [p, 1024], BF16))
        ob = E(nc.sbuf_tensor("ob", [m, 512], BF16))
        ps = E(nc.psum_tensor([m, 512], F32))

        @sq.dma("sync")
        def _(q):
            return [q.dma_start(out=lb[:, 0:m], in_=L1[:, :]),
                    q.dma_start(out=lb[:, m:2 * m], in_=L2[:, :])]

        for sl in range((nwide + 511) // 512):
            ss = sl * 512
            ww = min(512, nwide - ss)

            @sq.dma("sync")
            def _(q, ss=ss, ww=ww):
                return [q.dma_start(out=rb[:, 0:ww], in_=R1[:, ss:ss + ww]),
                        q.dma_start(out=rb[:, 512:512 + ww], in_=R2[:, ss:ss + ww])]

            @sq.ph("tensor")
            def _(t, ss=ss, ww=ww):
                t.matmul(ps[:, 0:ww], lb[:, 0:m], rb[:, 0:ww], start=True, stop=False)
                return t.matmul(ps[:, 0:ww], lb[:, m:2 * m], rb[:, 512:512 + ww],
                                start=False, stop=True)

            @sq.ph("scalar")
            def _(s, ss=ss, ww=ww):
                return s.copy(out=ob[:, 0:ww], in_=ps[:, 0:ww])

            @sq.dma("sync")
            def _(q, ss=ss, ww=ww):
                return [q.dma_start(out=out[:, ss:ss + ww], in_=ob[:, 0:ww])]

        sq.build(nc)
    return nc



def build_l1a_xz():
    TT_ = 2 * S
    nc = new_nc()
    xln = nc.dram_tensor("xln", [D, TT_], BF16, kind="ExternalInput").ap()
    wsl = nc.dram_tensor("wsl", [D, 512], F32, kind="ExternalInput").ap()
    o_xz = nc.dram_tensor("o_xz", [512, TT_], F32, kind="ExternalOutput").ap()
    sq = Seq()
    with contextlib.ExitStack() as cx:
        E = cx.enter_context
        wb = E(nc.sbuf_tensor("wb", [128, NCH * 512], BF16))
        xb = E(nc.sbuf_tensor("xb", [128, NCH * 512], BF16))
        ob = E(nc.sbuf_tensor("ob", [128, 4 * 512], F32))
        scr = E(nc.sbuf_tensor("scr", [128, 512], F32))
        ps = E(nc.psum_tensor([128, 2048], F32))

        for k in range(NCH):
            @sq.dma("sync")
            def _(q, k=k):
                return [q.dma_start(out=scr[:, :], in_=wsl[k * 128:(k + 1) * 128, :])]

            @sq.ph("vector")
            def _(v, k=k):
                return v.tensor_copy(wb[:, k * 512:(k + 1) * 512], scr[:, :])

        for tc in range(TT_ // 512):
            @sq.dma("sync")
            def _(q, tc=tc):
                return [q.dma_start(out=xb[:, k * 512:(k + 1) * 512],
                                    in_=xln[k * 128:(k + 1) * 128, tc * 512:(tc + 1) * 512])
                        for k in range(NCH)]

            @sq.ph("tensor")
            def _(t, tc=tc):
                for o in range(4):
                    for k in range(NCH):
                        last = t.matmul(ps[:, o * 512:(o + 1) * 512],
                                        wb[:, k * 512 + o * 128:k * 512 + o * 128 + 128],
                                        xb[:, k * 512:(k + 1) * 512],
                                        start=(k == 0), stop=(k == NCH - 1))
                return last

            @sq.ph("vector")
            def _(v, tc=tc):
                for o in range(4):
                    last = v.tensor_copy(ob[:, o * 512:(o + 1) * 512], ps[:, o * 512:(o + 1) * 512])
                return last

            @sq.dma("sync")
            def _(q, tc=tc):
                return [q.dma_start(out=o_xz[o * 128:(o + 1) * 128, tc * 512:(tc + 1) * 512],
                                    in_=ob[:, o * 512:(o + 1) * 512]) for o in range(4)]

        sq.build(nc)
    return nc



def build_l1a_conv():
    TT_ = 2 * S
    nc = new_nc()
    xz = nc.dram_tensor("xz", [512, TT_], F32, kind="ExternalInput").ap()
    convw = nc.dram_tensor("convw", [256, 4], F32, kind="ExternalInput").ap()
    convb = nc.dram_tensor("convb", [256, 1], F32, kind="ExternalInput").ap()
    o_xc = nc.dram_tensor("o_xc", [256, TT_], F32, kind="ExternalOutput").ap()
    o_xcb = nc.dram_tensor("o_xcb", [256, TT_], BF16, kind="ExternalOutput").ap()
    o_sz = nc.dram_tensor("o_sz", [256, TT_], BF16, kind="ExternalOutput").ap()
    kssm = nc.dram_tensor("kssm", [256, 4096], F32, kind="ExternalInput").ap()
    o_kb = nc.dram_tensor("o_kb", [256, 4096], BF16, kind="ExternalOutput").ap()
    sq = Seq()
    with contextlib.ExitStack() as cx:
        E = cx.enter_context
        xt = E(nc.sbuf_tensor("xt", [128, S + 3], F32))
        acc = E(nc.sbuf_tensor("acc", [128, S], F32))
        acc2 = E(nc.sbuf_tensor("acc2", [128, S], F32))
        res = E(nc.sbuf_tensor("res", [128, S], F32))
        resb = E(nc.sbuf_tensor("resb", [128, S], BF16))
        cw = E(nc.sbuf_tensor("cw", [128, 2 * 4], F32))
        cb = E(nc.sbuf_tensor("cb", [128, 2], F32))

        @sq.dma("sync")
        def _(q):
            ds = [q.dma_start(out=cw[:, 0:8].rearrange("p (c n) -> p c n", c=2),
                              in_=convw.rearrange("(c p) n -> p c n", p=128))]
            for c in range(2):
                ds.append(q.dma_start(out=cb[:, c:c + 1], in_=convb[c * 128:(c + 1) * 128, :]))
            return ds

        for cc in range(2):
            @sq.dma("sync")
            def _(q, cc=cc):
                return [q.dma_start(out=xt[:, 0:4096], in_=kssm[cc * 128:(cc + 1) * 128, :])]

            @sq.ph("vector")
            def _(v, cc=cc):
                return v.tensor_copy(resb[:, 0:4096], xt[:, 0:4096])

            @sq.dma("sync")
            def _(q, cc=cc):
                return [q.dma_start(out=o_kb[cc * 128:(cc + 1) * 128, :], in_=resb[:, 0:4096])]

        for cc in range(2):
            for bb in range(2):
                @sq.ph("vector")
                def _(v, cc=cc, bb=bb):
                    return v.memset(xt[:, 0:3], 0.0)

                @sq.dma("sync")
                def _(q, cc=cc, bb=bb):
                    return [q.dma_start(out=xt[:, 3:3 + S],
                                        in_=xz[cc * 128:(cc + 1) * 128, bb * S:(bb + 1) * S])]

                @sq.ph("vector")
                def _(v, cc=cc, bb=bb):
                    v.tensor_scalar(acc[:, :], xt[:, 0:S], cw[:, cc * 4 + 0:cc * 4 + 1], None, ALU.mult)
                    v.scalar_tensor_tensor(out=acc2[:, :], in0=xt[:, 1:1 + S],
                                           scalar=cw[:, cc * 4 + 1:cc * 4 + 2], in1=acc[:, :],
                                           op0=ALU.mult, op1=ALU.add)
                    v.scalar_tensor_tensor(out=acc[:, :], in0=xt[:, 2:2 + S],
                                           scalar=cw[:, cc * 4 + 2:cc * 4 + 3], in1=acc2[:, :],
                                           op0=ALU.mult, op1=ALU.add)
                    v.scalar_tensor_tensor(out=acc2[:, :], in0=xt[:, 3:3 + S],
                                           scalar=cw[:, cc * 4 + 3:cc * 4 + 4], in1=acc[:, :],
                                           op0=ALU.mult, op1=ALU.add)
                    return v.tensor_scalar(acc[:, :], acc2[:, :], cb[:, cc:cc + 1], None, ALU.add)

                @sq.ph("scalar")
                def _(s, cc=cc, bb=bb):
                    s.activation(out=res[:, :], in_=acc[:, :], func=AF.Silu)
                    return s.activation(out=resb[:, :], in_=acc[:, :], func=AF.Silu)

                @sq.dma("sync")
                def _(q, cc=cc, bb=bb):
                    return [q.dma_start(out=o_xc[cc * 128:(cc + 1) * 128, bb * S:(bb + 1) * S],
                                        in_=res[:, :]),
                            q.dma_start(out=o_xcb[cc * 128:(cc + 1) * 128, bb * S:(bb + 1) * S],
                                        in_=resb[:, :])]

                @sq.dma("sync")
                def _(q, cc=cc, bb=bb):
                    return [q.dma_start(out=xt[:, 3:3 + S],
                                        in_=xz[256 + cc * 128:256 + (cc + 1) * 128, bb * S:(bb + 1) * S])]

                @sq.ph("scalar")
                def _(s, cc=cc, bb=bb):
                    return s.activation(out=resb[:, :], in_=xt[:, 3:3 + S], func=AF.Silu)

                @sq.dma("sync")
                def _(q, cc=cc, bb=bb):
                    return [q.dma_start(out=o_sz[cc * 128:(cc + 1) * 128, bb * S:(bb + 1) * S],
                                        in_=resb[:, :])]

        sq.build(nc)
    return nc



def build_l1a_gate():
    TT_ = 2 * S
    nc = new_nc()
    yy = nc.dram_tensor("yy", [256, TT_], BF16, kind="ExternalInput").ap()
    xc = nc.dram_tensor("xc", [256, TT_], F32, kind="ExternalInput").ap()
    sz = nc.dram_tensor("sz", [256, TT_], BF16, kind="ExternalInput").ap()
    dsk = nc.dram_tensor("dsk", [256, 1], F32, kind="ExternalInput").ap()
    o_yg = nc.dram_tensor("o_yg", [256, TT_], BF16, kind="ExternalOutput").ap()
    sq = Seq()
    with contextlib.ExitStack() as cx:
        E = cx.enter_context
        ybt = E(nc.sbuf_tensor("ybt", [128, 4096], BF16))
        yft = E(nc.sbuf_tensor("yft", [128, 4096], F32))
        xct = E(nc.sbuf_tensor("xct", [128, 4096], F32))
        szt = E(nc.sbuf_tensor("szt", [128, 4096], BF16))
        szf = E(nc.sbuf_tensor("szf", [128, 4096], F32))
        ot = E(nc.sbuf_tensor("ot", [128, 4096], BF16))
        dct = E(nc.sbuf_tensor("dct", [128, 2], F32))

        @sq.dma("sync")
        def _(q):
            return [q.dma_start(out=dct[:, c:c + 1], in_=dsk[c * 128:(c + 1) * 128, :])
                    for c in range(2)]

        for cc in range(2):
            for tb in range(TT_ // 4096):
                @sq.dma("sync")
                def _(q, cc=cc, tb=tb):
                    sl = slice(tb * 4096, (tb + 1) * 4096)
                    return [q.dma_start(out=ybt[:, :], in_=yy[cc * 128:(cc + 1) * 128, sl]),
                            q.dma_start(out=xct[:, :], in_=xc[cc * 128:(cc + 1) * 128, sl]),
                            q.dma_start(out=szt[:, :], in_=sz[cc * 128:(cc + 1) * 128, sl])]

                @sq.ph("vector")
                def _(v, cc=cc, tb=tb):
                    v.tensor_copy(yft[:, :], ybt[:, :])
                    v.tensor_copy(szf[:, :], szt[:, :])
                    v.scalar_tensor_tensor(out=xct[:, :], in0=xct[:, :], scalar=dct[:, cc:cc + 1],
                                           in1=yft[:, :], op0=ALU.mult, op1=ALU.add)
                    return v.tensor_mul(ot[:, :], xct[:, :], szf[:, :])

                @sq.dma("sync")
                def _(q, cc=cc, tb=tb):
                    sl = slice(tb * 4096, (tb + 1) * 4096)
                    return [q.dma_start(out=o_yg[cc * 128:(cc + 1) * 128, sl], in_=ot[:, :])]

        sq.build(nc)
    return nc



def build_matres(KD):
    T = TSH
    nkc = KD // 128
    nc = new_nc()
    x_fm = nc.dram_tensor("x_fm", [D, T], F32, kind="ExternalInput").ap()
    act = nc.dram_tensor("act", [KD, T], BF16, kind="ExternalInput").ap()
    wT = nc.dram_tensor("wT", [KD, D], F32, kind="ExternalInput").ap()
    o_x = nc.dram_tensor("o_x", [D, T], F32, kind="ExternalOutput").ap()
    sq = Seq()
    with contextlib.ExitStack() as cx:
        E = cx.enter_context
        A = E(nc.sbuf_tensor("A", [128, NCH * T], F32))
        U = E(nc.sbuf_tensor("U", [128, nkc * T], BF16))
        wb = E(nc.sbuf_tensor("wb", [128, nkc * D], BF16))
        scr = E(nc.sbuf_tensor("scr", [128, 2048], F32))
        psM = E(nc.psum_tensor([128, 2048], F32))

        @sq.dma("sync")
        def _(q):
            ins = [q.dma_start(out=A[:, k * T:(k + 1) * T], in_=x_fm[k * 128:(k + 1) * 128, :])
                   for k in range(NCH)]
            ins += [q.dma_start(out=U[:, k * T:(k + 1) * T], in_=act[k * 128:(k + 1) * 128, :])
                    for k in range(nkc)]
            return ins

        for k in range(nkc):
            @sq.dma("sync")
            def _(q, k=k):
                return [q.dma_start(out=scr[:, 0:1024], in_=wT[k * 128:(k + 1) * 128, :])]

            @sq.ph("vector")
            def _(v, k=k):
                return v.tensor_copy(wb[:, k * D:(k + 1) * D], scr[:, 0:1024])

        for o in range(NCH):
            @sq.ph("tensor")
            def _(t, o=o):
                for tc in range(4):
                    for k in range(nkc):
                        last = t.matmul(psM[:, tc * 512:tc * 512 + 512],
                                        wb[:, k * D + o * 128:k * D + o * 128 + 128],
                                        U[:, k * T + tc * 512:k * T + tc * 512 + 512],
                                        start=(k == 0), stop=(k == nkc - 1))
                return last

            @sq.ph("vector")
            def _(v, o=o):
                return v.tensor_add(A[:, o * T:(o + 1) * T], A[:, o * T:(o + 1) * T], psM[:, 0:T])

        @sq.dma("sync")
        def _(q):
            return [q.dma_start(out=o_x[k * 128:(k + 1) * 128, :], in_=A[:, k * T:(k + 1) * T])
                    for k in range(NCH)]

        sq.build(nc)
    return nc



def build_l1ffn():
    T = TSH
    nc = new_nc()
    x_fm = nc.dram_tensor("x_fm", [D, T], F32, kind="ExternalInput").ap()
    w1T = nc.dram_tensor("w1T", [D, 4096], F32, kind="ExternalInput").ap()
    w2T = nc.dram_tensor("w2T", [4096, D], F32, kind="ExternalInput").ap()
    b1c = nc.dram_tensor("b1c", [4096, 1], F32, kind="ExternalInput").ap()
    b2c = nc.dram_tensor("b2c", [D, 1], F32, kind="ExternalInput").ap()
    hbuf = nc.dram_tensor("hbuf", [4096, T], BF16, kind="Internal").ap()
    o_x = nc.dram_tensor("o_x", [D, T], F32, kind="ExternalOutput").ap()
    sq = Seq()
    with contextlib.ExitStack() as cx:
        E = cx.enter_context
        A = E(nc.sbuf_tensor("A", [128, NCH * T], F32))
        Bt = E(nc.sbuf_tensor("Bt", [128, NCH * T], BF16))
        wqb = E(nc.sbuf_tensor("wqb", [128, 32768], BF16))
        scr = E(nc.sbuf_tensor("scr", [128, 2048], F32))
        sc2 = E(nc.sbuf_tensor("sc2", [128, 4096], F32))
        hstg = E(nc.sbuf_tensor("hstg", [128, T], BF16))
        hh = Bt
        bcol = E(nc.sbuf_tensor("bcol", [128, 48], F32))
        gcol = E(nc.sbuf_tensor("gcol", [128, 8], F32))
        rowsu = E(nc.sbuf_tensor("rowsu", [1, T], F32))
        rowva = E(nc.sbuf_tensor("rowva", [1, T], F32))
        ones_col = E(nc.sbuf_tensor("ones_col", [128, 1], F32))
        ones_row = E(nc.sbuf_tensor("ones_row", [1, 128], F32))
        psA = E(nc.psum_tensor([1, 512], F32))
        psR = E(nc.psum_tensor([128, 1024], F32))
        psM = E(nc.psum_tensor([128, 2048], F32))

        @sq.dma("sync")
        def _(q):
            ins = [q.dma_start(out=A[:, k * T:(k + 1) * T], in_=x_fm[k * 128:(k + 1) * 128, :])
                   for k in range(NCH)]
            for c in range(32):
                ins.append(q.dma_start(out=bcol[:, c:c + 1], in_=b1c[c * 128:(c + 1) * 128, :]))
            for c in range(8):
                ins.append(q.dma_start(out=bcol[:, 32 + c:33 + c], in_=b2c[c * 128:(c + 1) * 128, :]))
            return ins

        @sq.ph("vector")
        def _(v):
            v.memset(ones_col[:, :], 1.0)
            v.memset(ones_row[:, :], 1.0)
            return v.memset(gcol[:, :], 0.0)

        h = dict(xs=A, xln=Bt, scr=scr, row_su=rowsu, row_va=rowva,
                 psA=psA, psR=psR, ones_col=ones_col, ones_row=ones_row)
        add_ln_phases(sq, h, T)

        def quant_weight(src, nchunks, ncols, per_elem):
            @sq.ph("vector")
            def _(v):
                return v.memset(gcol[:, 0:1], 0.0)

            for k in range(nchunks):
                for cs in range(0, ncols, 2048):
                    ce = min(ncols, cs + 2048)

                    @sq.dma("sync")
                    def _(q, k=k, cs=cs, ce=ce):
                        return [q.dma_start(out=scr[:, 0:ce - cs],
                                            in_=src[k * 128:(k + 1) * 128, cs:ce])]

                    @sq.ph("scalar")
                    def _(s, k=k, cs=cs, ce=ce):
                        return s.activation(out=sc2[:, 0:ce - cs], in_=scr[:, 0:ce - cs], func=AF.Abs)

                    @sq.ph("vector")
                    def _(v, k=k, cs=cs, ce=ce):
                        v.reduce_sum(gcol[:, 1:2], sc2[:, 0:ce - cs], AX.X)
                        return v.tensor_add(gcol[:, 0:1], gcol[:, 0:1], gcol[:, 1:2])

            @sq.ph("tensor")
            def _(t):
                return t.matmul(psA[0:1, 0:1], ones_col[:, :], gcol[:, 0:1], start=True, stop=True)

            @sq.ph("vector")
            def _(v):
                return v.tensor_copy(rowsu[0:1, 0:1], psA[0:1, 0:1])

            @sq.ph("tensor")
            def _(t):
                return t.matmul(psR[:, 0:1], ones_row[:, :], rowsu[0:1, 0:1], start=True, stop=True)

            @sq.ph("vector")
            def _(v):
                v.tensor_scalar(gcol[:, 2:3], psR[:, 0:1], 1.0 / per_elem, 1e-5, ALU.mult, ALU.add)
                return v.reciprocal(gcol[:, 3:4], gcol[:, 2:3])

            for k in range(nchunks):
                for cs in range(0, ncols, 2048):
                    ce = min(ncols, cs + 2048)
                    w_ = ce - cs

                    @sq.dma("sync")
                    def _(q, k=k, cs=cs, ce=ce):
                        return [q.dma_start(out=scr[:, 0:ce - cs],
                                            in_=src[k * 128:(k + 1) * 128, cs:ce])]

                    @sq.ph("vector")
                    def _(v, k=k, cs=cs, ce=ce, w_=w_):
                        v.tensor_scalar(sc2[:, 0:w_], scr[:, 0:w_], gcol[:, 3:4], None, ALU.mult)
                        v.tensor_scalar(sc2[:, 2048:2048 + w_], sc2[:, 0:w_], 1.0, 12582912.0, ALU.mult, ALU.add)
                        v.tensor_scalar(sc2[:, 0:w_], sc2[:, 2048:2048 + w_], 1.0, -12582912.0, ALU.mult, ALU.add)
                        v.tensor_scalar_max(sc2[:, 2048:2048 + w_], sc2[:, 0:w_], -1.0)
                        v.tensor_scalar_min(sc2[:, 0:w_], sc2[:, 2048:2048 + w_], 1.0)
                        return v.tensor_copy(wqb[:, k * ncols + cs:k * ncols + ce], sc2[:, 0:w_])

        quant_weight(w1T, NCH, 4096, 4096.0 * 1024.0)

        for o in range(32):
            @sq.ph("tensor")
            def _(t, o=o):
                for tc in range(4):
                    for k in range(NCH):
                        last = t.matmul(psM[:, tc * 512:tc * 512 + 512],
                                        wqb[:, k * 4096 + o * 128:k * 4096 + o * 128 + 128],
                                        Bt[:, k * T + tc * 512:k * T + tc * 512 + 512],
                                        start=(k == 0), stop=(k == NCH - 1))
                return last

            @sq.ph("scalar")
            def _(s, o=o):
                return s.activation(out=hstg[:, :], in_=psM[:, 0:T], func=AF.Relu,
                                    bias=bcol[:, o:o + 1])

            @sq.dma("sync")
            def _(q, o=o):
                return [q.dma_start(out=hbuf[o * 128:(o + 1) * 128, :], in_=hstg[:, :])]

        quant_weight(w2T, 32, 1024, 4096.0 * 1024.0)

        for tc in range(4):
            @sq.dma("sync")
            def _(q, tc=tc):
                return [q.dma_start(out=hh[:, k * 512:(k + 1) * 512],
                                    in_=hbuf[k * 128:(k + 1) * 128, tc * 512:(tc + 1) * 512])
                        for k in range(32)]

            for o in range(NCH):
                @sq.ph("tensor")
                def _(t, tc=tc, o=o):
                    for k in range(32):
                        last = t.matmul(psM[:, 0:512],
                                        wqb[:, k * 1024 + o * 128:k * 1024 + o * 128 + 128],
                                        hh[:, k * 512:(k + 1) * 512],
                                        start=(k == 0), stop=(k == 31))
                    return last

                @sq.ph("scalar")
                def _(s, tc=tc, o=o):
                    return s.activation(out=scr[:, 0:512], in_=psM[:, 0:512], func=AF.Identity,
                                        bias=bcol[:, 32 + o:33 + o])

                @sq.ph("vector")
                def _(v, tc=tc, o=o):
                    sl = slice(o * T + tc * 512, o * T + tc * 512 + 512)
                    return v.tensor_add(A[:, sl], A[:, sl], scr[:, 0:512])

        @sq.dma("sync")
        def _(q):
            return [q.dma_start(out=o_x[k * 128:(k + 1) * 128, :], in_=A[:, k * T:(k + 1) * T])
                    for k in range(NCH)]

        sq.build(nc)
    return nc



def build_lb2():
    SQ = S
    nc = new_nc()
    xln = nc.dram_tensor("xln", [D, SQ], BF16, kind="ExternalInput").ap()
    wq_in = nc.dram_tensor("wq", [D, 256], F32, kind="ExternalInput").ap()
    wk_in = nc.dram_tensor("wk", [D, 256], F32, kind="ExternalInput").ap()
    wv_in = nc.dram_tensor("wv", [D, 256], F32, kind="ExternalInput").ap()
    mask_in = nc.dram_tensor("mask", [128, 128], F32, kind="ExternalInput").ap()
    invf_in = nc.dram_tensor("invf", [32, 1], F32, kind="ExternalInput").ap()
    o_ao = nc.dram_tensor("o_ao", [256, SQ], BF16, kind="ExternalOutput").ap()
    sq = Seq()
    NT = SQ // 128
    NQC = SQ // 512
    with contextlib.ExitStack() as cx:
        E = cx.enter_context
        QT = E(nc.sbuf_tensor("QT", [128, 2 * SQ], BF16))
        KT = E(nc.sbuf_tensor("KT", [128, 2 * SQ], BF16))
        VA = E(nc.sbuf_tensor("VA", [128, NT * 260], BF16))
        wqb = E(nc.sbuf_tensor("wqb", [128, NCH * 256], BF16))
        wkb = E(nc.sbuf_tensor("wkb", [128, NCH * 256], BF16))
        wvb = E(nc.sbuf_tensor("wvb", [128, NCH * 256], BF16))
        xch = E(nc.sbuf_tensor("xch", [128, NCH * 512], BF16))
        CW = 2048
        cosT = E(nc.sbuf_tensor("cosT", [32, CW], F32))
        sinT = E(nc.sbuf_tensor("sinT", [32, CW], F32))
        EB = E(nc.sbuf_tensor("EB", [32, CW], F32))
        OB = E(nc.sbuf_tensor("OB", [32, CW], F32))
        T1 = E(nc.sbuf_tensor("T1", [32, CW], F32))
        T2 = E(nc.sbuf_tensor("T2", [32, CW], F32))
        maskb = E(nc.sbuf_tensor("maskb", [128, 128], F32))
        invfb = E(nc.sbuf_tensor("invfb", [32, 1], F32))
        invfr = E(nc.sbuf_tensor("invfr", [1, 32], F32))
        posr = E(nc.sbuf_tensor("posr", [1, CW], F32))
        nrow = E(nc.sbuf_tensor("nrow", [1, CW], F32))
        ncol = E(nc.sbuf_tensor("ncol", [128, 8], F32))
        PTb = E(nc.sbuf_tensor("PTb", [128, 512], BF16))
        AOt = E(nc.sbuf_tensor("AOt", [64, 512], F32))
        rdrow = E(nc.sbuf_tensor("rdrow", [1, 512], F32))
        scr = E(nc.sbuf_tensor("scr", [128, 2048], F32))
        ones_col = E(nc.sbuf_tensor("ones_col", [128, 1], F32))
        ones_row = E(nc.sbuf_tensor("ones_row", [1, 128], F32))
        psS = E(nc.psum_tensor([128, 512], F32))
        psO = E(nc.psum_tensor([65, 512], F32))
        psR = E(nc.psum_tensor([128, 512], F32))
        psA = E(nc.psum_tensor([32, 512], F32))

        @sq.dma("sync")
        def _(q):
            return [q.dma_start(out=maskb[:, :], in_=mask_in[:, :]),
                    q.dma_start(out=invfb[:, :], in_=invf_in[:, :]),
                    q.dma_start(out=invfr[0:1, :], in_=invf_in.rearrange("a b -> b a")),
                    q.dma_start(out=scr[:, 0:1024].rearrange("p (c n) -> p c n", c=4),
                                in_=wq_in.rearrange("(c p) n -> p c n", p=128)[:, 0:4, :]),
                    q.dma_start(out=scr[:, 1024:2048].rearrange("p (c n) -> p c n", c=4),
                                in_=wq_in.rearrange("(c p) n -> p c n", p=128)[:, 4:8, :])]

        @sq.ph("vector")
        def _(v):
            v.memset(ones_col[:, :], 1.0)
            v.memset(ones_row[:, :], 1.0)
            v.tensor_copy(wqb[:, 0:1024], scr[:, 0:1024])
            return v.tensor_copy(wqb[:, 1024:2048], scr[:, 1024:2048])

        @sq.dma("sync")
        def _(q):
            return [q.dma_start(out=scr[:, 0:1024].rearrange("p (c n) -> p c n", c=4),
                                in_=wk_in.rearrange("(c p) n -> p c n", p=128)[:, 0:4, :]),
                    q.dma_start(out=scr[:, 1024:2048].rearrange("p (c n) -> p c n", c=4),
                                in_=wk_in.rearrange("(c p) n -> p c n", p=128)[:, 4:8, :])]

        @sq.ph("vector")
        def _(v):
            v.tensor_copy(wkb[:, 0:1024], scr[:, 0:1024])
            return v.tensor_copy(wkb[:, 1024:2048], scr[:, 1024:2048])

        @sq.dma("sync")
        def _(q):
            return [q.dma_start(out=scr[:, 0:1024].rearrange("p (c n) -> p c n", c=4),
                                in_=wv_in.rearrange("(c p) n -> p c n", p=128)[:, 0:4, :]),
                    q.dma_start(out=scr[:, 1024:2048].rearrange("p (c n) -> p c n", c=4),
                                in_=wv_in.rearrange("(c p) n -> p c n", p=128)[:, 4:8, :])]

        @sq.ph("vector")
        def _(v):
            v.tensor_copy(wvb[:, 0:1024], scr[:, 0:1024])
            v.tensor_copy(wvb[:, 1024:2048], scr[:, 1024:2048])
            last = v.memset(VA[:, :], 0.0)
            return last

        @sq.ph("vector")
        def _(v):
            for tl in range(NT):
                for hh in range(4):
                    last = v.memset(VA[:, tl * 260 + hh * 65 + 64:tl * 260 + hh * 65 + 65], 1.0)
            return last

        for tc in range(NQC):
            @sq.dma("sync")
            def _(q, tc=tc):
                return [q.dma_start(out=xch[:, k * 512:(k + 1) * 512],
                                    in_=xln[k * 128:(k + 1) * 128, tc * 512:(tc + 1) * 512])
                        for k in range(NCH)]

            for o in range(2):
                @sq.ph("tensor")
                def _(t, tc=tc, o=o):
                    for k in range(NCH):
                        last = t.matmul(psS[:, 0:512],
                                        wqb[:, k * 256 + o * 128:k * 256 + o * 128 + 128],
                                        xch[:, k * 512:(k + 1) * 512],
                                        start=(k == 0), stop=(k == NCH - 1))
                    return last

                @sq.ph("scalar")
                def _(s, tc=tc, o=o):
                    return s.copy(out=QT[:, o * SQ + tc * 512:o * SQ + (tc + 1) * 512],
                                  in_=psS[:, 0:512])

                @sq.ph("tensor")
                def _(t, tc=tc, o=o):
                    for k in range(NCH):
                        last = t.matmul(psS[:, 0:512],
                                        wkb[:, k * 256 + o * 128:k * 256 + o * 128 + 128],
                                        xch[:, k * 512:(k + 1) * 512],
                                        start=(k == 0), stop=(k == NCH - 1))
                    return last

                @sq.ph("scalar")
                def _(s, tc=tc, o=o):
                    return s.copy(out=KT[:, o * SQ + tc * 512:o * SQ + (tc + 1) * 512],
                                  in_=psS[:, 0:512])

            for j in range(4):
                tl = tc * 4 + j

                @sq.ph("tensor")
                def _(t, tc=tc, j=j, tl=tl):
                    for k in range(NCH):
                        last = t.matmul(psO[0:65, 0:256] if False else psS[:, 0:256],
                                        xch[:, k * 512 + j * 128:k * 512 + j * 128 + 128],
                                        wvb[:, k * 256:(k + 1) * 256],
                                        start=(k == 0), stop=(k == NCH - 1))
                    return last

                @sq.ph("vector")
                def _(v, tl=tl):
                    for hh in range(4):
                        last = v.tensor_copy(VA[:, tl * 260 + hh * 65:tl * 260 + hh * 65 + 64],
                                             psS[:, hh * 64:(hh + 1) * 64])
                    return last

        negB = E(nc.sbuf_tensor("negB", [128, 4], F32))
        TWO_PI = 2.0 * math.pi

        @sq.ph("vector")
        def _(v):
            return v.memset(ncol[:, :], 0.0)

        for cb in range(SQ // CW):
            @sq.ph("gpsimd")
            def _(g, cb=cb):
                return g.iota(posr[0:1, :], pattern=[[1, CW]], base=cb * CW,
                              channel_multiplier=0, allow_small_or_imprecise_dtypes=True)

            for sl4 in range(CW // 512):
                @sq.ph("tensor")
                def _(t, sl4=sl4):
                    return t.matmul(psA[:, 0:512], invfr[0:1, :],
                                    posr[0:1, sl4 * 512:(sl4 + 1) * 512], start=True, stop=True)

                @sq.ph("vector")
                def _(v, sl4=sl4):
                    sl = slice(sl4 * 512, (sl4 + 1) * 512)
                    v.tensor_scalar(T1[:, 0:512], psA[0:32, 0:512], 1.0 / TWO_PI, 12582912.0,
                                    ALU.mult, ALU.add)
                    v.tensor_scalar(EB[:, 0:512], T1[:, 0:512], 1.0, -12582912.0, ALU.mult, ALU.add)
                    v.scalar_tensor_tensor(out=cosT[:, sl], in0=EB[:, 0:512], scalar=-TWO_PI,
                                           in1=psA[0:32, 0:512], op0=ALU.mult, op1=ALU.add)
                    v.tensor_scalar(T1[:, 0:512], psA[0:32, 0:512], 1.0 / TWO_PI, 12582912.25,
                                    ALU.mult, ALU.add)
                    v.tensor_scalar(EB[:, 0:512], T1[:, 0:512], 1.0, -12582912.0, ALU.mult, ALU.add)
                    v.scalar_tensor_tensor(out=T1[:, 0:512], in0=EB[:, 0:512], scalar=-TWO_PI,
                                           in1=psA[0:32, 0:512], op0=ALU.mult, op1=ALU.add)
                    return v.tensor_scalar(sinT[:, sl], T1[:, 0:512], 1.0, 0.5 * math.pi,
                                           ALU.mult, ALU.add)

            @sq.ph("scalar")
            def _(s, cb=cb):
                s.activation(out=EB[:, :], in_=cosT[:, :], func=AF.Sin)
                return s.activation(out=OB[:, :], in_=sinT[:, :], func=AF.Sin)

            @sq.ph("vector")
            def _(v, cb=cb):
                v.tensor_copy(sinT[:, :], EB[:, :])
                return v.tensor_copy(cosT[:, :], OB[:, :])

            for mi, mat in enumerate((QT, KT)):
                for hh in range(4):
                    base = (hh // 2) * SQ + cb * CW
                    roff = (hh % 2) * 64
                    mcol = mi * 4 + hh

                    @sq.ph("vector")
                    def _(v, mat=mat, base=base, roff=roff):
                        v.tensor_copy(EB[:, :], mat[roff:roff + 32, base:base + CW])
                        return v.tensor_copy(OB[:, :], mat[roff + 32:roff + 64, base:base + CW])

                    @sq.ph("vector")
                    def _(v, mat=mat, base=base, roff=roff):
                        erow = mat[roff:roff + 32, base:base + CW]
                        orow = mat[roff + 32:roff + 64, base:base + CW]
                        v.tensor_mul(T1[:, :], EB[:, :], cosT[:, :])
                        v.tensor_mul(T2[:, :], OB[:, :], sinT[:, :])
                        v.tensor_sub(erow, T1[:, :], T2[:, :])
                        v.tensor_mul(T1[:, :], EB[:, :], sinT[:, :])
                        v.tensor_mul(T2[:, :], OB[:, :], cosT[:, :])
                        return v.tensor_add(orow, T1[:, :], T2[:, :])

                    @sq.ph("vector")
                    def _(v, mat=mat, base=base, roff=roff):
                        erow = mat[roff:roff + 32, base:base + CW]
                        orow = mat[roff + 32:roff + 64, base:base + CW]
                        v.tensor_mul(T1[:, :], erow, erow)
                        v.tensor_mul(T2[:, :], orow, orow)
                        return v.tensor_add(T1[:, :], T1[:, :], T2[:, :])

                    for sl4 in range(CW // 512):
                        @sq.ph("tensor")
                        def _(t, sl4=sl4):
                            return t.matmul(psA[0:1, 0:512], ones_col[0:32, 0:1],
                                            T1[:, sl4 * 512:(sl4 + 1) * 512],
                                            start=True, stop=True)

                        @sq.ph("vector")
                        def _(v, sl4=sl4):
                            return v.tensor_copy(nrow[0:1, sl4 * 512:(sl4 + 1) * 512],
                                                 psA[0:1, 0:512])

                    @sq.ph("vector")
                    def _(v, mcol=mcol):
                        v.reduce_max(nrow[0:1, 0:1], nrow[0:1, :], AX.X)
                        return v.tensor_max(ncol[0:1, mcol:mcol + 1], ncol[0:1, mcol:mcol + 1],
                                            nrow[0:1, 0:1])

        for hh in range(4):
            @sq.ph("vector")
            def _(v, hh=hh):
                return v.tensor_add(nrow[0:1, 0:1], ncol[0:1, hh:hh + 1],
                                    ncol[0:1, 4 + hh:5 + hh])

            @sq.ph("tensor")
            def _(t, hh=hh):
                return t.matmul(psR[:, 0:1], ones_row[:, :], nrow[0:1, 0:1], start=True, stop=True)

            @sq.ph("vector")
            def _(v, hh=hh):
                return v.tensor_scalar(negB[:, hh:hh + 1], psR[:, 0:1], -1.0 / 16.0, None, ALU.mult)

        for hh in range(4):
            base = (hh // 2) * SQ
            roff = (hh % 2) * 64
            for qc in range(NQC):
                qs = qc * 512
                ktmax = 4 * qc + 4
                for kt in range(ktmax):
                    @sq.ph("tensor")
                    def _(t, hh=hh, base=base, roff=roff, qs=qs, kt=kt):
                        return t.matmul(psS[:, 0:512],
                                        KT[roff:roff + 64, base + kt * 128:base + (kt + 1) * 128],
                                        QT[roff:roff + 64, base + qs:base + qs + 512],
                                        start=True, stop=True)

                    if kt >= 4 * qc:
                        @sq.ph("vector")
                        def _(v, qc=qc, kt=kt):
                            j = kt - 4 * qc
                            return v.tensor_add(psS[:, j * 128:(j + 1) * 128],
                                                psS[:, j * 128:(j + 1) * 128], maskb[:, :])

                        @sq.ph("vector")
                        def _(v, qc=qc, kt=kt):
                            j = kt - 4 * qc
                            if j >= 1:
                                return v.memset(psS[:, 0:j * 128], -1e30)
                            return v.tensor_copy(scr[0:1, 0:1], psS[0:1, 0:1])

                    @sq.ph("scalar")
                    def _(s, hh=hh, kt=kt):
                        return s.activation(out=PTb[:, :], in_=psS[:, 0:512], func=AF.Exp,
                                            scale=0.125, bias=negB[:, hh:hh + 1])

                    @sq.ph("tensor")
                    def _(t, hh=hh, qs=qs, kt=kt, qc=qc):
                        return t.matmul(psO[0:65, 0:512], VA[:, kt * 260 + hh * 65:kt * 260 + (hh + 1) * 65],
                                        PTb[:, :], start=(kt == 0), stop=(kt == 4 * qc + 3))

                @sq.ph("vector")
                def _(v, hh=hh, qs=qs):
                    v.reciprocal(rdrow[0:1, 0:512], psO[64:65, 0:512])
                    return v.tensor_copy(AOt[:, :], psO[0:64, 0:512])

                @sq.ph("tensor")
                def _(t, hh=hh, qs=qs):
                    return t.matmul(psR[0:64, 0:512], ones_row[0:1, 0:64], rdrow[0:1, 0:512],
                                    start=True, stop=True)

                @sq.ph("vector")
                def _(v, hh=hh, qs=qs):
                    return v.tensor_mul(PTb[0:64, 0:512], AOt[:, :], psR[0:64, 0:512])

                @sq.dma("sync")
                def _(q, hh=hh, qs=qs):
                    return [q.dma_start(out=o_ao[hh * 64:(hh + 1) * 64, qs:qs + 512],
                                        in_=PTb[0:64, 0:512])]

        sq.build(nc)
    return nc



def _run(nc, in_maps, cores=None):
    if cores is None:
        cores = list(range(NCORES))
    res = run_bass_kernel_spmd(nc, in_maps, core_ids=cores)
    return res.results


def _shards_fm(x):
    out = []
    for c in range(NCORES):
        b, sl = c // 4, c % 4
        out.append(np.ascontiguousarray(x[b, sl * TSH:(sl + 1) * TSH, :].T))
    return out


def _gather_fm(parts):
    x = np.empty((2, S, D), np.float32)
    for c in range(NCORES):
        b, sl = c // 4, c % 4
        x[b, sl * TSH:(sl + 1) * TSH, :] = parts[c].T
    return x


_IDENT = np.eye(128, dtype=np.float32)


def _sublayer_attn_a(x, P):
    nc = cached("a1", build_a1)
    wres_perm = np.ascontiguousarray(
        P["L0_attn_a_W_res"].reshape(D, 16, 2).transpose(0, 2, 1).reshape(D, 32))
    base = {
        "w_in": P["L0_attn_a_W_in"], "w_xp": P["L0_attn_a_W_xproj"],
        "w_dt": np.ascontiguousarray(np.concatenate(
            [P["L0_attn_a_W_dt"], P["L0_attn_a_b_dt"][None, :]], 0)),
        "w_res": wres_perm, "ident": _IDENT,
    }
    xs = _shards_fm(x)
    res = _run(nc, [dict(base, x_fm=xs[c]) for c in range(NCORES)])
    dt = {}; dtx = {}; bc = {}; xh = {}; sz = {}
    for b in range(2):
        dt[b] = np.concatenate([res[b * 4 + i]["o_dt"] for i in range(4)], axis=1)
        dtx[b] = np.concatenate([res[b * 4 + i]["o_dtx"] for i in range(4)], axis=1)
        bc[b] = np.concatenate([res[b * 4 + i]["o_bc"] for i in range(4)], axis=1)
        xh[b] = np.concatenate([res[b * 4 + i]["o_xhat"] for i in range(4)], axis=0)
        sz[b] = [res[b * 4 + i]["o_siluz"] for i in range(4)]

    NST = 1024
    nc2 = cached("a2", lambda: build_a2(NST))
    p = np.arange(128)
    nl = p // 32
    da = p % 32
    en4 = (nl[:, None] == np.arange(4)[None, :]).astype(np.float32)
    en4t = np.ascontiguousarray(en4.T)
    en32 = (da[:, None] == np.arange(32)[None, :]).astype(np.float32)
    emask = (np.arange(32)[:, None] == da[None, :]).astype(np.float32)
    phases = np.linspace(0.0, 2.0 * np.pi * (1.0 - 1.0 / 16.0), 16)
    ypart = [np.empty((32, S * 32), np.float32) for _ in range(NCORES)]
    state = []
    for c in range(NCORES):
        b, g = c // 4, c % 4
        ng = 4 * g + nl
        hre0 = np.ascontiguousarray(
            np.broadcast_to((0.01 * np.cos(phases))[ng][:, None], (128, 32))).astype(np.float32)
        him0 = np.ascontiguousarray(
            np.broadcast_to((0.01 * np.sin(phases))[ng][:, None], (128, 32))).astype(np.float32)
        th0 = np.full((128, 32), 0.01, np.float32)
        state.append([hre0, him0, th0])
    core_streams = []
    for c in range(NCORES):
        b, g = c // 4, c % 4
        ng4 = 4 * g + np.arange(4)
        ngp = 4 * g + nl
        dt_s = np.ascontiguousarray(dt[b].T.reshape(S, 32, 32).transpose(1, 0, 2).reshape(32, S * 32))
        dtx_s = np.ascontiguousarray(dtx[b].T.reshape(S, 32, 32).transpose(1, 0, 2).reshape(32, S * 32))
        bcb = bc[b]
        bre_s = np.ascontiguousarray(bcb[0 * 32 + ngp * 2 + 0, :])
        bim_s = np.ascontiguousarray(bcb[0 * 32 + ngp * 2 + 1, :])
        cre_s = np.ascontiguousarray(bcb[1 * 32 + ngp * 2 + 0, :])
        cim_s = np.ascontiguousarray(bcb[1 * 32 + ngp * 2 + 1, :])
        xr_s = np.ascontiguousarray(xh[b][:, ngp].T)
        xi_s = np.ascontiguousarray(xh[b][:, 16 + ngp].T)
        alogp = np.ascontiguousarray(P["L0_attn_a_A_log"][ngp][None, :]).astype(np.float32)
        core_streams.append((dt_s, dtx_s, bre_s, bim_s, cre_s, cim_s, xr_s, xi_s, alogp))
    for lk in range(S // NST):
        t0 = lk * NST
        maps = []
        for c in range(NCORES):
            dt_s, dtx_s, bre_s, bim_s, cre_s, cim_s, xr_s, xi_s, alogp = core_streams[c]
            maps.append({
                "h_re0": state[c][0], "h_im0": state[c][1], "th0": state[c][2],
                "dt_s": np.ascontiguousarray(dt_s[:, t0 * 32:(t0 + NST) * 32]),
                "dtx_s": np.ascontiguousarray(dtx_s[:, t0 * 32:(t0 + NST) * 32]),
                "bre_s": np.ascontiguousarray(bre_s[:, t0:t0 + NST]),
                "bim_s": np.ascontiguousarray(bim_s[:, t0:t0 + NST]),
                "cre_s": np.ascontiguousarray(cre_s[:, t0:t0 + NST]),
                "cim_s": np.ascontiguousarray(cim_s[:, t0:t0 + NST]),
                "xr_s": np.ascontiguousarray(xr_s[:, t0:t0 + NST]),
                "xi_s": np.ascontiguousarray(xi_s[:, t0:t0 + NST]),
                "en4": en4, "en4t": en4t, "en32": en32, "emask": emask, "alogp": alogp,
            })
        res2 = _run(nc2, maps)
        for c in range(NCORES):
            state[c] = [res2[c]["o_hre"], res2[c]["o_him"], res2[c]["o_th"]]
            ypart[c][:, t0 * 32:(t0 + NST) * 32] = res2[c]["o_y"]

    nc3 = cached("a3", build_a3)
    woaug = np.ascontiguousarray(np.concatenate(
        [P["L0_attn_a_W_out"], P["L0_attn_a_b_out"][None, :]], 0))
    maps = []
    for c in range(NCORES):
        b, sl = c // 4, c % 4
        m = {"x_fm": _shards_fm(x)[c], "siluz": sz[b][sl], "w_out": woaug}
        for i in range(4):
            yp = ypart[b * 4 + i]
            yps = yp.reshape(32, S, 32)[:, sl * TSH:(sl + 1) * TSH, :]
            m[f"yp{i}"] = np.ascontiguousarray(yps.transpose(0, 2, 1).reshape(D, TSH))
        maps.append(m)
    res3 = _run(nc3, maps)
    return _gather_fm([res3[c]["o_x"] for c in range(NCORES)])


def _sublayer_ffn(x, P):
    nc1 = cached("f1", build_f1)
    w1 = P["L0_ffn_w1"]
    base = {
        "w1t": np.ascontiguousarray(w1[0:D, :]), "w1bot": np.ascontiguousarray(w1[D:2 * D, :]),
        "b1r": np.ascontiguousarray(P["L0_ffn_b1"][None, :]),
        "w2": P["L0_ffn_w2"], "slot_init": P["L0_ffn_slot_init"],
        "taub": np.full((128, 1), float(np.asarray(P["L0_ffn_tau"]).reshape(-1)[0]), np.float32),
        "b2r": np.ascontiguousarray(P["L0_ffn_b2"][None, :]),
        "ident": _IDENT,
    }
    xs = _shards_fm(x)
    res1 = _run(nc1, [dict(base, x_fm=xs[c]) for c in range(NCORES)])

    nc2 = cached("f2", build_f2)
    wihT = np.ascontiguousarray(np.concatenate(
        [P["L0_ffn_Wih"].T, P["L0_ffn_bih"][None, :]], 0))
    whhT = np.ascontiguousarray(np.concatenate(
        [P["L0_ffn_Whh"].T, P["L0_ffn_bhh"][None, :]], 0))
    base2 = {
        "slot_init": P["L0_ffn_slot_init"], "wihT": wihT, "whhT": whhT,
        "wq": P["L0_ffn_wq"], "wk": P["L0_ffn_wk"], "wv": P["L0_ffn_wv"],
        "wvp": P["L0_ffn_wvp"],
        "bq": np.ascontiguousarray(P["L0_ffn_bq"][:, None]),
        "bk": np.ascontiguousarray(P["L0_ffn_bk"][:, None]),
        "bv": np.ascontiguousarray(P["L0_ffn_bv"][:, None]),
        "bvp": np.ascontiguousarray(P["L0_ffn_bvp"][None, :]),
        "lng": np.ascontiguousarray(np.broadcast_to(P["L0_ffn_ln_g"][None, :], (64, D))),
        "lnb": np.ascontiguousarray(np.broadcast_to(P["L0_ffn_ln_b"][None, :], (64, D))),
        "ident": _IDENT,
    }
    maps2 = []
    for c in range(NCORES):
        b = c // 4
        m = dict(base2)
        for i in range(4):
            m[f"sp{i}"] = res1[b * 4 + i]["o_spart"]
        maps2.append(m)
    res2 = _run(nc2, maps2)

    nc3 = cached("f3", build_f3)
    woaug = np.ascontiguousarray(np.concatenate(
        [P["L0_ffn_wo"], P["L0_ffn_bo"][None, :]], 0))
    maps3 = []
    for c in range(NCORES):
        maps3.append({"x_fm": xs[c], "alphaT": np.ascontiguousarray(res1[c]["o_alpha"].T),
                      "sv": res2[c]["o_sv"], "wo": woaug})
    res3 = _run(nc3, maps3)
    return _gather_fm([res3[c]["o_x"] for c in range(NCORES)])


def _sublayer_attn_b(x, P):
    nc1 = cached("c1", build_c1)
    waug = np.ascontiguousarray(np.concatenate(
        [P["L0_attn_b_attn_W"], P["L0_attn_b_attn_b"][None, :]], 0))
    xs = _shards_fm(x)
    res1 = _run(nc1, [{"x_fm": xs[c], "w_in": waug} for c in range(NCORES)])
    import ml_dtypes
    q0 = {c: res1[c]["o_q"] for c in range(NCORES)}
    vv = {c: res1[c]["o_v"] for c in range(NCORES)}
    idcol = P["L0_attn_b_identity"].astype(ml_dtypes.bfloat16)
    idpad = np.ascontiguousarray(np.broadcast_to(idcol[:, None], (D, 2047)))
    wfaug = np.ascontiguousarray(np.concatenate(
        [P["L0_attn_b_freq_W"], P["L0_attn_b_freq_b"][None, :]], 0))
    woutaug = np.ascontiguousarray(np.concatenate(
        [P["L0_attn_b_out_W"], P["L0_attn_b_out_b"][None, :]], 0))

    ncA = cached("c2A", lambda: build_c2("A"))
    ncB = cached("c2B", lambda: build_c2("B"))
    mapsA, mapsB, coresA, coresB = [], [], [], []
    for c in range(NCORES):
        b, sl = c // 4, c % 4
        if sl == 0:
            halo = idpad
        else:
            halo = q0[c - 1][:, TSH - 2047:]
        qext = np.ascontiguousarray(np.concatenate([halo, q0[c]], axis=1))
        m = {"qext": qext, "wf": wfaug, "wout": woutaug, "ident": _IDENT}
        if sl == 0:
            mapsA.append(m); coresA.append(c)
        else:
            mapsB.append(m); coresB.append(c)
    resA = _run(ncA, mapsA, coresA)
    resB = _run(ncB, mapsB, coresB)
    q11 = {}
    for i, c in enumerate(coresA):
        q11[c] = resA[i]["o_q"]
    for i, c in enumerate(coresB):
        q11[c] = resB[i]["o_q"]

    nc34 = cached("c34", build_c34)
    maps = []
    for c in range(NCORES):
        sl = c % 4
        prev = idpad[:, 0:TSH] if sl == 0 else q11[c - 1]
        if sl == 0:
            prev = np.ascontiguousarray(np.broadcast_to(idcol[:, None], (D, TSH)))
        maps.append({"qcur": q11[c], "qprev": prev, "wf": wfaug, "wout": woutaug, "ident": _IDENT})
    res12 = _run(nc34, maps)
    q12 = {c: res12[c]["o_q"] for c in range(NCORES)}
    maps = []
    for c in range(NCORES):
        sl = c % 4
        if sl <= 1:
            prev = np.ascontiguousarray(np.broadcast_to(idcol[:, None], (D, TSH)))
        else:
            prev = q12[c - 2]
        maps.append({"qcur": q12[c], "qprev": prev, "wf": wfaug, "wout": woutaug, "ident": _IDENT})
    res13 = _run(nc34, maps)

    nc5 = cached("c5", build_c5)
    projaug = np.ascontiguousarray(np.concatenate(
        [P["L0_attn_b_proj_W"], P["L0_attn_b_proj_b"][None, :]], 0))
    maps = [{"x_fm": xs[c], "q13": res13[c]["o_q"], "vv": vv[c], "proj": projaug}
            for c in range(NCORES)]
    res5 = _run(nc5, maps)
    return _gather_fm([res5[c]["o_x"] for c in range(NCORES)])


def _fft_consts():
    import ml_dtypes
    bf = ml_dtypes.bfloat16
    r = np.arange(64)[:, None]
    k0 = np.arange(128)[None, :]
    W1 = np.concatenate([np.cos(2 * np.pi * r * k0 / 128.0),
                         -np.sin(2 * np.pi * r * k0 / 128.0)], axis=1).astype(bf)
    c = np.arange(128)
    k1 = np.arange(128)
    th = 2 * np.pi * c[None, :, None] * (np.arange(128)[:, None, None] + 128 * k1[None, None, :]) / 16384.0
    W2C = np.cos(th).astype(bf)
    W2S = np.sin(th).astype(bf)
    thT = np.transpose(th, (0, 2, 1))
    V2C = np.cos(thT).astype(bf)
    V2S = np.sin(thT).astype(bf)
    ph = 2 * np.pi * np.arange(128)[:, None] * np.arange(64)[None, :] / 128.0
    WIC = (np.cos(ph) / 16384.0).astype(bf)
    WIS = (-np.sin(ph) / 16384.0).astype(bf)
    return W1, W2C, W2S, V2C, V2S, WIC, WIS


def _sublayer_l1_attn_a(x, P):
    import ml_dtypes
    bf = ml_dtypes.bfloat16
    ncln = cached("lnonly", build_lnonly)
    xs = _shards_fm(x)
    resln = _run(ncln, [{"x_fm": xs[c]} for c in range(NCORES)])
    xlnb = {b: np.concatenate([resln[b * 4 + i]["o_xln"] for i in range(4)], axis=1) for b in range(2)}
    xln_full = np.concatenate([xlnb[0], xlnb[1]], axis=1)

    ncxz = cached("l1axz", build_l1a_xz)
    WinT = np.ascontiguousarray(P["L1_attn_a_W_in"].T)
    maps = []
    for cr in range(NCORES):
        idx = np.concatenate([np.arange(cr * 256, cr * 256 + 256),
                              2048 + np.arange(cr * 256, cr * 256 + 256)])
        maps.append({"xln": xln_full, "wsl": np.ascontiguousarray(WinT[:, idx])})
    resxz = _run(ncxz, maps)

    nccv = cached("l1aconv", build_l1a_conv)
    maps = []
    for cr in range(NCORES):
        chs = slice(cr * 256, cr * 256 + 256)
        maps.append({"xz": resxz[cr]["o_xz"],
                     "convw": np.ascontiguousarray(P["L1_attn_a_conv_w"][chs]),
                     "convb": np.ascontiguousarray(P["L1_attn_a_conv_b"][chs][:, None]),
                     "kssm": np.ascontiguousarray(P["L1_attn_a_k_ssm"][chs])})
    rescv = _run(nccv, maps)

    W1, W2C, W2S, V2C, V2S, WIC, WIS = _fft_consts()

    ncs1 = cached("fft1", lambda: build_bmm_sigstat(768))
    maps = []
    for cr in range(NCORES):
        xcb = rescv[cr]["o_xcb"]
        kb = rescv[cr]["o_kb"]
        stat = np.zeros((768, 64, 128), bf)
        stat[0:256] = xcb[:, 0:S].reshape(256, 64, 128)
        stat[256:512] = xcb[:, S:2 * S].reshape(256, 64, 128)
        stat[512:768, 0:32, :] = kb.reshape(256, 32, 128)
        maps.append({"stat": stat, "rhs": W1})
    res1 = _run(ncs1, maps)

    ncs2 = cached("fft2", lambda: build_bmm_perk(128, 128, 128, 768))
    maps = []
    for cr in range(NCORES):
        G = res1[cr]["out"]
        Gre = np.ascontiguousarray(np.transpose(G[:, :, 0:128], (2, 1, 0)))
        Gim = np.ascontiguousarray(np.transpose(G[:, :, 128:256], (2, 1, 0)))
        maps.append({"A": W2C, "B": W2S, "C": (-W2S).astype(bf), "Dm": W2C,
                     "U": Gre, "V": Gim})
    res2 = _run(ncs2, maps)

    nccm = cached("fftc", lambda: build_cmul(128, 256, 2))
    maps = []
    for cr in range(NCORES):
        Xre = res2[cr]["ore"]
        Xim = res2[cr]["oim"]
        maps.append({"xre": np.ascontiguousarray(Xre[:, :, 0:512]),
                     "xim": np.ascontiguousarray(Xim[:, :, 0:512]),
                     "kre": np.ascontiguousarray(Xre[:, :, 512:768]),
                     "kim": np.ascontiguousarray(Xim[:, :, 512:768])})
    rescm = _run(nccm, maps)

    nci1 = cached("ffti1", lambda: build_bmm_perk(128, 128, 128, 512))
    maps = []
    for cr in range(NCORES):
        maps.append({"A": V2C, "B": (-V2S).astype(bf), "C": V2S, "Dm": V2C,
                     "U": rescm[cr]["yre"], "V": rescm[cr]["yim"]})
    resi1 = _run(nci1, maps)

    nci2 = cached("ffti2", lambda: build_bmm_acc2(128, 64, 128 * 512))
    maps = []
    for cr in range(NCORES):
        Pre = resi1[cr]["ore"]
        Pim = resi1[cr]["oim"]
        maps.append({"L1": WIC, "L2": WIS,
                     "R1": np.ascontiguousarray(Pre.reshape(128, 128 * 512)),
                     "R2": np.ascontiguousarray(Pim.reshape(128, 128 * 512))})
    resi2 = _run(nci2, maps)

    ncg = cached("l1agate", build_l1a_gate)
    maps = []
    for cr in range(NCORES):
        yo = resi2[cr]["out"].reshape(64, 128, 512)
        yt = np.ascontiguousarray(np.transpose(yo, (2, 1, 0)))
        yb16 = np.ascontiguousarray(np.transpose(yt, (0, 2, 1)).reshape(512, S))
        yy = np.empty((256, 2 * S), bf)
        yy[:, 0:S] = yb16[0:256]
        yy[:, S:2 * S] = yb16[256:512]
        chs = slice(cr * 256, cr * 256 + 256)
        maps.append({"yy": yy, "xc": rescv[cr]["o_xc"], "sz": rescv[cr]["o_sz"],
                     "dsk": np.ascontiguousarray(P["L1_attn_a_D_skip"][chs][:, None])})
    resg = _run(ncg, maps)

    ncmr = cached("matres2048", lambda: build_matres(2048))
    WoutT = np.ascontiguousarray(P["L1_attn_a_W_out"].T)
    maps = []
    for c in range(NCORES):
        b, sl = c // 4, c % 4
        ts = slice(b * 2 * 0 + 0, 0)
        ygall = np.empty((2048, TSH), bf)
        for j in range(NCORES):
            ygall[j * 256:(j + 1) * 256, :] = resg[j]["o_yg"][:, b * S + sl * TSH: b * S + (sl + 1) * TSH]
        maps.append({"x_fm": xs[c], "act": ygall, "wT": WoutT})
    resmr = _run(ncmr, maps)
    return _gather_fm([resmr[c]["o_x"] for c in range(NCORES)])


def _sublayer_l1_ffn(x, P):
    ncf = cached("l1ffn", build_l1ffn)
    xs = _shards_fm(x)
    w1T = np.ascontiguousarray(P["L1_ffn_W1"].T)
    w2T = np.ascontiguousarray(P["L1_ffn_W2"].T)
    maps = [{"x_fm": xs[c], "w1T": w1T, "w2T": w2T,
             "b1c": np.ascontiguousarray(P["L1_ffn_b1"][:, None]),
             "b2c": np.ascontiguousarray(P["L1_ffn_b2"][:, None])} for c in range(NCORES)]
    res = _run(ncf, maps)
    return _gather_fm([res[c]["o_x"] for c in range(NCORES)])


def _sublayer_l1_attn_b(x, P):
    import ml_dtypes
    ncln = cached("lnonly", build_lnonly)
    xs = _shards_fm(x)
    resln = _run(ncln, [{"x_fm": xs[c]} for c in range(NCORES)])
    xlnb = {b: np.concatenate([resln[b * 4 + i]["o_xln"] for i in range(4)], axis=1) for b in range(2)}

    perm = np.concatenate([np.arange(0, 64, 2), np.arange(1, 64, 2)])
    invf = (10000.0 ** (-np.arange(0, 64, 2) / 64.0)).astype(np.float32)[:, None]
    mask = np.where(np.arange(128)[:, None] > np.arange(128)[None, :], -1e30, 0.0).astype(np.float32)

    nc2 = cached("lb2", build_lb2)
    WqT = P["L1_attn_b_Wq"].T
    WkT = P["L1_attn_b_Wk"].T
    WvT = P["L1_attn_b_Wv"].T
    maps = []
    for c in range(NCORES):
        b, hg = c // 4, c % 4
        hsl = []
        for i in range(4):
            h = 4 * hg + i
            hsl.append(h * 64 + perm)
        hperm = np.concatenate(hsl)
        hplain = np.concatenate([np.arange(h * 64, h * 64 + 64) for h in (4 * hg + np.arange(4))])
        maps.append({"xln": xlnb[b],
                     "wq": np.ascontiguousarray(WqT[:, hperm]),
                     "wk": np.ascontiguousarray(WkT[:, hperm]),
                     "wv": np.ascontiguousarray(WvT[:, hplain]),
                     "mask": mask, "invf": invf})
    res2 = _run(nc2, maps)

    ncmr = cached("matres1024", lambda: build_matres(1024))
    WoT = np.ascontiguousarray(P["L1_attn_b_Wo"].T)
    maps = []
    for c in range(NCORES):
        b, sl = c // 4, c % 4
        ao = np.empty((1024, TSH), WoT.dtype)
        import ml_dtypes as _m
        ao = np.empty((1024, TSH), _m.bfloat16)
        for i in range(4):
            ao[i * 256:(i + 1) * 256, :] = res2[b * 4 + i]["o_ao"][:, sl * TSH:(sl + 1) * TSH]
        maps.append({"x_fm": xs[c], "act": ao, "wT": WoT})
    resmr = _run(ncmr, maps)
    return _gather_fm([resmr[c]["o_x"] for c in range(NCORES)])


def kernel(**inputs):
    P = {k: np.asarray(v, np.float32) for k, v in inputs.items()}
    x = P["L0_attn_a_x"]
    x = _sublayer_attn_a(x, P)
    x = _sublayer_ffn(x, P)
    x = _sublayer_attn_b(x, P)
    x = _sublayer_ffn(x, P)
    x = _sublayer_l1_attn_a(x, P)
    x = _sublayer_l1_ffn(x, P)
    x = _sublayer_l1_attn_b(x, P)
    x = _sublayer_l1_ffn(x, P)
    return x.astype(np.float32)
